# version 14
# baseline (speedup 1.0000x reference)
import numpy as np
import ml_dtypes
import concourse.bass as bass
import concourse.mybir as mybir
from concourse.bass_utils import run_bass_kernel_spmd

F32 = mybir.dt.float32
BF16 = mybir.dt.bfloat16
AF = mybir.ActivationFunctionType
ALU = mybir.AluOpType
AX = mybir.AxisListType
BF = ml_dtypes.bfloat16

ENGINES = ("tensor", "vector", "scalar", "gpsimd", "sync")


def _regs(lst):
    out = []
    if lst is None:
        return out
    if isinstance(lst, tuple) and len(lst) == 2 and isinstance(lst[1], (str, int)):
        lst = [lst]
    elif not isinstance(lst, (list, tuple)):
        lst = [lst]
    for item in lst:
        if isinstance(item, tuple):
            t, k = item
        else:
            t, k = item, None
        if not hasattr(t, "name"):
            t = t.tensor
        out.append((t.name, k))
    return out


class Prog:
    def __init__(self, nc, dma_pool=40):
        self.nc = nc
        self.ops = {e: [] for e in ENGINES}
        self._ctx = []
        self.chain = {}
        self.count = {}
        for c in ENGINES:
            self._mk_chain(c)
        self.dma_sems_hw = [self._mk_chain(f"dqh{i}") for i in range(dma_pool)]
        self.dma_sems_sw = [self._mk_chain(f"dqs{i}") for i in range(16)]
        self._dma_rr = {"hw": 0, "sw": 0}
        self.state = {}
        self.tile_keys = {}
        self.covered = {e: {} for e in ENGINES}
        self.n_ops = 0
        self._scope = None

    def _mk_chain(self, name):
        cm = self.nc.semaphore(f"ch_{name}")
        self.chain[name] = cm.__enter__()
        self._ctx.append(cm)
        self.count[name] = 0
        return name

    def sbuf(self, name, shape, dtype):
        cm = self.nc.sbuf_tensor(name, shape, dtype)
        h = cm.__enter__()
        (self._scope if self._scope is not None else self._ctx).append(cm)
        return h

    def psum(self, name, shape, dtype=F32):
        cm = self.nc.psum_tensor(name, shape, dtype)
        h = cm.__enter__()
        (self._scope if self._scope is not None else self._ctx).append(cm)
        return h

    def push_scope(self):
        self._scope = []

    def pop_scope(self):
        for cm in reversed(self._scope):
            cm.__exit__(None, None, None)
        self._scope = None

    def layer_barrier(self):
        for e in ENGINES:
            for c, v in self.count.items():
                if v > 0 and self.covered[e].get(c, 0) < v and c != e:
                    sh = self.chain[c]
                    self.raw(e, lambda eng, sh=sh, v=v: eng.wait_ge(sh, v))
                self.covered[e][c] = max(self.covered[e].get(c, 0), self.count[c])
        self.state = {}
        self.tile_keys = {}

    def _st(self, rk):
        if rk not in self.state:
            self.state[rk] = {"w": None, "r": []}
            self.tile_keys.setdefault(rk[0], set()).add(rk[1])
        return self.state[rk]

    def _dep_read(self, rk):
        deps = []
        name, key = rk
        st = self._st(rk)
        if st["w"]:
            deps.append(st["w"])
        if key is not None:
            stw = self._st((name, None))
            if stw["w"]:
                deps.append(stw["w"])
        else:
            for k in self.tile_keys.get(name, ()):
                if k is None:
                    continue
                stk = self._st((name, k))
                if stk["w"]:
                    deps.append(stk["w"])
        return deps

    def _dep_write(self, rk):
        deps = []
        name, key = rk
        sts = [self._st(rk)]
        if key is not None:
            sts.append(self._st((name, None)))
        else:
            sts += [self._st((name, k)) for k in self.tile_keys.get(name, ())
                    if k is not None]
        for st in sts:
            if st["w"]:
                deps.append(st["w"])
            deps.extend(st["r"])
        return deps

    def _emit(self, eng, fn, rl, wl, mychain, inc_by, extra_waits=(), tag=None,
              strict=False):
        deps = []
        for rk in rl:
            deps += self._dep_read(rk)
        for wk in wl:
            deps += self._dep_write(wk)
        need = {}
        for (c, v) in deps:
            if eng == "tensor" and c == "tensor":
                continue
            if (not strict) and v <= self.covered[eng].get(c, 0):
                continue
            need[c] = max(need.get(c, 0), v)
        waits = list(extra_waits)
        for c, v in sorted(need.items()):
            waits.append((self.chain[c], v))
            self.covered[eng][c] = v
        self.count[mychain] += inc_by
        myval = self.count[mychain]
        mysem = self.chain[mychain]
        self.n_ops += 1

        def thunk(engine, waits=waits, fn=fn, mysem=mysem, inc_by=inc_by, tag=tag):
            for (s, v) in waits:
                engine.wait_ge(s, v)
            instr = fn(engine)
            assert instr is not None, f"op {tag} returned None"
            instr.then_inc(mysem, inc_by)
        self.ops[eng].append(thunk)

        me = (mychain, myval)
        for rk in rl:
            self._st(rk)["r"].append(me)
        for wk in wl:
            name, key = wk
            st = self._st(wk)
            st["w"] = me
            st["r"] = []
            if key is None:
                for k in list(self.tile_keys.get(name, ())):
                    if k is not None:
                        self.state[(name, k)] = {"w": me, "r": []}
        return myval

    def op(self, eng, fn, reads=None, writes=None, tag=None, strict=False):
        return self._emit(eng, fn, _regs(reads), _regs(writes), eng, 1, tag=tag,
                          strict=strict)

    def dma(self, eng, fn, reads=None, writes=None, tag=None):
        pool = "sw" if eng == "gpsimd" else "hw"
        sems = self.dma_sems_sw if pool == "sw" else self.dma_sems_hw
        c = sems[self._dma_rr[pool]]
        self._dma_rr[pool] = (self._dma_rr[pool] + 1) % len(sems)
        extra = []
        if self.count[c] > 0:
            v = self.count[c]
            if self.covered[eng].get(c, 0) < v:
                extra.append((self.chain[c], v))
                self.covered[eng][c] = v
        return self._emit(eng, fn, _regs(reads), _regs(writes), c, 16,
                          extra_waits=extra, tag=tag)

    def raw(self, eng, fn):
        self.ops[eng].append(fn)

    def drain_dmas(self, eng):
        for c in self.dma_sems_hw + self.dma_sems_sw:
            v = self.count[c]
            if v > 0 and self.covered[eng].get(c, 0) < v:
                s = self.chain[c]
                self.raw(eng, lambda e, s=s, v=v: e.wait_ge(s, v))
                self.covered[eng][c] = v

    def finalize(self):
        nc = self.nc
        with nc.Block() as block:
            @block.tensor
            def _(e):
                for f in self.ops["tensor"]:
                    f(e)

            @block.vector
            def _(e):
                for f in self.ops["vector"]:
                    f(e)

            @block.scalar
            def _(e):
                for f in self.ops["scalar"]:
                    f(e)

            @block.gpsimd
            def _(e):
                for f in self.ops["gpsimd"]:
                    f(e)

            @block.sync
            def _(e):
                for f in self.ops["sync"]:
                    f(e)
        for cm in reversed(self._ctx):
            cm.__exit__(None, None, None)
        self._ctx = []


class View:
    def __init__(self, ap, name):
        self._ap = ap
        self.name = name

    def __getitem__(self, idx):
        return self._ap[idx]



def mm(P, out_ap, lhsT_ap, rhs_ap, reads, wkey, start=True, stop=True, tag=None):
    P.op("tensor",
         lambda e: e.matmul(out_ap, lhsT_ap, rhs_ap, start=start, stop=stop,
                            skip_group_check=True),
         reads=reads, writes=[wkey], tag=tag)


def load_weight(P, C, dst, src_dram, nchunks, ncols, tag="w"):
    for k in range(nchunks):
        P.dma("gpsimd",
              lambda e, k=k: e.dma_start(out=dst[:, k, :],
                                         in_=src_dram[k * 128:(k + 1) * 128, :]),
              writes=[(dst, f"k{k}")], tag=f"{tag}{k}")


class Ctx:
    pass


def setup_common(nc, P, need):
    C = Ctx()
    C.nc, C.P = nc, P
    C.idn_d = nc.dram_tensor("c_idn", [128, 128], BF16, kind="ExternalInput").ap()
    C.idnf_d = nc.dram_tensor("c_idnf", [128, 128], F32, kind="ExternalInput").ap()
    C.onescol_d = nc.dram_tensor("c_ones", [128, 128], BF16, kind="ExternalInput").ap()
    C.onesf_d = nc.dram_tensor("c_onesf", [128, 1], F32, kind="ExternalInput").ap()
    C.idn = P.sbuf("idn", [128, 128], BF16)
    C.idnf = P.sbuf("idnf", [128, 128], F32)
    C.onescol = P.sbuf("onescol", [128, 128], BF16)
    C.onesf = P.sbuf("onesf", [128, 1], F32)
    P.dma("sync", lambda e: e.dma_start(out=C.idn[:], in_=C.idn_d[:]), writes=[C.idn])
    P.dma("sync", lambda e: e.dma_start(out=C.idnf[:], in_=C.idnf_d[:]), writes=[C.idnf])
    P.dma("sync", lambda e: e.dma_start(out=C.onescol[:], in_=C.onescol_d[:]), writes=[C.onescol])
    P.dma("sync", lambda e: e.dma_start(out=C.onesf[:], in_=C.onesf_d[:]), writes=[C.onesf])
    if "masks" in need:
        C.mlow_d = nc.dram_tensor("c_mlow", [128, 128], F32, kind="ExternalInput").ap()
        C.mtriu_d = nc.dram_tensor("c_mtriu", [128, 128], BF16, kind="ExternalInput").ap()
        C.mup_d = nc.dram_tensor("c_mup", [128, 128], BF16, kind="ExternalInput").ap()
        C.mlow = P.sbuf("mlow", [128, 128], F32)
        C.mtriu = P.sbuf("mtriu", [128, 128], BF16)
        C.mup = P.sbuf("mup", [128, 128], BF16)
        P.dma("sync", lambda e: e.dma_start(out=C.mlow[:], in_=C.mlow_d[:]), writes=[C.mlow])
        P.dma("sync", lambda e: e.dma_start(out=C.mtriu[:], in_=C.mtriu_d[:]), writes=[C.mtriu])
        P.dma("sync", lambda e: e.dma_start(out=C.mup[:], in_=C.mup_d[:]), writes=[C.mup])
    return C


def emit_ln(P, C, x_dram_b, xn_fm, ps_t, tc, tag):
    xt = C.ln_xt[tc % 2]
    P.dma("sync", lambda e: e.dma_start(out=xt[:], in_=x_dram_b[tc * 128:(tc + 1) * 128, :]),
          writes=[xt], tag=f"lnx{tag}")
    s1, nm, vv, t1, t2, rstd = (C.ln_small[:, i:i + 1] for i in range(6))
    sm = C.ln_small
    P.op("vector", lambda e: e.reduce_sum(s1, xt[:], axis=AX.X), reads=[xt], writes=[(sm, "s1")])
    P.op("vector", lambda e: e.tensor_scalar(nm, s1, -1.0 / 1024, None, ALU.mult),
         reads=[(sm, "s1")], writes=[(sm, "nm")])
    va, vb = C.ln_small[:, 6:7], C.ln_small[:, 7:8]
    pa, pb = C.ln_ps
    P.op("scalar", lambda e: e.activation(pa[:, :], xt[:, 0:512], AF.Square, bias=nm, scale=1.0,
                                          accum_out=va),
         reads=[xt, (sm, "nm")], writes=[(pa, "g"), (sm, "va")])
    P.op("scalar", lambda e: e.activation(pb[:, :], xt[:, 512:1024], AF.Square, bias=nm, scale=1.0,
                                          accum_out=vb),
         reads=[xt, (sm, "nm")], writes=[(pb, "g"), (sm, "vb")])
    P.op("vector", lambda e: e.tensor_tensor(vv, va, vb, ALU.add),
         reads=[(sm, "va"), (sm, "vb")], writes=[(sm, "vv")])
    P.op("vector", lambda e: e.tensor_scalar(t1, vv, 1.0 / 1024, 1e-5, ALU.mult, ALU.add),
         reads=[(sm, "vv")], writes=[(sm, "t1")])
    P.op("scalar", lambda e: e.activation(t2, t1, AF.Sqrt, bias=0.0, scale=1.0),
         reads=[(sm, "t1")], writes=[(sm, "t2")])
    P.op("vector", lambda e: e.reciprocal(rstd, t2), reads=[(sm, "t2")], writes=[(sm, "rstd")])
    P.op("vector", lambda e: e.tensor_scalar(C.ln_xn[:], xt[:], nm, rstd, ALU.add, ALU.mult),
         reads=[xt, (sm, "nm"), (sm, "rstd")], writes=[C.ln_xn])
    for c in range(8):
        pt = ps_t[c % 2]
        P.op("tensor", lambda e, c=c, pt=pt: e.transpose(pt[:, 0:128], C.ln_xn[:, c * 128:(c + 1) * 128], C.idn[:]),
             reads=[C.ln_xn, C.idn], writes=[(pt, "t")])
        P.op("vector", lambda e, c=c, pt=pt: e.tensor_copy(xn_fm[:, c, tc * 128:(tc + 1) * 128], pt[:, 0:128]),
             reads=[(pt, "t")], writes=[(xn_fm, f"c{c}_{tc}")])



def build_layer0(nc, P, C=None, x_kind="ExternalInput", out_kind="ExternalOutput", dbg=False):
    if C is None:
        C = setup_common(nc, P, need=())
    dt = nc.dram_tensor
    x_d = dt("x", [2, 2048, 1024], F32, kind=x_kind).ap()
    x1_d = dt("x1", [2, 2048, 1024], F32, kind=out_kind).ap()
    C.x1_d = x1_d
    s0_d = dt("s0scr", [2, 2048, 1024], BF16, kind="ExternalOutput" if dbg else "Internal").ap()
    if dbg:
        dbg_xn = dt("dbg_xn", [128, 8, 2048], BF16, kind="ExternalOutput").ap()
        dbg_scon = dt("dbg_scon", [128, 16, 8], F32, kind="ExternalOutput").ap()
        dbg_mask = dt("dbg_mask", [128, 16, 8], F32, kind="ExternalOutput").ap()
        dbg_g = dt("dbg_g", [128, 16384], BF16, kind="ExternalOutput").ap()
        dbg_qp = dt("dbg_qp", [128, 2048], BF16, kind="ExternalOutput").ap()
        dbg_kp = dt("dbg_kp", [128, 2048], BF16, kind="ExternalOutput").ap()
        dbg_pt = dt("dbg_pt", [128, 16, 512], BF16, kind="ExternalOutput").ap()
        dbg_ofm = dt("dbg_ofm", [128, 16384], BF16, kind="ExternalOutput").ap()
        dbg_wa = dt("dbg_wa", [128, 8, 1024], BF16, kind="ExternalOutput").ap()
        dbg_bo = dt("dbg_bo", [1, 1024], BF16, kind="ExternalOutput").ap()
        dbg_s0st = dt("dbg_s0st", [128, 1024], BF16, kind="ExternalOutput").ap()

    wq_d = dt("l0a_wq", [1024, 1024], F32, kind="ExternalInput").ap()
    wk_d = dt("l0a_wk", [1024, 1024], F32, kind="ExternalInput").ap()
    wv_d = dt("l0a_wv", [1024, 1024], F32, kind="ExternalInput").ap()
    wo_d = dt("l0a_wo", [1024, 1024], F32, kind="ExternalInput").ap()
    bq_d = dt("l0a_bq", [128, 8], F32, kind="ExternalInput").ap()
    bk_d = dt("l0a_bk", [128, 8], F32, kind="ExternalInput").ap()
    bv_d = dt("l0a_bv", [1, 1024], F32, kind="ExternalInput").ap()
    bo_d = dt("l0a_bo", [1, 1024], F32, kind="ExternalInput").ap()
    fwq_d = dt("l0f_wq", [1024, 1024], F32, kind="ExternalInput").ap()
    fwk_d = dt("l0f_wk", [1024, 1024], F32, kind="ExternalInput").ap()
    fbq_d = dt("l0f_bq", [128, 8], F32, kind="ExternalInput").ap()
    fbk_d = dt("l0f_bk", [128, 8], F32, kind="ExternalInput").ap()
    ewq_d = dt("l0f_ewq", [8, 128, 128], F32, kind="ExternalInput").ap()
    ewk_d = dt("l0f_ewk", [8, 128, 128], F32, kind="ExternalInput").ap()
    ebq_d = dt("l0f_ebq", [128, 8], F32, kind="ExternalInput").ap()
    ebk_d = dt("l0f_ebk", [128, 8], F32, kind="ExternalInput").ap()
    wexp_d = dt("l0f_wexp", [8, 1024, 1024], F32, kind="ExternalInput").ap()
    bexp_d = dt("l0f_bexp", [1, 8192], F32, kind="ExternalInput").ap()

    S = P.sbuf
    WA = S("WA", [128, 8, 1024], BF16)
    WB = S("WB", [128, 8, 1024], BF16)
    WC = S("WC", [128, 8, 1024], BF16)
    xn_fm = S("xn_fm", [128, 8, 2048], BF16)
    SCR1 = S("SCR1", [128, 16384], BF16)
    ofm = SCR1[:, :].rearrange("p (h t) -> p h t", h=8)
    g_acc = SCR1[:, :].rearrange("p (t f) -> p t f", t=16)
    SCR2 = S("SCR2", [128, 16384], BF16)
    qf_fm = SCR2[:, :].rearrange("p (h t) -> p h t", h=8)
    PT = SCR2[:, 0:8192].rearrange("p (j i) -> p j i", j=16)
    SCR3 = S("SCR3", [128, 16384], BF16)
    kf_fm = SCR3[:, :].rearrange("p (h t) -> p h t", h=8)
    qp = SCR3[:, 0:2048]
    kp = SCR3[:, 2048:4096]
    Vp = SCR3[:, 4096:6176].rearrange("p (t h f) -> p t h f", t=16, h=2)
    obt = SCR2[0:64, 8192:10240]
    dnr = SCR2[64:65, 10240:10752]
    dens = S("dens", [128, 512], F32)
    denb = S("denb", [64, 512], BF16)
    s0st = S("s0st", [128, 1024], BF16)
    bias8a = S("bias8a", [128, 8], F32)
    bias8b = S("bias8b", [128, 8], F32)
    bvrow = S("bvrow", [1, 1024], BF16)
    borow = bvrow
    fb_q = S("fb_q", [128, 8], F32)
    fb_k = S("fb_k", [128, 8], F32)
    ewq_s = S("ewq_s", [128, 8, 128], BF16)
    ewk_s = S("ewk_s", [128, 8, 128], BF16)
    ebq_s = S("ebq_s", [128, 8], F32)
    ebk_s = S("ebk_s", [128, 8], F32)
    qe = SCR1[:, 0:2048]
    ke = SCR1[:, 2048:4096]
    Pe = [SCR1[:, 4096:6144], SCR1[:, 6144:8192]]
    accw = S("accw", [128, 2048], F32)
    denp = S("denp", [128, 4], F32)
    dd1 = S("dd1", [128, 2], F32)
    scon = S("scon", [128, 16, 8], F32)
    maskt = S("maskt", [128, 16, 8], F32)
    mxt = S("mxt", [128, 1], F32)
    bexps = S("bexps", [1, 1024], BF16)
    _lnxt = S("ln_xt0", [128, 1024], F32)
    C.ln_xt = [_lnxt, _lnxt]
    C.ln_xn = S("ln_xn", [128, 1024], BF16)
    C.ln_small = S("ln_small", [128, 8], F32)
    C.ln_ps = None
    gs = C.ln_xn
    s0t = s0st
    xt2 = C.ln_xt[0]
    prt = accw[:, 0:1024]
    x1t = accw[:, 1024:2048]

    psAB = P.psum("psAB", [128, 1024])
    psCD = P.psum("psCD", [128, 1024])
    ps0 = View(psAB[:, 0:512], "v_ps0")
    ps1 = View(psAB[:, 512:1024], "v_ps1")
    ps2 = View(psCD[:, 0:512], "v_ps2")
    ps3 = View(psCD[:, 512:1024], "v_ps3")
    ps4 = P.psum("ps4", [128, 512])
    ps5 = P.psum("ps5", [128, 512])
    psT = [P.psum("psTa", [128, 1024], BF16), P.psum("psTb", [128, 1024], BF16)]
    C.ln_ps = (ps4, ps5)

    for t, d in ((bias8a, bq_d), (bias8b, bk_d), (fb_q, fbq_d), (fb_k, fbk_d),
                 (ebq_s, ebq_d), (ebk_s, ebk_d)):
        P.dma("sync", lambda e, t=t, d=d: e.dma_start(out=t[:], in_=d[:]), writes=[t])
    P.dma("gpsimd", lambda e: e.dma_start(out=bvrow[:], in_=bv_d[:]), writes=[bvrow])
    for e_ in range(8):
        P.dma("gpsimd", lambda e, e_=e_: e.dma_start(out=ewq_s[:, e_, :], in_=ewq_d[e_]),
              writes=[(ewq_s, f"e{e_}")])
        P.dma("gpsimd", lambda e, e_=e_: e.dma_start(out=ewk_s[:, e_, :], in_=ewk_d[e_]),
              writes=[(ewk_s, f"e{e_}")])
    SC = 1.0 / np.sqrt(128.0)
    P.op("vector", lambda e: e.tensor_scalar(ebq_s[:], ebq_s[:], SC, None, ALU.mult),
         reads=[ebq_s], writes=[ebq_s])

    for b in range(2):
        xb = x_d[b]
        load_weight(P, C, WA, wq_d, 8, 1024, tag=f"wq{b}")
        load_weight(P, C, WB, wk_d, 8, 1024, tag=f"wk{b}")
        load_weight(P, C, WC, wv_d, 8, 1024, tag=f"wv{b}")
        for tc in range(16):
            emit_ln(P, C, xb, xn_fm, psT, tc, tag=f"{b}_{tc}")
        if dbg and b == 0:
            P.dma("sync", lambda e: e.dma_start(out=dbg_xn[:], in_=xn_fm[:, :, :]),
                  reads=[xn_fm])
        P.op("gpsimd", lambda e: e.memset(SCR1[:, :], 0.0), writes=[(SCR1, None)])
        P.op("gpsimd", lambda e: e.memset(SCR2[:, :], 0.0), writes=[(SCR2, None)])
        P.op("gpsimd", lambda e: e.memset(SCR3[:, :], 0.0), writes=[(SCR3, None)])
        P.op("vector", lambda e: e.memset(Vp[:, :, :, 64:65], 1.0), writes=[(SCR3, "ones")])
        P.dma("gpsimd", lambda e: e.dma_start(out=bvrow[:], in_=bv_d[:]), writes=[bvrow], tag=f"bv{b}")

        for hp in range(8):
            for tt in range(4):
                ps = (ps0, ps1)[tt % 2]
                for k in range(8):
                    mm(P, ps[:, :], WA[:, k, hp * 128:(hp + 1) * 128],
                       xn_fm[:, k, tt * 512:(tt + 1) * 512],
                       reads=[(WA, f"k{k}"), xn_fm], wkey=(ps, "g"),
                       start=(k == 0), stop=(k == 7))
                P.op("vector", lambda e, ps=ps, tt=tt, hp=hp: e.tensor_scalar(
                    qp[:, tt * 512:(tt + 1) * 512], ps[:, :], bias8a[:, hp:hp + 1], 1.0 / 32,
                    ALU.add, ALU.mult),
                    reads=[(ps, "g"), bias8a], writes=[(SCR3, f"qt{tt}")])
                ps_ = (ps2, ps3)[tt % 2]
                for k in range(8):
                    mm(P, ps_[:, :], WB[:, k, hp * 128:(hp + 1) * 128],
                       xn_fm[:, k, tt * 512:(tt + 1) * 512],
                       reads=[(WB, f"k{k}"), xn_fm], wkey=(ps_, "g"),
                       start=(k == 0), stop=(k == 7))
                P.op("vector", lambda e, ps_=ps_, tt=tt, hp=hp: e.tensor_scalar(
                    kp[:, tt * 512:(tt + 1) * 512], ps_[:, :], bias8b[:, hp:hp + 1], None,
                    ALU.add),
                    reads=[(ps_, "g"), bias8b], writes=[(SCR3, f"kt{tt}")])
            for tc2 in range(16):
                ps = (ps0, ps1, ps2, ps3)[tc2 % 4]
                for k in range(8):
                    mm(P, ps[:, 0:128], xn_fm[:, k, tc2 * 128:(tc2 + 1) * 128],
                       WC[:, k, hp * 128:(hp + 1) * 128],
                       reads=[xn_fm, (WC, f"k{k}")], wkey=(ps, "g"),
                       start=(k == 0), stop=False)
                mm(P, ps[:, 0:128], C.onescol[0:1, 0:128],
                   bvrow[0:1, hp * 128:(hp + 1) * 128],
                   reads=[C.onescol, bvrow], wkey=(ps, "g"), start=False, stop=True)
                P.op("vector", lambda e, ps=ps, tc2=tc2: e.tensor_copy(
                    Vp[:, tc2, :, 0:64],
                    ps[:, 0:128].rearrange("p (h f) -> p h f", h=2)),
                    reads=[(ps, "g")], writes=[(SCR3, f"v{tc2}")])
            if dbg and b == 0 and hp == 0:
                P.dma("sync", lambda e: e.dma_start(out=dbg_qp[:], in_=qp[:, :]),
                      reads=[(SCR3, "qt0"), (SCR3, "qt1"), (SCR3, "qt2"), (SCR3, "qt3")])
                P.dma("sync", lambda e: e.dma_start(out=dbg_kp[:], in_=kp[:, :]),
                      reads=[(SCR3, "kt0"), (SCR3, "kt1"), (SCR3, "kt2"), (SCR3, "kt3")])
            for h01 in range(2):
                base = 64 * h01
                for it in range(4):
                    psO = ps4
                    for jc in range(16):
                        ps = (ps0, ps1, ps2, ps3)[jc % 4]
                        mm(P, ps[:, :], kp[base:base + 64, jc * 128:(jc + 1) * 128],
                           qp[base:base + 64, it * 512:(it + 1) * 512],
                           reads=[(SCR3, f"kt{jc // 4}"), (SCR3, f"qt{it}")], wkey=(ps, "g"))
                        if jc % 2 == 1:
                            pr = (psAB, psCD)[(jc % 4) // 2]
                            k0, k1 = ((ps0, "g"), (ps1, "g")) if (jc % 4) == 1 else ((ps2, "g"), (ps3, "g"))
                            P.op("scalar", lambda e, pr=pr, jc=jc: e.activation(
                                SCR2[:, (jc - 1) * 512:(jc + 1) * 512], pr[:, :],
                                AF.Exp, bias=0.0, scale=1.0),
                                reads=[k0, k1],
                                writes=[(SCR2, f"j{jc - 1}"), (SCR2, f"j{jc}")])
                    if dbg and b == 0 and hp == 0 and h01 == 0 and it == 0:
                        P.dma("sync", lambda e: e.dma_start(out=dbg_pt[:], in_=PT[:, :, :]),
                              reads=[(SCR2, f"j{j}") for j in range(16)])
                    for jc in range(16):
                        mm(P, psO[0:65, :], Vp[:, jc, h01, 0:65], PT[:, jc, :],
                           reads=[(SCR3, f"v{jc}"), (SCR3, "ones"), (SCR2, f"j{jc}")],
                           wkey=(psO, "g"), start=(jc == 0), stop=(jc == 15))
                    P.op("vector", lambda e, psO=psO: e.tensor_copy(dens[64:65, :], psO[64:65, :]),
                         reads=[(psO, "g")], writes=[(dens, "d")])
                    P.op("vector", lambda e: e.reciprocal(dens[64:65, :], dens[64:65, :]),
                         reads=[(dens, "d")], writes=[(dens, "r")])
                    P.op("vector", lambda e: e.tensor_copy(dnr, dens[64:65, :]),
                         reads=[(dens, "r")], writes=[(SCR2, "dnr")])
                    pb = ps5
                    mm(P, pb[0:64, :], C.onescol[64:65, 0:64], dnr,
                       reads=[C.onescol, (SCR2, "dnr")], wkey=(pb, "g"))
                    P.op("vector", lambda e, pb=pb: e.tensor_copy(denb[:, :], pb[0:64, :]),
                         reads=[(pb, "g")], writes=[denb])
                    if h01 == 0:
                        dest = ofm[0:64, hp, it * 512:(it + 1) * 512]
                        wk = (SCR1, f"oa{hp}_{it}")
                    else:
                        dest = obt[0:64, it * 512:(it + 1) * 512]
                        wk = (SCR2, f"obt{it}")
                    P.op("vector", lambda e, dest=dest, psO=psO: e.tensor_tensor(
                        dest, psO[0:64, :], denb[:, :], ALU.mult),
                        reads=[(psO, "g"), denb], writes=[wk])
                if h01 == 1:
                    P.dma("sync", lambda e, hp=hp: e.dma_start(
                        out=ofm[64:128, hp, :], in_=obt[:, :]),
                        reads=[(SCR2, "obt0"), (SCR2, "obt1"), (SCR2, "obt2"), (SCR2, "obt3")],
                        writes=[(SCR1, f"ob{hp}")], tag=f"obmv{hp}")
        if dbg and b == 0:
            P.dma("sync", lambda e: e.dma_start(out=dbg_ofm[:], in_=SCR1[:, :]),
                  reads=[(SCR1, None)])
        load_weight(P, C, WA, wo_d, 8, 1024, tag=f"wo{b}")
        P.dma("gpsimd", lambda e: e.dma_start(out=borow[:], in_=bo_d[:]), writes=[bvrow], tag=f"bo{b}")
        if dbg and b == 0:
            P.dma("sync", lambda e: e.dma_start(out=dbg_wa[:], in_=WA[:, :, :]),
                  reads=[(WA, f"k{k}") for k in range(8)])
            P.dma("sync", lambda e: e.dma_start(out=dbg_bo[:], in_=borow[:, :]), reads=[borow])
        for tc in range(16):
            for ft in range(2):
                ps = (ps0, ps1)[ft]
                for pr in range(8):
                    mm(P, ps[:, :], ofm[:, pr, tc * 128:(tc + 1) * 128],
                       WA[:, pr, ft * 512:(ft + 1) * 512],
                       reads=[(SCR1, f"oa{pr}_{tc // 4}"), (SCR1, f"ob{pr}"), (WA, f"k{pr}")],
                       wkey=(ps, "g"), start=(pr == 0), stop=False)
                mm(P, ps[:, :], C.onescol[0:1, 0:128], borow[0:1, ft * 512:(ft + 1) * 512],
                   reads=[C.onescol, borow], wkey=(ps, "g"), start=False, stop=True)
                P.op("vector", lambda e, ps=ps, ft=ft: e.tensor_copy(
                    s0st[:, ft * 512:(ft + 1) * 512], ps[:, :]),
                    reads=[(ps, "g")], writes=[(s0st, f"f{ft}")])
            P.dma("sync", lambda e, tc=tc, b=b: e.dma_start(
                out=s0_d[b, tc * 128:(tc + 1) * 128, :], in_=s0st[:, :]),
                reads=[s0st], writes=[(s0_d, f"{b}_{tc}")], tag=f"s0s{tc}")
            if dbg and b == 0 and tc == 0:
                P.dma("sync", lambda e: e.dma_start(out=dbg_s0st[:], in_=s0st[:, :]),
                      reads=[s0st])

        load_weight(P, C, WA, fwq_d, 8, 1024, tag=f"fwq{b}")
        load_weight(P, C, WB, fwk_d, 8, 1024, tag=f"fwk{b}")
        P.op("gpsimd", lambda e: e.memset(SCR1[:, :], 0.0), writes=[(SCR1, None)])
        P.op("gpsimd", lambda e: e.memset(SCR2[:, :], 0.0), writes=[(SCR2, None)])
        P.op("gpsimd", lambda e: e.memset(SCR3[:, :], 0.0), writes=[(SCR3, None)])
        for mc in range(8):
            for tt in range(4):
                ps = (ps0, ps1)[tt % 2]
                for k in range(8):
                    mm(P, ps[:, :], WA[:, k, mc * 128:(mc + 1) * 128],
                       xn_fm[:, k, tt * 512:(tt + 1) * 512],
                       reads=[(WA, f"k{k}"), xn_fm], wkey=(ps, "g"),
                       start=(k == 0), stop=(k == 7))
                P.op("vector", lambda e, ps=ps, mc=mc, tt=tt: e.tensor_scalar(
                    qf_fm[:, mc, tt * 512:(tt + 1) * 512], ps[:, :],
                    fb_q[:, mc:mc + 1], SC, ALU.add, ALU.mult),
                    reads=[(ps, "g"), fb_q], writes=[(SCR2, f"m{mc}")])
                ps_ = (ps2, ps3)[tt % 2]
                for k in range(8):
                    mm(P, ps_[:, :], WB[:, k, mc * 128:(mc + 1) * 128],
                       xn_fm[:, k, tt * 512:(tt + 1) * 512],
                       reads=[(WB, f"k{k}"), xn_fm], wkey=(ps_, "g"),
                       start=(k == 0), stop=(k == 7))
                P.op("vector", lambda e, ps_=ps_, mc=mc, tt=tt: e.tensor_scalar(
                    kf_fm[:, mc, tt * 512:(tt + 1) * 512], ps_[:, :],
                    fb_k[:, mc:mc + 1], None, ALU.add),
                    reads=[(ps_, "g"), fb_k], writes=[(SCR3, f"m{mc}")])
        for e_ in range(8):
            for h in range(8):
                for tt in range(4):
                    ps = (ps0, ps1)[tt % 2]
                    mm(P, ps[:, :], ewq_s[:, e_, :], qf_fm[:, h, tt * 512:(tt + 1) * 512],
                       reads=[(ewq_s, f"e{e_}"), (SCR2, f"m{h}")], wkey=(ps, "g"))
                    P.op("vector", lambda e, ps=ps, tt=tt, e_=e_: e.tensor_scalar(
                        qe[:, tt * 512:(tt + 1) * 512], ps[:, :], ebq_s[:, e_:e_ + 1],
                        None, ALU.add),
                        reads=[(ps, "g"), ebq_s], writes=[(SCR1, f"qet{tt}")])
                    ps_ = (ps2, ps3)[tt % 2]
                    mm(P, ps_[:, :], ewk_s[:, e_, :], kf_fm[:, h, tt * 512:(tt + 1) * 512],
                       reads=[(ewk_s, f"e{e_}"), (SCR3, f"m{h}")], wkey=(ps_, "g"))
                    P.op("vector", lambda e, ps_=ps_, tt=tt, e_=e_: e.tensor_scalar(
                        ke[:, tt * 512:(tt + 1) * 512], ps_[:, :], ebk_s[:, e_:e_ + 1],
                        None, ALU.add),
                        reads=[(ps_, "g"), ebk_s], writes=[(SCR1, f"ket{tt}")])
                for sc in range(16):
                    pe = Pe[sc % 2]
                    for tt2 in range(4):
                        ps = (ps0, ps1, ps2, ps3)[tt2]
                        mm(P, ps[:, :], qe[:, sc * 128:(sc + 1) * 128],
                           ke[:, tt2 * 512:(tt2 + 1) * 512],
                           reads=[(SCR1, f"qet{sc // 4}"), (SCR1, f"ket{tt2}")], wkey=(ps, "g"))
                        if tt2 % 2 == 1:
                            pr = (psAB, psCD)[tt2 // 2]
                            k0, k1 = ((ps0, "g"), (ps1, "g")) if tt2 == 1 else ((ps2, "g"), (ps3, "g"))
                            P.op("scalar", lambda e, pr=pr, pe=pe, tt2=tt2: e.activation(
                                pe[:, (tt2 - 1) * 512:(tt2 + 1) * 512], pr[:, :], AF.Exp,
                                bias=0.0, scale=1.0, accum_out=denp[:, tt2 // 2:tt2 // 2 + 1]),
                                reads=[k0, k1],
                                writes=[(SCR1, f"pe{sc % 2}"), (denp, f"t{tt2 // 2}")])
                    P.op("vector", lambda e: e.tensor_reduce(dd1[:, 0:1], denp[:, 0:2], AX.X, ALU.add),
                         reads=[(denp, None)], writes=[(dd1, "s")])
                    P.op("vector", lambda e: e.reciprocal(dd1[:, 1:2], dd1[:, 0:1]),
                         reads=[(dd1, "s")], writes=[(dd1, "r")])
                    first = (h == 0 and sc == 0)
                    if first:
                        P.op("vector", lambda e, pe=pe: e.tensor_scalar(
                            accw[:, :], pe[:, :], dd1[:, 1:2], None, ALU.mult),
                            reads=[(SCR1, f"pe{sc % 2}"), (dd1, "r")], writes=[(accw, None)])
                    else:
                        P.op("vector", lambda e, pe=pe: e.scalar_tensor_tensor(
                            accw[:, :], pe[:, :], dd1[:, 1:2], accw[:, :],
                            ALU.mult, ALU.add),
                            reads=[(SCR1, f"pe{sc % 2}"), (dd1, "r"), (accw, None)],
                            writes=[(accw, None)])
            for tc in range(16):
                P.op("tensor", lambda e, tc=tc: e.matmul(
                    ps5[:, 0:1], accw[:, tc * 128:(tc + 1) * 128], C.onesf[:, 0:1],
                    start=True, stop=True, skip_group_check=True),
                    reads=[(accw, None), C.onesf], writes=[(ps5, "g")])
                P.op("vector", lambda e, tc=tc, e_=e_: e.tensor_scalar(
                    scon[:, tc, e_:e_ + 1], ps5[:, 0:1], 1.0 / 16384.0, None, ALU.mult),
                    reads=[(ps5, "g")], writes=[(scon, f"t{tc}e{e_}")])
        for tc in range(16):
            P.op("vector", lambda e, tc=tc: e.tensor_reduce(
                mxt[:, 0:1], scon[:, tc, :], AX.X, ALU.max),
                reads=[(scon, None)], writes=[mxt])
            P.op("vector", lambda e, tc=tc: e.tensor_scalar(
                maskt[:, tc, :], scon[:, tc, :], mxt[:, 0:1], None, ALU.is_equal),
                reads=[(scon, None), mxt], writes=[(maskt, f"t{tc}")])
        if dbg and b == 0:
            P.dma("sync", lambda e: e.dma_start(out=dbg_scon[:], in_=scon[:, :, :]), reads=[(scon, None)])
            P.dma("sync", lambda e: e.dma_start(out=dbg_mask[:], in_=maskt[:, :, :]), reads=[(maskt, None)])
        P.op("gpsimd", lambda e: e.memset(SCR1[:, :], 0.0), writes=[(SCR1, None)])
        for e_ in range(8):
            W = (WA, WB)[e_ % 2]
            load_weight(P, C, W, wexp_d[e_], 8, 1024, tag=f"we{e_}")
            P.dma("gpsimd", lambda e, e_=e_: e.dma_start(
                out=bexps[:, :], in_=bexp_d[0:1, e_ * 1024:(e_ + 1) * 1024]),
                writes=[bexps])
            for tc in range(16):
                for ft in range(2):
                    ps = (ps0, ps1, ps2, ps3)[(tc * 2 + ft) % 4]
                    for k in range(8):
                        mm(P, ps[:, :], xn_fm[:, k, tc * 128:(tc + 1) * 128],
                           W[:, k, ft * 512:(ft + 1) * 512],
                           reads=[xn_fm, (W, f"k{k}")], wkey=(ps, "g"),
                           start=(k == 0), stop=False)
                    mm(P, ps[:, :], C.onescol[0:1, 0:128],
                       bexps[0:1, ft * 512:(ft + 1) * 512],
                       reads=[C.onescol, bexps], wkey=(ps, "g"), start=False, stop=True)
                    P.op("vector", lambda e, ps=ps, tc=tc, ft=ft, e_=e_: e.scalar_tensor_tensor(
                        g_acc[:, tc, ft * 512:(ft + 1) * 512], ps[:, :],
                        maskt[:, tc, e_:e_ + 1], g_acc[:, tc, ft * 512:(ft + 1) * 512],
                        ALU.mult, ALU.add),
                        reads=[(ps, "g"), (maskt, f"t{tc}"), (SCR1, f"g{tc}_{ft}")],
                        writes=[(SCR1, f"g{tc}_{ft}")])
        if dbg and b == 0:
            P.dma("sync", lambda e: e.dma_start(out=dbg_g[:], in_=SCR1[:, :]),
                  reads=[(SCR1, None)])
        for tc in range(16):
            P.op("scalar", lambda e, tc=tc: e.activation(
                gs[:, :], g_acc[:, tc, :], AF.Silu, bias=0.0, scale=1.0),
                reads=[(SCR1, f"g{tc}_0"), (SCR1, f"g{tc}_1")], writes=[gs])
            P.dma("sync", lambda e, tc=tc, b=b: e.dma_start(
                out=s0t[:, :], in_=s0_d[b, tc * 128:(tc + 1) * 128, :]),
                reads=[(s0_d, f"{b}_{tc}")], writes=[s0t], tag=f"s0l{tc}")
            P.dma("scalar", lambda e, tc=tc, xb=xb: e.dma_start(
                out=xt2[:, :], in_=xb[tc * 128:(tc + 1) * 128, :]),
                writes=[xt2], tag=f"xl{tc}")
            P.op("vector", lambda e: e.tensor_tensor(prt, s0t[:, :], gs[:, :], ALU.mult),
                 reads=[s0t, gs], writes=[(accw, "prt")])
            P.op("vector", lambda e: e.tensor_tensor(x1t, xt2[:, :], prt, ALU.add),
                 reads=[xt2, (accw, "prt")], writes=[(accw, "x1t")])
            P.dma("sync", lambda e, tc=tc, b=b: e.dma_start(
                out=x1_d[b, tc * 128:(tc + 1) * 128, :], in_=x1t),
                reads=[(accw, "x1t")], tag=f"x1s{tc}")
    P.drain_dmas("sync")
    P.drain_dmas("gpsimd")
    P.drain_dmas("scalar")
    return C




def build_layer1(nc, P, C=None, x_kind="ExternalInput", out_kind="ExternalOutput", dbg=False):
    if C is None or not hasattr(C, "mlow"):
        C = setup_common(nc, P, need=("masks",)) if C is None else C
        if not hasattr(C, "mlow"):
            raise RuntimeError("fused build needs masks in C")
    dt = nc.dram_tensor
    x1_d = getattr(C, "x1_d", None) if C is not None else None
    if x1_d is None:
        x1_d = dt("x1", [2, 2048, 1024], F32, kind=x_kind).ap()
    out_d = dt("out", [2, 2048, 1024], F32, kind=out_kind).ap()
    q1scr = dt("q1scr", [2, 8, 128, 2048], BF16, kind="Internal").ap()
    k1scr = dt("k1scr", [2, 8, 128, 2048], BF16, kind="Internal").ap()
    fsscr = dt("fsscr", [2, 8, 128, 2048], BF16, kind="ExternalOutput" if dbg else "Internal").ap()
    flscr = dt("flscr", [2, 8, 128, 2048], BF16, kind="ExternalOutput" if dbg else "Internal").ap()
    dlscr = dt("dlscr", [2, 8, 128, 2048], BF16, kind="ExternalOutput" if dbg else "Internal").ap()
    s1scr = dt("s1scr", [2, 2048, 1024], BF16, kind="ExternalOutput" if dbg else "Internal").ap()
    rsd_dram = dt("rsd_dram", [16, 128], F32, kind="Internal").ap()
    if dbg:
        dbg_w = dt("dbg_w", [128, 16, 16], F32, kind="ExternalOutput").ap()
        dbg_comb = dt("dbg_comb", [128, 16, 8], F32, kind="ExternalOutput").ap()
        dbg_stats = dt("dbg_stats", [128, 16, 32], F32, kind="ExternalOutput").ap()
        dbg_lg = dt("dbg_lg", [128, 16, 16], F32, kind="ExternalOutput").ap()
        dbg_h1 = dt("dbg_h1", [128, 512], BF16, kind="ExternalOutput").ap()
        dbg_s32 = dt("dbg_s32", [128, 2048], BF16, kind="ExternalOutput").ap()
        dbg_pre = dt("dbg_pre", [128, 512], F32, kind="ExternalOutput").ap()
        dbg_xn1 = dt("dbg_xn1", [128, 8, 2048], BF16, kind="ExternalOutput").ap()
        dbg_w1 = dt("dbg_w1", [128, 2048], BF16, kind="ExternalOutput").ap()
        dbg_on = dt("dbg_on", [128, 16384], BF16, kind="ExternalOutput").ap()
        dbg_wb = dt("dbg_wb", [128, 512], F32, kind="ExternalOutput").ap()
        dbg_rsd = dt("dbg_rsd", [1, 2048], BF16, kind="ExternalOutput").ap()
        dbg_wfm = dt("dbg_wfm", [16, 2048], BF16, kind="ExternalOutput").ap()
        dbg_sel = dt("dbg_sel", [16, 2048], BF16, kind="ExternalOutput").ap()
        dbg_lgf = dt("dbg_lgf", [16, 2048], BF16, kind="ExternalOutput").ap()
        dbg_rtb = dt("dbg_rtb", [128, 16], F32, kind="ExternalOutput").ap()

    wq_d = dt("l1a_wq", [1024, 1024], F32, kind="ExternalInput").ap()
    wk_d = dt("l1a_wk", [1024, 1024], F32, kind="ExternalInput").ap()
    wv_d = dt("l1a_wv", [1024, 1024], F32, kind="ExternalInput").ap()
    wb_d = dt("l1a_wb", [128, 32], F32, kind="ExternalInput").ap()
    cq_d = dt("l1a_cq", [128, 32], F32, kind="ExternalInput").ap()
    ck_d = dt("l1a_ck", [128, 32], F32, kind="ExternalInput").ap()
    cv_d = dt("l1a_cv", [128, 32], F32, kind="ExternalInput").ap()
    fs_d = dt("l1a_fs", [128, 24], F32, kind="ExternalInput").ap()
    fl_d = dt("l1a_fl", [128, 504], F32, kind="ExternalInput").ap()
    w1_d = dt("l1a_w1", [1024, 2048], F32, kind="ExternalInput").ap()
    w1c9_d = dt("l1a_w1c9", [128, 2048], F32, kind="ExternalInput").ap()
    b1c_d = dt("l1a_b1", [128, 16], F32, kind="ExternalInput").ap()
    w2_d = dt("l1a_w2", [128, 256], F32, kind="ExternalInput").ap()
    b2r_d = dt("l1a_b2", [16, 1], F32, kind="ExternalInput").ap()
    glt_d = dt("l1a_glt", [1, 4], F32, kind="ExternalInput").ap()
    onw_d = dt("l1a_onw", [128, 2], F32, kind="ExternalInput").ap()
    wo_d = dt("l1a_wo", [1024, 1024], F32, kind="ExternalInput").ap()
    sel_d = dt("c_sel", [16, 2048], BF16, kind="ExternalInput").ap()
    wg_d = dt("l1f_wg", [128, 64], F32, kind="ExternalInput").ap()
    bg_d = dt("l1f_bg", [8, 1], F32, kind="ExternalInput").ap()
    fw1_d = dt("l1f_w1", [8, 1024, 2048], F32, kind="ExternalInput").ap()
    fb1_d = dt("l1f_b1", [8, 128, 16], F32, kind="ExternalInput").ap()
    fw2_d = dt("l1f_w2", [8, 2048, 1024], F32, kind="ExternalInput").ap()
    fb2_d = dt("l1f_b2", [1, 8192], F32, kind="ExternalInput").ap()

    S = P.sbuf
    xn1 = S("xn1", [128, 8, 2048], BF16)
    SCRA = S("L1A", [128, 16896], BF16)
    ARW = S("ARW", [128, 16384], BF16)
    ARX = S("ARX", [128, 16384], BF16)
    ARS = S("ARS", [128, 8192], BF16)
    CV = S("CVf", [128, 2051], BF16)
    CA = S("CAf", [128, 2048], BF16)
    OUTF = S("OUTF", [128, 1024], F32)
    qkst = S("qkst", [128, 2048], BF16)
    beta_fm = S("beta_fm", [4, 2048], BF16)
    beta_tm = S("beta_tm", [128, 16, 4], F32)
    stats_tm = S("stats_tm", [128, 16, 32], F32)
    sq_st = S("sq_st", [128, 2048], BF16)
    stats32 = S("stats32", [128, 2048], BF16)
    logits_fm = S("logits_fm", [16, 2048], BF16)
    lg_tm = S("lg_tm", [128, 16, 16], F32)
    w_tm = S("w_tm", [128, 16, 16], F32)
    wfm16 = S("wfm16", [16, 2048], BF16)
    sel_sb = S("sel_sb", [16, 2048], BF16)
    wbst = S("wbst", [128, 512], BF16)
    diag_a = S("diag_a", [128, 128], BF16)
    diag_b = S("diag_b", [128, 128], BF16)
    smallA = S("smallA", [128, 16], F32)
    smallB = S("smallB", [128, 16], F32)
    nrm16 = S("nrm16", [128, 16], F32)
    rsd16 = S("rsd16", [16, 128], F32)
    glt_sb = S("glt_sb", [1, 8], F32)
    rt16 = S("rt16", [1, 24], F32)
    rt16b = S("rt16b", [1, 16], BF16)
    rtb = S("rtb", [128, 16], F32)
    b1c = S("b1c", [128, 16], F32)
    w2_sb = S("w2_sb", [128, 16, 16], BF16)
    b2r = S("b2r", [16, 1], F32)
    onw = S("onw", [128, 2], F32)
    wb_sb = S("wb_sb", [128, 8, 4], BF16)
    cw_sb = S("cw_sb", [128, 8, 12], F32)
    fsw = S("fsw", [128, 8, 3], F32)
    flw = S("flw", [128, 8, 63], F32)
    wg_sb = S("wg_sb", [128, 8, 8], BF16)
    bgc = S("bgc", [8, 1], F32)
    fb1c = S("fb1c", [128, 16], F32)
    fb2r = S("fb2r", [1, 1024], BF16)
    w1c9 = S("w1c9", [128, 2048], BF16)
    comb_tm = S("comb_tm", [128, 16, 8], F32)
    msk1 = S("msk1", [128, 8], F32)
    msk2 = S("msk2", [128, 8], F32)
    lmsk = S("lmsk", [128, 8], F32)
    tmp8 = S("tmp8", [128, 8], F32)
    lnxt = S("l1_lnxt", [128, 1024], F32)
    C.ln_xt = [lnxt, lnxt]
    C.ln_xn = CA[:, 0:1024]
    C.ln_small = S("l1_lnsm", [128, 8], F32)

    rsd_row = beta_fm[0:1, 0:2048]
    def v1s(mc):
        return SCRA[:, mc * 2112:(mc + 1) * 2112]
    g1acc = SCRA[:, 0:16384].rearrange("p (t f) -> p t f", t=16)
    q1h = ARS[:, 0:4096].rearrange("p (c t) -> p c t", c=2)
    k1h = ARS[:, 4096:8192].rearrange("p (c t) -> p c t", c=2)
    dfm = [ARS[:, 0:2048], ARS[:, 2048:4096]]
    h1f = ARS[:, 0:8192].rearrange("p (c t) -> p c t", c=16)
    kn_tm = ARW[:, 0:4096].rearrange("p (c d) -> p c d", c=16)
    kb_tm = ARW[:, 4096:8192].rearrange("p (c d) -> p c d", c=16)
    vb_tm = ARW[:, 8192:12288].rearrange("p (c d) -> p c d", c=16)
    TMPa = ARW[:, 12288:16384].rearrange("p (c d) -> p c d", c=16)
    Ssb = ARW[:, 12288:12800]
    u_sb = ARW[:, 12800:13056]
    u2_sb = ARW[:, 13056:13312]
    wfm_c = ARW[:, 13312:13568]
    Wv_ = [ARW[:, 0:8192].rearrange("p (c f) -> p c f", c=8),
           ARW[:, 8192:16384].rearrange("p (c f) -> p c f", c=8)]
    w1v = ARW[:, 0:16384].rearrange("p (c f) -> p c f", c=8)
    brs = ARW[:, 0:4096].rearrange("p (c t) -> p c t", c=2)
    brl = ARW[:, 4096:8192].rearrange("p (c t) -> p c t", c=2)
    brd = ARW[:, 8192:12288].rearrange("p (c t) -> p c t", c=2)
    brstg = ARW[:, 0:2048]
    wov = ARW[:, 0:8192].rearrange("p (c f) -> p c f", c=8)
    w1e = ARW[:, 0:16384].rearrange("p (c f) -> p c f", c=8)
    qn_fm = ARX[:, 0:4096].rearrange("p (c t) -> p c t", c=2)
    kn_fm = ARX[:, 4096:8192].rearrange("p (c t) -> p c t", c=2)
    kb_fm = ARX[:, 8192:12288].rearrange("p (c t) -> p c t", c=2)
    attnT = ARX[:, 12288:14336].rearrange("p (c t) -> p c t", c=16)
    CHA = ARX[:, 14336:16384]
    def chs(i):
        return CHA[:, i * 128:(i + 1) * 128]
    o_n = ARX[:, 0:16384].rearrange("p (c t) -> p c t", c=8)
    w2e = ARX[:, 0:16384].rearrange("p (c f) -> p c f", c=16)

    ps0 = P.psum("l1ps0", [128, 512])
    ps1 = P.psum("l1ps1", [128, 512])
    ps2 = P.psum("l1ps2", [128, 512])
    ps3 = P.psum("l1ps3", [128, 512])
    ps4 = P.psum("l1ps4", [128, 512])
    ps5 = P.psum("l1ps5", [128, 512])
    psS = P.psum("l1psS", [128, 512])
    psTt = P.psum("l1psT", [128, 1024], BF16)
    psT = [psTt, psTt]
    C.ln_ps = (ps4, ps5)

    def retag(tile, keys, tag="rt"):
        ap1 = tile[0:1, 0:1]
        P.op("vector", lambda e: e.memset(ap1, 0.0),
             writes=[(tile, k) for k in keys], tag=tag, strict=True)

    TMPK = [f"tmp{tc}_{dc}" for tc in range(16) for dc in range(2)]
    SCANK = ["Ssb", "u", "u2", "wfm0", "wfm1"]
    ARSQK = ["q0", "q1", "k0", "k1"]
    ARSDK = [f"d{dvc}_{c}" for dvc in range(2) for c in range(16)]

    P.dma("sync", lambda e: e.dma_start(out=b1c[:], in_=b1c_d[:]), writes=[b1c])
    P.dma("sync", lambda e: e.dma_start(out=b2r[:], in_=b2r_d[:]), writes=[b2r])
    P.dma("sync", lambda e: e.dma_start(out=onw[:], in_=onw_d[:]), writes=[onw])
    P.dma("sync", lambda e: e.dma_start(out=bgc[:], in_=bg_d[:]), writes=[bgc])
    P.dma("sync", lambda e: e.dma_start(out=glt_sb[:, 0:4], in_=glt_d[:]), writes=[(glt_sb, "l")])
    P.dma("gpsimd", lambda e: e.dma_start(out=wb_sb[:], in_=wb_d[:].rearrange("p (c f) -> p c f", c=8)), writes=[wb_sb])
    P.dma("sync", lambda e: e.dma_start(out=cw_sb[:, :, 0:4], in_=cq_d[:].rearrange("p (c f) -> p c f", c=8)), writes=[(cw_sb, "q")])
    P.dma("sync", lambda e: e.dma_start(out=cw_sb[:, :, 4:8], in_=ck_d[:].rearrange("p (c f) -> p c f", c=8)), writes=[(cw_sb, "k")])
    P.dma("sync", lambda e: e.dma_start(out=cw_sb[:, :, 8:12], in_=cv_d[:].rearrange("p (c f) -> p c f", c=8)), writes=[(cw_sb, "v")])
    P.dma("sync", lambda e: e.dma_start(out=fsw[:], in_=fs_d[:].rearrange("p (c f) -> p c f", c=8)), writes=[fsw])
    P.dma("sync", lambda e: e.dma_start(out=flw[:], in_=fl_d[:].rearrange("p (c f) -> p c f", c=8)), writes=[flw])
    P.dma("gpsimd", lambda e: e.dma_start(out=wg_sb[:], in_=wg_d[:].rearrange("p (c f) -> p c f", c=8)), writes=[wg_sb])
    P.dma("gpsimd", lambda e: e.dma_start(out=w2_sb[:], in_=w2_d[:].rearrange("p (c f) -> p c f", c=16)), writes=[w2_sb])
    P.dma("sync", lambda e: e.dma_start(out=sel_sb[:], in_=sel_d[:]), writes=[sel_sb])
    P.dma("gpsimd", lambda e: e.dma_start(out=w1c9[:], in_=w1c9_d[:]), writes=[w1c9])

    g_ = glt_sb
    P.op("scalar", lambda e: e.activation(g_[:, 4:8], g_[:, 0:4], AF.Abs, bias=0.0, scale=1.0),
         reads=[(g_, "l")], writes=[(g_, "na")])
    P.op("scalar", lambda e: e.activation(g_[:, 4:8], g_[:, 4:8], AF.Exp, bias=0.0, scale=-1.0),
         reads=[(g_, "na")], writes=[(g_, "ex")])
    P.op("scalar", lambda e: e.activation(g_[:, 4:8], g_[:, 4:8], AF.Ln, bias=1.0, scale=1.0),
         reads=[(g_, "ex")], writes=[(g_, "ln")])
    P.op("vector", lambda e: e.tensor_scalar_max(rt16[:, 0:4], g_[:, 0:4], 0.0),
         reads=[(g_, "l")], writes=[(rt16, "mx")])
    P.op("vector", lambda e: e.tensor_tensor(rt16[:, 0:4], rt16[:, 0:4], g_[:, 4:8], ALU.add),
         reads=[(rt16, "mx"), (g_, "ln")], writes=[(rt16, "sp")])
    P.op("vector", lambda e: e.tensor_scalar(rt16[:, 0:4], rt16[:, 0:4], 1.0, 1e-4, ALU.mult, ALU.add),
         reads=[(rt16, "sp")], writes=[(rt16, "t")])
    P.op("vector", lambda e: e.reciprocal(rt16[:, 4:8], rt16[:, 0:4]),
         reads=[(rt16, "t")], writes=[(rt16, "r")])
    P.op("vector", lambda e: e.tensor_copy(
        rt16b[:, :].rearrange("p (h w) -> p h w", h=4),
        rt16[:, 4:8].rearrange("p (h w) -> p h w", w=1).broadcast_to((1, 4, 4))),
        reads=[(rt16, "r")], writes=[rt16b])
    mm(P, ps5[:, 0:16], C.onescol[0:1, 0:128], rt16b[0:1, :],
       reads=[C.onescol, rt16b], wkey=(ps5, "g"))
    P.op("vector", lambda e: e.tensor_copy(rtb[:, :], ps5[:, 0:16]),
         reads=[(ps5, "g")], writes=[rtb])

    for b in range(2):
        for tc in range(16):
            emit_ln(P, C, x1_d[b], xn1, psT, tc, tag=f"l1ln{b}_{tc}")
        P.op("gpsimd", lambda e: e.memset(SCRA[:, :], 0.0), writes=[(SCRA, None)])
        P.op("gpsimd", lambda e: e.memset(ARW[:, :], 0.0), writes=[(ARW, None)])
        P.op("gpsimd", lambda e: e.memset(ARX[:, :], 0.0), writes=[(ARX, None)])
        P.op("gpsimd", lambda e: e.memset(ARS[:, :], 0.0), writes=[(ARS, None)])

        for pi, (w_d_, scr, cwoff) in enumerate(((wq_d, q1scr, 0), (wk_d, k1scr, 4), (wv_d, None, 8))):
            W = Wv_[pi % 2]
            for k in range(8):
                P.dma("gpsimd", lambda e, k=k, W=W, w_d_=w_d_: e.dma_start(
                    out=W[:, k, :], in_=w_d_[k * 128:(k + 1) * 128, :]),
                    writes=[(ARW, f"w{pi % 2}_{k}")], tag=f"l1w{pi}_{k}")
            for mc in range(8):
                for tt in range(4):
                    ps = (ps0, ps1)[tt % 2]
                    for k in range(8):
                        mm(P, ps[:, :], W[:, k, mc * 128:(mc + 1) * 128],
                           xn1[:, k, tt * 512:(tt + 1) * 512],
                           reads=[(ARW, f"w{pi % 2}_{k}"), xn1], wkey=(ps, "g"),
                           start=(k == 0), stop=(k == 7))
                    P.op("vector", lambda e, ps=ps, tt=tt: e.tensor_copy(
                        CV[:, 3 + tt * 512:3 + (tt + 1) * 512], ps[:, :]),
                        reads=[(ps, "g")], writes=[(CV, f"t{tt}")])
                P.op("vector", lambda e: e.memset(CV[:, 0:3], 0.0), writes=[(CV, "pad")])
                cvr = [(CV, f"t{t_}") for t_ in range(4)] + [(CV, "pad")]
                for tt in range(4):
                    ps = (ps2, ps3)[tt % 2]
                    for kk in range(4):
                        off = kk + tt * 512
                        dg = (diag_a, diag_b)[kk % 2]
                        dgk = ((diag_a, None) if kk % 2 == 0 else (diag_b, None))
                        P.op("vector", lambda e, dg=dg, mc=mc, kk=kk, co=cwoff: e.tensor_scalar(
                            dg[:, :], C.idn[:], cw_sb[:, mc, co + kk:co + kk + 1], None, ALU.mult),
                            reads=[C.idn, cw_sb], writes=[dgk], tag="dgc")
                        mm(P, ps[:, :], dg[:, :], CV[:, off:off + 512],
                           reads=[dgk] + cvr, wkey=(ps, "g"),
                           start=(kk == 0), stop=(kk == 3))
                    if pi == 2:
                        dst = v1s(mc)[:, 64 + tt * 512:64 + (tt + 1) * 512]
                        dk_ = (SCRA, f"v{mc}")
                    else:
                        dst = qkst[:, tt * 512:(tt + 1) * 512]
                        dk_ = (qkst, f"t{tt}")
                    P.op("scalar", lambda e, dst=dst, ps=ps: e.activation(dst, ps[:, :], AF.Silu, bias=0.0, scale=1.0),
                         reads=[(ps, "g")], writes=[dk_])
                if pi != 2:
                    P.dma("sync", lambda e, scr=scr, b=b, mc=mc: e.dma_start(
                        out=scr[b, mc], in_=qkst[:, :]),
                        reads=[(qkst, f"t{t_}") for t_ in range(4)],
                        writes=[(scr, f"{b}_{mc}")], tag=f"qk{pi}{mc}")
        for tt in range(4):
            ps = (ps2, ps3)[tt % 2]
            for k in range(8):
                mm(P, ps[0:4, :], wb_sb[:, k, :], xn1[:, k, tt * 512:(tt + 1) * 512],
                   reads=[wb_sb, xn1], wkey=(ps, "g"), start=(k == 0), stop=(k == 7))
            P.op("scalar", lambda e, ps=ps, tt=tt: e.activation(
                beta_fm[:, tt * 512:(tt + 1) * 512], ps[0:4, :], AF.Sigmoid, bias=0.0, scale=1.0),
                reads=[(ps, "g")], writes=[(beta_fm, f"t{tt}")])
        for tc in range(16):
            P.op("tensor", lambda e, tc=tc: e.transpose(
                psTt[:, 0:4], beta_fm[0:4, tc * 128:(tc + 1) * 128], C.idn[0:4, 0:4]),
                reads=[(beta_fm, f"t{tc // 4}"), C.idn], writes=[(psTt, "t")])
            P.op("vector", lambda e, tc=tc: e.tensor_copy(beta_tm[:, tc, :], psTt[:, 0:4]),
                 reads=[(psTt, "t")], writes=[(beta_tm, f"t{tc}")])

        for mc in range(8):
            for (taps, wt_, scr, keyp) in ((3, fsw, fsscr, "fs"), (63, flw, flscr, "fl")):
                for tt in range(4):
                    ps = (ps0, ps1)[tt % 2]
                    for kk in range(taps):
                        off = 64 - (taps - 1) + kk + tt * 512
                        dg = (diag_a, diag_b)[kk % 2]
                        dgk = ((diag_a, None) if kk % 2 == 0 else (diag_b, None))
                        P.op("vector", lambda e, dg=dg, wt_=wt_, mc=mc, kk=kk: e.tensor_scalar(
                            dg[:, :], C.idn[:], wt_[:, mc, kk:kk + 1], None, ALU.mult),
                            reads=[C.idn, wt_], writes=[dgk], tag="dgb")
                        mm(P, ps[:, :], dg[:, :], v1s(mc)[:, off:off + 512],
                           reads=[dgk, (SCRA, f"v{mc}")], wkey=(ps, "g"),
                           start=(kk == 0), stop=(kk == taps - 1))
                    P.op("vector", lambda e, ps=ps, tt=tt: e.tensor_copy(
                        qkst[:, tt * 512:(tt + 1) * 512], ps[:, :]),
                        reads=[(ps, "g")], writes=[(qkst, f"f{tt}")])
                P.dma("sync", lambda e, scr=scr, b=b, mc=mc: e.dma_start(
                    out=scr[b, mc], in_=qkst[:, :]),
                    reads=[(qkst, f"f{t_}") for t_ in range(4)],
                    writes=[(scr, f"{b}_{mc}")], tag=f"{keyp}{mc}")
        for h in range(4):
            retag(ARS, ARSDK + ARSQK, tag=f"rtars{h}")
            P.dma("sync", lambda e, b=b, h=h: e.dma_start(out=q1h[:, 0, :], in_=q1scr[b, 2 * h]),
                  reads=[(q1scr, f"{b}_{2 * h}")], writes=[(ARS, "q0")])
            P.dma("sync", lambda e, b=b, h=h: e.dma_start(out=q1h[:, 1, :], in_=q1scr[b, 2 * h + 1]),
                  reads=[(q1scr, f"{b}_{2 * h + 1}")], writes=[(ARS, "q1")])
            P.dma("scalar", lambda e, b=b, h=h: e.dma_start(out=k1h[:, 0, :], in_=k1scr[b, 2 * h]),
                  reads=[(k1scr, f"{b}_{2 * h}")], writes=[(ARS, "k0")])
            P.dma("scalar", lambda e, b=b, h=h: e.dma_start(out=k1h[:, 1, :], in_=k1scr[b, 2 * h + 1]),
                  reads=[(k1scr, f"{b}_{2 * h + 1}")], writes=[(ARS, "k1")])
            retag(ARW, SCANK + TMPK + [f"vb{t}" for t in range(16)] + [f"kn{t}" for t in range(16)] + [f"kb{t}" for t in range(16)], tag=f"rtarw{h}")
            for tc in range(16):
                for dc in range(2):
                    P.op("tensor", lambda e, tc=tc, dc=dc, h=h: e.transpose(
                        psTt[:, 0:128], v1s(2 * h + dc)[:, 64 + tc * 128:64 + (tc + 1) * 128], C.idn[:]),
                        reads=[(SCRA, f"v{2 * h + dc}"), C.idn], writes=[(psTt, "t")])
                    P.op("vector", lambda e, tc=tc, dc=dc: e.tensor_copy(
                        TMPa[:, tc, dc * 128:(dc + 1) * 128], psTt[:, 0:128]),
                        reads=[(psTt, "t")], writes=[(ARW, f"tmp{tc}_{dc}")])
                P.op("vector", lambda e, tc=tc, h=h: e.tensor_scalar(
                    vb_tm[:, tc, :], TMPa[:, tc, :], beta_tm[:, tc, h:h + 1], None, ALU.mult),
                    reads=[(ARW, f"tmp{tc}_0"), (ARW, f"tmp{tc}_1"), (beta_tm, f"t{tc}")],
                    writes=[(ARW, f"vb{tc}")])
            for (which, srcv, sm) in (("k", k1h, smallA), ("q", q1h, smallB)):
                for tc in range(16):
                    for dc in range(2):
                        P.op("tensor", lambda e, tc=tc, dc=dc, srcv=srcv: e.transpose(
                            psTt[:, 0:128], srcv[:, dc, tc * 128:(tc + 1) * 128], C.idn[:]),
                            reads=[(ARS, f"{which}{dc}"), C.idn], writes=[(psTt, "t")])
                        P.op("vector", lambda e, tc=tc, dc=dc: e.tensor_copy(
                            TMPa[:, tc, dc * 128:(dc + 1) * 128], psTt[:, 0:128]),
                            reads=[(psTt, "t")], writes=[(ARW, f"tmp{tc}_{dc}")])
                    P.op("scalar", lambda e, tc=tc, sm=sm: e.activation(
                        ps4[:, 0:256], TMPa[:, tc, :], AF.Square, bias=0.0, scale=1.0,
                        accum_out=sm[:, 0:1]),
                        reads=[(ARW, f"tmp{tc}_0"), (ARW, f"tmp{tc}_1")],
                        writes=[(ps4, "g"), (sm, "n")])
                    P.op("vector", lambda e, sm=sm: e.tensor_scalar(sm[:, 1:2], sm[:, 0:1], 1.0, 1e-6, ALU.mult, ALU.add),
                         reads=[(sm, "n")], writes=[(sm, "n1")])
                    P.op("scalar", lambda e, sm=sm: e.activation(sm[:, 2:3], sm[:, 1:2], AF.Sqrt, bias=0.0, scale=1.0),
                         reads=[(sm, "n1")], writes=[(sm, "n2")])
                    P.op("vector", lambda e, sm=sm: e.reciprocal(sm[:, 3:4], sm[:, 2:3]),
                         reads=[(sm, "n2")], writes=[(sm, "n3")])
                    if which == "k":
                        P.op("vector", lambda e, tc=tc, sm=sm: e.tensor_scalar(
                            kn_tm[:, tc, :], TMPa[:, tc, :], sm[:, 3:4], None, ALU.mult),
                            reads=[(ARW, f"tmp{tc}_0"), (ARW, f"tmp{tc}_1"), (sm, "n3")],
                            writes=[(ARW, f"kn{tc}")])
                        P.op("vector", lambda e, tc=tc, h=h: e.tensor_scalar(
                            kb_tm[:, tc, :], kn_tm[:, tc, :], beta_tm[:, tc, h:h + 1], None, ALU.mult),
                            reads=[(ARW, f"kn{tc}"), (beta_tm, f"t{tc}")], writes=[(ARW, f"kb{tc}")])
                    else:
                        P.op("vector", lambda e, tc=tc, sm=sm: e.tensor_scalar(
                            TMPa[:, tc, :], TMPa[:, tc, :], sm[:, 3:4], None, ALU.mult),
                            reads=[(ARW, f"tmp{tc}_0"), (ARW, f"tmp{tc}_1"), (sm, "n3")],
                            writes=[(ARW, f"tmp{tc}_0"), (ARW, f"tmp{tc}_1")])
                if which == "k":
                    for (src, dstv, kp_, srckeys) in (
                            (kn_tm, kn_fm, "knf", "kn"), (kb_tm, kb_fm, "kbf", "kb")):
                        for tc in range(16):
                            for dc in range(2):
                                P.op("tensor", lambda e, tc=tc, dc=dc, src=src: e.transpose(
                                    psTt[:, 0:128], src[:, tc, dc * 128:(dc + 1) * 128], C.idn[:]),
                                    reads=[(ARW, f"{srckeys}{tc}"), C.idn], writes=[(psTt, "t")])
                                P.op("vector", lambda e, tc=tc, dc=dc, dstv=dstv, kp_=kp_: e.tensor_copy(
                                    dstv[:, dc, tc * 128:(tc + 1) * 128], psTt[:, 0:128]),
                                    reads=[(psTt, "t")], writes=[(ARX, f"{kp_}{dc}_{tc}")])
                else:
                    for tc in range(16):
                        for dc in range(2):
                            P.op("tensor", lambda e, tc=tc, dc=dc: e.transpose(
                                psTt[:, 0:128], TMPa[:, tc, dc * 128:(dc + 1) * 128], C.idn[:]),
                                reads=[(ARW, f"tmp{tc}_0"), (ARW, f"tmp{tc}_1"), C.idn],
                                writes=[(psTt, "t")])
                            P.op("vector", lambda e, tc=tc, dc=dc: e.tensor_copy(
                                qn_fm[:, dc, tc * 128:(tc + 1) * 128], psTt[:, 0:128]),
                                reads=[(psTt, "t")], writes=[(ARX, f"qn{dc}_{tc}")])
            for c in range(16):
                ps = (ps2, ps3)[c % 2]
                for dc in range(2):
                    mm(P, ps[:, 0:128], kn_fm[:, dc, c * 128:(c + 1) * 128],
                       qn_fm[:, dc, c * 128:(c + 1) * 128],
                       reads=[(ARX, f"knf{dc}_{c}"), (ARX, f"qn{dc}_{c}")], wkey=(ps, "g"),
                       start=(dc == 0), stop=(dc == 1))
                P.op("vector", lambda e, ps=ps, c=c: e.tensor_tensor(
                    attnT[:, c, :], ps[:, 0:128], C.mtriu[:], ALU.mult),
                    reads=[(ps, "g"), C.mtriu], writes=[(ARX, f"at{c}")])
            retag(ARW, TMPK + SCANK, tag=f"rtscan{h}")
            retag(ARS, ARSQK + ARSDK, tag=f"rtsdf{h}")
            P.op("vector", lambda e: e.memset(Ssb[:], 0.0), writes=[(ARW, "Ssb")])
            for c in range(16):
                p_ = c % 2
                N_, Q_, qa_, qb_, na_, nb_, R_ = [chs(sl + 8 * p_) for sl in range(7)]
                kN, kQ, kqa, kqb, kna, knb, kR = [
                    f"ch{s}{p_}" for s in ("N", "Q", "qa", "qb", "na", "nb", "R")]
                psN = (ps2, ps3)[c % 2]
                psQ_ = (ps0, ps1)[c % 2]
                for dc in range(2):
                    mm(P, psN[:, 0:128], kb_fm[:, dc, c * 128:(c + 1) * 128],
                       kn_fm[:, dc, c * 128:(c + 1) * 128],
                       reads=[(ARX, f"kbf{dc}_{c}"), (ARX, f"knf{dc}_{c}")], wkey=(psN, "g"),
                       start=(dc == 0), stop=(dc == 1))
                for dc in range(2):
                    mm(P, psQ_[:, 0:128], kn_fm[:, dc, c * 128:(c + 1) * 128],
                       kb_fm[:, dc, c * 128:(c + 1) * 128],
                       reads=[(ARX, f"knf{dc}_{c}"), (ARX, f"kbf{dc}_{c}")], wkey=(psQ_, "g"),
                       start=(dc == 0), stop=(dc == 1))
                P.op("vector", lambda e, psN=psN, N_=N_: e.tensor_tensor(N_, psN[:, 0:128], C.mlow[:], ALU.mult),
                     reads=[(psN, "g"), C.mlow], writes=[(ARX, kN)])
                P.op("vector", lambda e, psQ_=psQ_, Q_=Q_: e.tensor_tensor(Q_, psQ_[:, 0:128], C.mup[:], ALU.mult),
                     reads=[(psQ_, "g"), C.mup], writes=[(ARX, kQ)])
                P.op("vector", lambda e, Q_=Q_, R_=R_: e.tensor_tensor(R_, Q_, C.idn[:], ALU.add),
                     reads=[(ARX, kQ), C.idn], writes=[(ARX, kR)])
                curNk, curQk, curNap, curQap = kN, kQ, N_, Q_
                for i in range(6):
                    psA = (ps2, ps3)[(c + i + 1) % 2]
                    psB = (ps0, ps1)[(c + i + 1) % 2]
                    mm(P, psA[:, 0:128], curNap, curQap,
                       reads=[(ARX, curNk), (ARX, curQk)], wkey=(psA, "g"))
                    mm(P, psB[:, 0:128], curQap, curNap,
                       reads=[(ARX, curQk), (ARX, curNk)], wkey=(psB, "g"))
                    newQ, kq = ((qa_, kqa), (qb_, kqb))[i % 2]
                    newN, kn = ((na_, kna), (nb_, knb))[i % 2]
                    P.op("vector", lambda e, psA=psA, newQ=newQ: e.tensor_copy(newQ, psA[:, 0:128]),
                         reads=[(psA, "g")], writes=[(ARX, kq)])
                    P.op("scalar", lambda e, psB=psB, newN=newN: e.copy(newN, psB[:, 0:128]),
                         reads=[(psB, "g")], writes=[(ARX, kn)])
                    mm(P, psA[:, 0:128], newN, R_,
                       reads=[(ARX, kn), (ARX, kR)], wkey=(psA, "g"),
                       start=True, stop=False)
                    mm(P, psA[:, 0:128], C.idn[:], R_,
                       reads=[C.idn, (ARX, kR)], wkey=(psA, "g"), start=False, stop=True)
                    P.op("vector", lambda e, psA=psA, R_=R_: e.tensor_copy(R_, psA[:, 0:128]),
                         reads=[(psA, "g")], writes=[(ARX, kR)])
                    curNk, curQk, curNap, curQap = kn, kq, newN, newQ
                mm(P, ps0[:, 0:256], R_, vb_tm[:, c, :],
                   reads=[(ARX, kR), (ARW, f"vb{c}")], wkey=(ps0, "g"))
                P.op("vector", lambda e: e.tensor_copy(u_sb[:], ps0[:, 0:256]),
                     reads=[(ps0, "g")], writes=[(ARW, "u")])
                for dc in range(2):
                    mm(P, ps1[:, 0:128], kb_tm[:, c, dc * 128:(dc + 1) * 128], R_,
                       reads=[(ARW, f"kb{c}"), (ARX, kR)], wkey=(ps1, "g"))
                    P.op("vector", lambda e, dc=dc: e.tensor_copy(
                        wfm_c[:, dc * 128:(dc + 1) * 128], ps1[:, 0:128]),
                        reads=[(ps1, "g")], writes=[(ARW, f"wfm{dc}")])
                for dc in range(2):
                    mm(P, ps0[:, 0:256], wfm_c[:, dc * 128:(dc + 1) * 128],
                       Ssb[:, dc * 256:(dc + 1) * 256],
                       reads=[(ARW, f"wfm{dc}"), (ARW, "Ssb")], wkey=(ps0, "g"),
                       start=(dc == 0), stop=(dc == 1))
                P.op("vector", lambda e: e.tensor_tensor(u2_sb[:], u_sb[:], ps0[:, 0:256], ALU.subtract),
                     reads=[(ARW, "u"), (ps0, "g")], writes=[(ARW, "u2")])
                for dvc in range(2):
                    ps_ = (ps2, ps3)[dvc]
                    for dc in range(2):
                        mm(P, ps_[:, 0:128], Ssb[:, dc * 256 + dvc * 128:dc * 256 + (dvc + 1) * 128],
                           qn_fm[:, dc, c * 128:(c + 1) * 128],
                           reads=[(ARW, "Ssb"), (ARX, f"qn{dc}_{c}")], wkey=(ps_, "g"),
                           start=(dc == 0), stop=False)
                    mm(P, ps_[:, 0:128], u2_sb[:, dvc * 128:(dvc + 1) * 128], attnT[:, c, :],
                       reads=[(ARW, "u2"), (ARX, f"at{c}")], wkey=(ps_, "g"),
                       start=False, stop=True)
                    P.op("vector", lambda e, ps_=ps_, c=c, dvc=dvc: e.tensor_copy(
                        dfm[dvc][:, c * 128:(c + 1) * 128], ps_[:, 0:128]),
                        reads=[(ps_, "g")], writes=[(ARS, f"d{dvc}_{c}")])
                for dc in range(2):
                    mm(P, psS[:, dc * 256:(dc + 1) * 256], kn_tm[:, c, dc * 128:(dc + 1) * 128],
                       u2_sb[:, 0:256],
                       reads=[(ARW, f"kn{c}"), (ARW, "u2")], wkey=(psS, "S"),
                       start=(c == 0 and dc == 0), stop=(c == 15 and dc == 1))
                P.op("vector", lambda e: e.tensor_copy(Ssb[:], psS[:, :]),
                     reads=[(psS, "S")], writes=[(ARW, "Ssb")])
            for dvc in range(2):
                P.dma("sync", lambda e, b=b, h=h, dvc=dvc: e.dma_start(
                    out=dlscr[b, 2 * h + dvc], in_=dfm[dvc][:, :]),
                    reads=[(ARS, f"d{dvc}_{c_}") for c_ in range(16)],
                    writes=[(dlscr, f"{b}_{2 * h + dvc}")], tag=f"dl{h}{dvc}")

        retag(ARW, [f"w{i}_{k}" for i in range(2) for k in range(8)] + ["brstg"] + [f"g1w{k}" for k in range(8)], tag="rtstat")
        for br_i, scr in enumerate((fsscr, flscr, dlscr, None)):
            for h in range(4):
                psA, psB = (ps0, ps1) if (h % 2 == 0) else (ps2, ps3)
                for dc in range(2):
                    ch_ = 2 * h + dc
                    if scr is not None:
                        P.dma("sync", lambda e, scr=scr, b=b, ch_=ch_: e.dma_start(
                            out=brstg[:], in_=scr[b, ch_]),
                            reads=[(scr, f"{b}_{ch_}")], writes=[(ARW, "brstg")], tag="stl")
                        src_ap = brstg
                        src_k = (ARW, "brstg")
                    else:
                        src_ap = v1s(ch_)[:, 64:2112]
                        src_k = (SCRA, f"v{ch_}")
                    P.op("scalar", lambda e, src_ap=src_ap: e.activation(
                        sq_st[:, :], src_ap, AF.Square, bias=0.0, scale=1.0),
                        reads=[src_k], writes=[(sq_st, None)])
                    for tc in range(16):
                        mm(P, psA[:, tc:tc + 1], src_ap[:, tc * 128:(tc + 1) * 128],
                           C.onescol[:, 0:1],
                           reads=[src_k, C.onescol], wkey=(psA, "g"),
                           start=(dc == 0 and tc == 0), stop=(dc == 1 and tc == 15))
                        mm(P, psB[:, tc:tc + 1], sq_st[:, tc * 128:(tc + 1) * 128],
                           C.onescol[:, 0:1],
                           reads=[(sq_st, None), C.onescol], wkey=(psB, "g"),
                           start=(dc == 0 and tc == 0), stop=(dc == 1 and tc == 15))
                cm = br_i * 8 + h
                cs = br_i * 8 + 4 + h
                P.op("vector", lambda e, psA=psA, cm=cm: e.tensor_scalar(
                    stats_tm[:, :, cm:cm + 1], psA[:, 0:16].rearrange("p (a one) -> p a one", one=1),
                    1.0 / 256, None, ALU.mult),
                    reads=[(psA, "g")], writes=[(stats_tm, f"c{cm}")])
                P.op("vector", lambda e, psB=psB, cs=cs: e.tensor_scalar(
                    stats_tm[:, :, cs:cs + 1], psB[:, 0:16].rearrange("p (a one) -> p a one", one=1),
                    1.0 / 256, None, ALU.mult),
                    reads=[(psB, "g")], writes=[(stats_tm, f"c{cs}")])
        STK = [(stats_tm, f"c{i}") for i in range(32)]
        P.op("gpsimd", lambda e: e.memset(stats32[:, :], 0.0),
             writes=[(stats32, f"t{t_}") for t_ in range(16)] + [(stats32, "z")])
        for tc in range(16):
            mview = stats_tm[:, tc, :].rearrange("p (br k) -> p br k", k=8)[:, :, 0:4]
            svview = stats_tm[:, tc, :].rearrange("p (br k) -> p br k", k=8)[:, :, 4:8]
            tview = smallA[:, 0:16].rearrange("p (br k) -> p br k", k=4)
            P.op("vector", lambda e, mview=mview, tview=tview: e.tensor_tensor(
                tview, mview, mview, ALU.mult), reads=STK, writes=[(smallA, "m2")])
            P.op("vector", lambda e, svview=svview, tview=tview: e.tensor_tensor(
                svview, svview, tview, ALU.subtract),
                reads=STK + [(smallA, "m2")], writes=STK)
            P.op("vector", lambda e, svview=svview: e.tensor_scalar_max(
                svview, svview, 1e-6), reads=STK, writes=STK)
            P.op("scalar", lambda e, svview=svview: e.activation(
                svview, svview, AF.Sqrt, bias=0.0, scale=1.0), reads=STK, writes=STK)
            P.op("tensor", lambda e, tc=tc: e.transpose(
                ps5[0:32, 0:128], stats_tm[:, tc, :], C.idnf[:]),
                reads=STK, writes=[(ps5, "g")])
            P.op("vector", lambda e, tc=tc: e.tensor_copy(
                stats32[0:32, tc * 128:(tc + 1) * 128], ps5[0:32, 0:128]),
                reads=[(ps5, "g")], writes=[(stats32, f"t{tc}")])
        if dbg and b == 0:
            P.dma("sync", lambda e: e.dma_start(out=dbg_stats[:], in_=stats_tm[:, :, :]),
                  reads=STK)

        if dbg and b == 0:
            P.dma("sync", lambda e: e.dma_start(out=dbg_xn1[:], in_=xn1[:, :, :]), reads=[xn1])
        retag(ARW, ["brstg"] + [f"g1w{k}" for k in range(8)], tag="rtgate")
        h1g = qkst
        retag(qkst, [None, "s0", "s1", "s1l"] + [f"t{t_}" for t_ in range(4)], tag="rth1g")
        for k in range(8):
            P.dma("gpsimd", lambda e, k=k: e.dma_start(
                out=w1v[:, k, :], in_=w1_d[k * 128:(k + 1) * 128, :]),
                writes=[(ARW, f"g1w{k}")], tag=f"w1l{k}")
        if dbg and b == 0:
            P.dma("sync", lambda e: e.dma_start(out=dbg_w1[:], in_=w1v[:, 0, :]),
                  reads=[(ARW, "g1w0")])
        for mh in range(16):
            for tt in range(4):
                ps = (ps0, ps1, ps4, ps2)[tt]
                for k in range(8):
                    mm(P, ps[:, :], w1v[:, k, mh * 128:(mh + 1) * 128],
                       xn1[:, k, tt * 512:(tt + 1) * 512],
                       reads=[(ARW, f"g1w{k}"), xn1], wkey=(ps, "g"),
                       start=(k == 0), stop=False)
                mm(P, ps[:, :], w1c9[:, mh * 128:(mh + 1) * 128],
                   stats32[:, tt * 512:(tt + 1) * 512],
                   reads=[w1c9, (stats32, "z")] + [(stats32, f"t{t_}") for t_ in range(4 * tt, 4 * tt + 4)],
                   wkey=(ps, "g"), start=False, stop=True)
                if dbg and b == 0 and mh == 0 and tt == 0:
                    P.op("vector", lambda e, ps=ps: e.tensor_copy(OUTF[:, 0:512], ps[:, 0:512]),
                         reads=[(ps, "g")], writes=[OUTF])
                    P.dma("sync", lambda e: e.dma_start(out=dbg_pre[:], in_=OUTF[:, 0:512]),
                          reads=[OUTF])
                P.op("scalar", lambda e, ps=ps, tt=tt, mh=mh: e.activation(
                    h1g[:, tt * 512:(tt + 1) * 512], ps[:, :], AF.Gelu, bias=b1c[:, mh:mh + 1], scale=1.0),
                    reads=[(ps, "g"), b1c], writes=[(qkst, f"t{tt}")])
                if dbg and b == 0 and mh == 0 and tt == 0:
                    P.dma("sync", lambda e: e.dma_start(out=dbg_h1[:], in_=h1g[:, 0:512]),
                          reads=[(qkst, "t0")])
                mm(P, ps5[0:16, 0:512], w2_sb[:, mh, :], h1g[:, tt * 512:(tt + 1) * 512],
                   reads=[w2_sb, (qkst, f"t{tt}")], wkey=(ps5, "g"))
                if mh == 0:
                    P.op("vector", lambda e, tt=tt: e.tensor_copy(
                        logits_fm[:, tt * 512:(tt + 1) * 512], ps5[0:16, 0:512]),
                        reads=[(ps5, "g")], writes=[(logits_fm, f"t{tt}")])
                else:
                    P.op("vector", lambda e, tt=tt: e.tensor_tensor(
                        logits_fm[:, tt * 512:(tt + 1) * 512],
                        logits_fm[:, tt * 512:(tt + 1) * 512], ps5[0:16, 0:512], ALU.add),
                        reads=[(ps5, "g"), (logits_fm, f"t{tt}")], writes=[(logits_fm, f"t{tt}")])
        P.op("vector", lambda e: e.tensor_scalar(logits_fm[:, :], logits_fm[:, :], b2r[:, 0:1], None, ALU.add),
             reads=[b2r] + [(logits_fm, f"t{t_}") for t_ in range(4)], writes=[(logits_fm, None)])
        if dbg and b == 0:
            P.dma("sync", lambda e: e.dma_start(out=dbg_lgf[:], in_=logits_fm[:, :]),
                  reads=[(logits_fm, None)])
            P.dma("sync", lambda e: e.dma_start(out=dbg_s32[:], in_=stats32[:, :]),
                  reads=[(stats32, "z")] + [(stats32, f"t{t_}") for t_ in range(16)])
        for tc in range(16):
            P.op("tensor", lambda e, tc=tc: e.transpose(
                psTt[:, 0:16], logits_fm[0:16, tc * 128:(tc + 1) * 128], C.idn[0:16, 0:16]),
                reads=[(logits_fm, None), C.idn], writes=[(psTt, "t")])
            P.op("vector", lambda e, tc=tc: e.tensor_tensor(
                lg_tm[:, tc, :], psTt[:, 0:16], rtb[:, :], ALU.mult),
                reads=[(psTt, "t"), rtb], writes=[(lg_tm, f"t{tc}")])
            l4 = lg_tm[:, tc, :].rearrange("p (h w) -> p h w", h=4)
            P.op("vector", lambda e, l4=l4: e.tensor_reduce(
                smallA[:, 0:4].rearrange("p (h w) -> p h w", h=4), l4, AX.X, ALU.max),
                reads=[(lg_tm, f"t{tc}")], writes=[(smallA, "gm")])
            P.op("vector", lambda e, l4=l4: e.tensor_tensor(
                l4, l4, smallA[:, 0:4].rearrange("p (h w) -> p h w", w=1).broadcast_to((128, 4, 4)),
                ALU.subtract),
                reads=[(lg_tm, f"t{tc}"), (smallA, "gm")], writes=[(lg_tm, f"t{tc}")])
            P.op("scalar", lambda e, tc=tc: e.activation(
                w_tm[:, tc, :], lg_tm[:, tc, :], AF.Exp, bias=0.0, scale=1.0),
                reads=[(lg_tm, f"t{tc}")], writes=[(w_tm, f"t{tc}")])
            w4 = w_tm[:, tc, :].rearrange("p (h w) -> p h w", h=4)
            P.op("vector", lambda e, w4=w4: e.tensor_reduce(
                smallB[:, 0:4].rearrange("p (h w) -> p h w", h=4), w4, AX.X, ALU.add),
                reads=[(w_tm, f"t{tc}")], writes=[(smallB, "gs")])
            P.op("vector", lambda e: e.reciprocal(smallB[:, 4:8], smallB[:, 0:4]),
                 reads=[(smallB, "gs")], writes=[(smallB, "gr")])
            P.op("vector", lambda e, w4=w4: e.tensor_tensor(
                w4, w4, smallB[:, 4:8].rearrange("p (h w) -> p h w", w=1).broadcast_to((128, 4, 4)),
                ALU.mult),
                reads=[(w_tm, f"t{tc}"), (smallB, "gr")], writes=[(w_tm, f"t{tc}")])
            P.op("tensor", lambda e, tc=tc: e.transpose(
                ps5[0:16, 0:128], w_tm[:, tc, :], C.idnf[:]),
                reads=[(w_tm, f"t{tc}")], writes=[(ps5, "g")])
            P.op("vector", lambda e, tc=tc: e.tensor_copy(
                wfm16[0:16, tc * 128:(tc + 1) * 128], ps5[0:16, 0:128]),
                reads=[(ps5, "g")], writes=[(wfm16, f"t{tc}")])
        if dbg and b == 0:
            P.dma("sync", lambda e: e.dma_start(out=dbg_w[:], in_=w_tm[:, :, :]),
                  reads=[(w_tm, f"t{t_}") for t_ in range(16)])
            P.dma("sync", lambda e: e.dma_start(out=dbg_lg[:], in_=lg_tm[:, :, :]),
                  reads=[(lg_tm, f"t{t_}") for t_ in range(16)])
            P.dma("sync", lambda e: e.dma_start(out=dbg_rtb[:], in_=rtb[:, :]), reads=[rtb])

        if dbg and b == 0:
            P.dma("scalar", lambda e: e.dma_start(out=dbg_wfm[:], in_=wfm16[:, :]),
                  reads=[(wfm16, f"t{t_}") for t_ in range(16)])
            P.dma("scalar", lambda e: e.dma_start(out=dbg_sel[:], in_=sel_sb[:, :]),
                  reads=[sel_sb])
        retag(ARX, [f"ch{s_}{p_}" for s_ in ("N", "Q", "qa", "qb", "na", "nb", "R") for p_ in range(2)]
              + [f"at{c}" for c in range(16)]
              + [f"{kp}{dc}_{tc}" for kp in ("qn", "knf", "kbf") for dc in range(2) for tc in range(16)]
              + [f"on{ch}" for ch in range(8)], tag="rtcomb")
        retag(ARW, ["brstg", "brs0", "brs1", "brl0", "brl1", "brd0", "brd1"]
              + [f"g1w{k}" for k in range(8)], tag="rtcomb2")
        wrow = rsd_row
        for h in range(4):
            for (vv, scr, kk_) in ((brs, fsscr, "brs"), (brl, flscr, "brl"), (brd, dlscr, "brd")):
                for dc in range(2):
                    P.dma("sync", lambda e, scr=scr, b=b, h=h, dc=dc, vv=vv: e.dma_start(
                        out=vv[:, dc, :], in_=scr[b, 2 * h + dc]),
                        reads=[(scr, f"{b}_{2 * h + dc}")],
                        writes=[(ARW, f"{kk_}{dc}")], tag=f"cb{kk_}{dc}")
            for br in range(4):
                r = h * 4 + br
                P.dma("scalar", lambda e, r=r: e.dma_start(out=wrow, in_=wfm16[r:r + 1, :]),
                      reads=[(wfm16, f"t{t_}") for t_ in range(16)],
                      writes=[(beta_fm, "row")], tag=f"wrow{r}")
                for dc in range(2):
                    ch_ = 2 * h + dc
                    for tt in range(4):
                        sl = slice(tt * 512, (tt + 1) * 512)
                        bt, bk_ = ((brs[:, dc, sl], (ARW, f"brs{dc}")),
                                   (brl[:, dc, sl], (ARW, f"brl{dc}")),
                                   (brd[:, dc, sl], (ARW, f"brd{dc}")),
                                   (v1s(ch_)[:, 64 + tt * 512:64 + (tt + 1) * 512],
                                    (SCRA, f"v{ch_}")))[br]
                        ps_ = (ps0, ps1)[(dc * 4 + tt) % 2]
                        mm(P, ps_[:, :], C.onescol[0:1, 0:128], wrow[0:1, sl],
                           reads=[C.onescol, (beta_fm, "row")], wkey=(ps_, "wb"))
                        P.op("vector", lambda e, ps_=ps_: e.tensor_copy(wbst[:, :], ps_[:, :]),
                             reads=[(ps_, "wb")], writes=[wbst])
                        if br == 0:
                            P.op("vector", lambda e, bt=bt, ch_=ch_, sl=sl: e.tensor_tensor(
                                o_n[:, ch_, sl], bt, wbst[:, :], ALU.mult),
                                reads=[bk_, wbst], writes=[(ARX, f"on{ch_}")])
                        else:
                            P.op("vector", lambda e, bt=bt: e.tensor_tensor(
                                sq_st[:, 0:512], bt, wbst[:, :], ALU.mult),
                                reads=[bk_, wbst], writes=[(sq_st, "cb")])
                            P.op("vector", lambda e, ch_=ch_, sl=sl: e.tensor_tensor(
                                o_n[:, ch_, sl], o_n[:, ch_, sl], sq_st[:, 0:512], ALU.add),
                                reads=[(sq_st, "cb"), (ARX, f"on{ch_}")], writes=[(ARX, f"on{ch_}")])
            for dc in range(2):
                ch_ = 2 * h + dc
                P.op("scalar", lambda e, ch_=ch_: e.activation(
                    sq_st[:, :], o_n[:, ch_, :], AF.Square, bias=0.0, scale=1.0),
                    reads=[(ARX, f"on{ch_}")], writes=[(sq_st, None)])
                for tc in range(16):
                    mm(P, ps4[:, tc:tc + 1], sq_st[:, tc * 128:(tc + 1) * 128], C.onescol[:, 0:1],
                       reads=[(sq_st, None), C.onescol], wkey=(ps4, "g"),
                       start=(dc == 0 and tc == 0), stop=(dc == 1 and tc == 15))
            P.op("vector", lambda e: e.tensor_scalar(nrm16[:, :], ps4[:, 0:16], 1.0 / 256, 1e-5, ALU.mult, ALU.add),
                 reads=[(ps4, "g")], writes=[(nrm16, "a")])
            P.op("scalar", lambda e: e.activation(nrm16[:, :], nrm16[:, :], AF.Sqrt, bias=0.0, scale=1.0),
                 reads=[(nrm16, "a")], writes=[(nrm16, "b")])
            P.op("vector", lambda e: e.reciprocal(nrm16[:, :], nrm16[:, :]),
                 reads=[(nrm16, "b")], writes=[(nrm16, "c")])
            P.op("tensor", lambda e: e.transpose(ps5[0:16, 0:128], nrm16[:, :], C.idnf[:]),
                 reads=[(nrm16, "c")], writes=[(ps5, "g")])
            P.op("vector", lambda e: e.tensor_copy(rsd16[:, :], ps5[0:16, 0:128]),
                 reads=[(ps5, "g")], writes=[rsd16])
            P.dma("sync", lambda e: e.dma_start(out=rsd_dram[:, :], in_=rsd16[:, :]),
                  reads=[rsd16], writes=[(rsd_dram, "d")], tag="rsro")
            P.dma("gpsimd", lambda e: e.dma_start(out=rsd_row, in_=rsd_dram[:].rearrange("a f -> (a f)")[None, :]),
                  reads=[(rsd_dram, "d")], writes=[(beta_fm, "row")], tag="rsrow")
            if dbg and b == 0 and h == 0:
                P.dma("scalar", lambda e: e.dma_start(out=dbg_rsd[:], in_=rsd_row),
                      reads=[(beta_fm, "row")])
            for tt in range(4):
                mm(P, ps0[:, :], C.onescol[0:1, 0:128], rsd_row[0:1, tt * 512:(tt + 1) * 512],
                   reads=[C.onescol, (beta_fm, "row")], wkey=(ps0, "g"))
                for dc in range(2):
                    ch_ = 2 * h + dc
                    sl = slice(tt * 512, (tt + 1) * 512)
                    P.op("vector", lambda e, ch_=ch_, sl=sl, dc=dc: e.scalar_tensor_tensor(
                        o_n[:, ch_, sl], o_n[:, ch_, sl], onw[:, dc:dc + 1], ps0[:, :],
                        ALU.mult, ALU.mult),
                        reads=[(ARX, f"on{ch_}"), onw, (ps0, "g")], writes=[(ARX, f"on{ch_}")])

        if dbg and b == 0:
            P.dma("sync", lambda e: e.dma_start(out=dbg_on[:], in_=ARX[:, 0:16384]),
                  reads=[(ARX, f"on{ch}") for ch in range(8)])
        retag(ARW, [f"wo{k}" for k in range(8)] + ["brs0", "brs1", "brl0", "brl1", "brd0", "brd1", "brstg"], tag="rtwo")
        for k in range(8):
            P.dma("gpsimd", lambda e, k=k: e.dma_start(
                out=wov[:, k, :], in_=wo_d[k * 128:(k + 1) * 128, :]),
                writes=[(ARW, f"wo{k}")], tag=f"wol{k}")
        for tc in range(16):
            for ft in range(2):
                ps = (ps0, ps1)[ft]
                for ch_ in range(8):
                    mm(P, ps[:, :], o_n[:, ch_, tc * 128:(tc + 1) * 128],
                       wov[:, ch_, ft * 512:(ft + 1) * 512],
                       reads=[(ARX, f"on{ch_}"), (ARW, f"wo{ch_}")], wkey=(ps, "g"),
                       start=(ch_ == 0), stop=(ch_ == 7))
                P.op("vector", lambda e, ps=ps, ft=ft: e.tensor_copy(
                    qkst[:, ft * 512 + 0:ft * 512 + 512] if False else qkst[:, ft * 512:(ft + 1) * 512],
                    ps[:, :]),
                    reads=[(ps, "g")], writes=[(qkst, f"s{ft}")])
            P.dma("sync", lambda e, b=b, tc=tc: e.dma_start(
                out=s1scr[b, tc * 128:(tc + 1) * 128, :], in_=qkst[:, 0:1024]),
                reads=[(qkst, "s0"), (qkst, "s1")],
                writes=[(s1scr, f"{b}_{tc}")], tag=f"s1s{tc}")

        logf = logits_fm[0:8, :]
        retag(logits_fm, [f"t{t_}" for t_ in range(4)] + [None] + [f"lf{t_}" for t_ in range(4)], tag="rtlf")
        for tt in range(4):
            ps = (ps2, ps3)[tt % 2]
            for k in range(8):
                mm(P, ps[0:8, :], wg_sb[:, k, :], xn1[:, k, tt * 512:(tt + 1) * 512],
                   reads=[wg_sb, xn1], wkey=(ps, "g"), start=(k == 0), stop=(k == 7))
            P.op("vector", lambda e, ps=ps, tt=tt: e.tensor_scalar(
                logf[:, tt * 512:(tt + 1) * 512], ps[0:8, :], bgc[:, 0:1], None, ALU.add),
                reads=[(ps, "g"), bgc], writes=[(logits_fm, f"lf{tt}")])
        for tc in range(16):
            P.op("tensor", lambda e, tc=tc: e.transpose(
                psTt[:, 0:8], logf[0:8, tc * 128:(tc + 1) * 128], C.idn[0:8, 0:8]),
                reads=[(logits_fm, f"lf{tc // 4}"), C.idn], writes=[(psTt, "t")])
            P.op("vector", lambda e, tc=tc: e.tensor_copy(smallA[:, 0:8], psTt[:, 0:8]),
                 reads=[(psTt, "t")], writes=[(smallA, "lg")])
            P.op("vector", lambda e: e.reduce_max(smallB[:, 0:1], smallA[:, 0:8], axis=AX.X),
                 reads=[(smallA, "lg")], writes=[(smallB, "m1")])
            P.op("vector", lambda e: e.tensor_scalar(msk1[:, :], smallA[:, 0:8], smallB[:, 0:1], None, ALU.is_equal),
                 reads=[(smallA, "lg"), (smallB, "m1")], writes=[msk1])
            P.op("vector", lambda e: e.tensor_scalar(tmp8[:, :], msk1[:, :], -1e30, None, ALU.mult),
                 reads=[msk1], writes=[(tmp8, "a")])
            P.op("vector", lambda e: e.tensor_tensor(lmsk[:, :], smallA[:, 0:8], tmp8[:, :], ALU.add),
                 reads=[(smallA, "lg"), (tmp8, "a")], writes=[lmsk])
            P.op("vector", lambda e: e.reduce_max(smallB[:, 1:2], lmsk[:, :], axis=AX.X),
                 reads=[lmsk], writes=[(smallB, "m2")])
            P.op("vector", lambda e: e.tensor_scalar(msk2[:, :], lmsk[:, :], smallB[:, 1:2], None, ALU.is_equal),
                 reads=[lmsk, (smallB, "m2")], writes=[msk2])
            P.op("vector", lambda e: e.tensor_tensor(smallB[:, 2:3], smallB[:, 0:1], smallB[:, 1:2], ALU.subtract),
                 reads=[(smallB, "m1"), (smallB, "m2")], writes=[(smallB, "d")])
            P.op("scalar", lambda e: e.activation(smallB[:, 3:4], smallB[:, 2:3], AF.Sigmoid, bias=0.0, scale=1.0),
                 reads=[(smallB, "d")], writes=[(smallB, "w1")])
            P.op("vector", lambda e: e.tensor_scalar(smallB[:, 4:5], smallB[:, 3:4], -1.0, 1.0, ALU.mult, ALU.add),
                 reads=[(smallB, "w1")], writes=[(smallB, "w2")])
            P.op("vector", lambda e: e.tensor_scalar(tmp8[:, :], msk2[:, :], smallB[:, 4:5], None, ALU.mult),
                 reads=[msk2, (smallB, "w2")], writes=[(tmp8, "b")])
            P.op("vector", lambda e, tc=tc: e.scalar_tensor_tensor(
                comb_tm[:, tc, :], msk1[:, :], smallB[:, 3:4], tmp8[:, :], ALU.mult, ALU.add),
                reads=[msk1, (smallB, "w1"), (tmp8, "b")], writes=[(comb_tm, f"t{tc}")])
        if dbg and b == 0:
            P.dma("sync", lambda e: e.dma_start(out=dbg_comb[:], in_=comb_tm[:, :, :]),
                  reads=[(comb_tm, f"t{t_}") for t_ in range(16)])
        retag(SCRA, [f"v{mc}" for mc in range(8)] + [f"ga{tc}_{ft}" for tc in range(16) for ft in range(2)], tag="rtg1")
        retag(ARX, [f"on{ch}" for ch in range(8)] + [f"fw2_{k}" for k in range(16)], tag="rtw2")
        retag(ARW, [f"wo{k}" for k in range(8)] + [f"fw1_{k}" for k in range(8)], tag="rtw1")
        retag(ARS, ARSQK + ARSDK + [f"h1_{k}" for k in range(16)], tag="rth1")
        P.op("gpsimd", lambda e: e.memset(SCRA[:, 0:16384], 0.0),
             writes=[(SCRA, f"ga{tc}_{ft}") for tc in range(16) for ft in range(2)])
        for e_ in range(8):
            for k in range(8):
                P.dma("gpsimd", lambda e, k=k, e_=e_: e.dma_start(
                    out=w1e[:, k, :], in_=fw1_d[e_, k * 128:(k + 1) * 128, :]),
                    writes=[(ARW, f"fw1_{k}")], tag=f"fw1l{k}")
            for k in range(16):
                P.dma("gpsimd", lambda e, k=k, e_=e_: e.dma_start(
                    out=w2e[:, k, :], in_=fw2_d[e_, k * 128:(k + 1) * 128, :]),
                    writes=[(ARX, f"fw2_{k}")], tag=f"fw2l{k}")
            P.dma("sync", lambda e, e_=e_: e.dma_start(out=fb1c[:], in_=fb1_d[e_]),
                  writes=[fb1c], tag="fb1l")
            P.dma("gpsimd", lambda e, e_=e_: e.dma_start(
                out=fb2r[:, :], in_=fb2_d[0:1, e_ * 1024:(e_ + 1) * 1024]),
                writes=[fb2r], tag="fb2l")
            for tt in range(4):
                for mh in range(16):
                    ps = (ps0, ps1, ps4)[mh % 3]
                    for k in range(8):
                        mm(P, ps[:, :], w1e[:, k, mh * 128:(mh + 1) * 128],
                           xn1[:, k, tt * 512:(tt + 1) * 512],
                           reads=[(ARW, f"fw1_{k}"), xn1], wkey=(ps, "g"),
                           start=(k == 0), stop=(k == 7))
                    P.op("scalar", lambda e, ps=ps, mh=mh: e.activation(
                        h1f[:, mh, :], ps[:, :], AF.Relu, bias=fb1c[:, mh:mh + 1], scale=1.0),
                        reads=[(ps, "g"), fb1c], writes=[(ARS, f"h1_{mh}")])
                for tc2 in range(4):
                    tc = tt * 4 + tc2
                    for ft in range(2):
                        ps = (ps2, ps3, ps5)[(tc2 * 2 + ft) % 3]
                        for k in range(16):
                            mm(P, ps[:, :], h1f[:, k, tc2 * 128:(tc2 + 1) * 128],
                               w2e[:, k, ft * 512:(ft + 1) * 512],
                               reads=[(ARS, f"h1_{k}"), (ARX, f"fw2_{k}")], wkey=(ps, "g"),
                               start=(k == 0), stop=False)
                        mm(P, ps[:, :], C.onescol[0:1, 0:128], fb2r[0:1, ft * 512:(ft + 1) * 512],
                           reads=[C.onescol, fb2r], wkey=(ps, "g"), start=False, stop=True)
                        P.op("vector", lambda e, ps=ps, tc=tc, ft=ft, e_=e_: e.scalar_tensor_tensor(
                            g1acc[:, tc, ft * 512:(ft + 1) * 512], ps[:, :],
                            comb_tm[:, tc, e_:e_ + 1], g1acc[:, tc, ft * 512:(ft + 1) * 512],
                            ALU.mult, ALU.add),
                            reads=[(ps, "g"), (comb_tm, f"t{tc}"), (SCRA, f"ga{tc}_{ft}")],
                            writes=[(SCRA, f"ga{tc}_{ft}")])
        for tc in range(16):
            P.op("scalar", lambda e, tc=tc: e.activation(
                C.ln_xn[:, :], g1acc[:, tc, :], AF.Silu, bias=0.0, scale=1.0),
                reads=[(SCRA, f"ga{tc}_0"), (SCRA, f"ga{tc}_1")], writes=[C.ln_xn])
            P.dma("sync", lambda e, b=b, tc=tc: e.dma_start(
                out=qkst[:, 0:1024], in_=s1scr[b, tc * 128:(tc + 1) * 128, :]),
                reads=[(s1scr, f"{b}_{tc}")], writes=[(qkst, "s1l")], tag=f"s1l{tc}")
            P.dma("scalar", lambda e, b=b, tc=tc: e.dma_start(
                out=lnxt[:, :], in_=x1_d[b, tc * 128:(tc + 1) * 128, :]),
                writes=[lnxt], tag=f"x1l{tc}")
            P.op("vector", lambda e: e.tensor_tensor(sq_st[:, 0:1024], qkst[:, 0:1024], C.ln_xn[:, :], ALU.mult),
                 reads=[(qkst, "s1l"), C.ln_xn], writes=[(sq_st, "pr")])
            P.op("vector", lambda e: e.tensor_tensor(OUTF[:, :], lnxt[:, :], sq_st[:, 0:1024], ALU.add),
                 reads=[lnxt, (sq_st, "pr")], writes=[OUTF])
            P.dma("sync", lambda e, b=b, tc=tc: e.dma_start(
                out=out_d[b, tc * 128:(tc + 1) * 128, :], in_=OUTF[:, :]),
                reads=[OUTF], tag=f"outs{tc}")
    P.drain_dmas("sync")
    P.drain_dmas("gpsimd")
    P.drain_dmas("scalar")
    return C



def _consts():
    idn = np.eye(128, dtype=BF)
    onescol = np.ones((128, 128), dtype=BF)
    onesf = np.ones((128, 1), np.float32)
    return {"c_idn": idn, "c_ones": onescol, "c_onesf": onesf,
            "c_idnf": np.eye(128, dtype=np.float32)}


def _l0_inputs(inputs, b0):
    t = lambda a: np.ascontiguousarray(np.asarray(a).T)
    col8 = lambda a: np.ascontiguousarray(np.asarray(a).reshape(8, 128).T)
    d = {
        "x": np.ascontiguousarray(inputs["L0_attn_x"][b0:b0 + 2]),
        "l0a_wq": t(inputs["L0_attn_Wq"]), "l0a_wk": t(inputs["L0_attn_Wk"]),
        "l0a_wv": t(inputs["L0_attn_Wv"]), "l0a_wo": t(inputs["L0_attn_Wo"]),
        "l0a_bq": col8(inputs["L0_attn_bq"]), "l0a_bk": col8(inputs["L0_attn_bk"]),
        "l0a_bv": np.ascontiguousarray(inputs["L0_attn_bv"].reshape(1, 1024)),
        "l0a_bo": np.ascontiguousarray(inputs["L0_attn_bo"].reshape(1, 1024)),
        "l0f_wq": t(inputs["L0_ffn_Wq"]), "l0f_wk": t(inputs["L0_ffn_Wk"]),
        "l0f_bq": col8(inputs["L0_ffn_bq"]), "l0f_bk": col8(inputs["L0_ffn_bk"]),
        "l0f_ewq": np.ascontiguousarray(inputs["L0_ffn_eWq"].transpose(0, 2, 1)),
        "l0f_ewk": np.ascontiguousarray(inputs["L0_ffn_eWk"].transpose(0, 2, 1)),
        "l0f_ebq": np.ascontiguousarray(inputs["L0_ffn_ebq"].T),
        "l0f_ebk": np.ascontiguousarray(inputs["L0_ffn_ebk"].T),
        "l0f_wexp": np.ascontiguousarray(inputs["L0_ffn_Wexp"]),
        "l0f_bexp": np.ascontiguousarray(inputs["L0_ffn_bexp"].reshape(1, 8192)),
    }
    d.update(_consts())
    return d


def _l1_inputs(inputs, x1=None):
    cw = lambda a, g: np.ascontiguousarray(np.asarray(a).reshape(g, 128, -1).transpose(1, 0, 2).reshape(128, -1))
    sel = np.zeros((16, 2048), dtype=BF)
    for r in range(16):
        sel[r, r * 128:(r + 1) * 128] = 1
    mlow = -np.tril(np.ones((128, 128), np.float32), -1)
    mtriu = np.triu(np.ones((128, 128), np.float32), 0).astype(BF)
    mup = (-np.triu(np.ones((128, 128), np.float32), 1)).astype(BF)
    d = {
        "l1a_wq": np.ascontiguousarray(inputs["L1_attn_Wq"]),
        "l1a_wk": np.ascontiguousarray(inputs["L1_attn_Wk"]),
        "l1a_wv": np.ascontiguousarray(inputs["L1_attn_Wv"]),
        "l1a_wb": cw(inputs["L1_attn_Wb"], 8),
        "l1a_cq": cw(inputs["L1_attn_conv_q_w"], 8),
        "l1a_ck": cw(inputs["L1_attn_conv_k_w"], 8),
        "l1a_cv": cw(inputs["L1_attn_conv_v_w"], 8),
        "l1a_fs": cw(inputs["L1_attn_fir_short_w"].reshape(1024, 3), 8),
        "l1a_fl": cw(inputs["L1_attn_fir_long_w"].reshape(1024, 63), 8),
        "l1a_w1": np.ascontiguousarray(inputs["L1_attn_mlp_w1"][0:1024]),
        "l1a_w1c9": np.ascontiguousarray(np.pad(inputs["L1_attn_mlp_w1"][1024:1056], ((0, 96), (0, 0)))),
        "l1a_b1": np.ascontiguousarray(inputs["L1_attn_mlp_b1"].reshape(16, 128).T),
        "l1a_w2": cw(inputs["L1_attn_mlp_w2"], 16),
        "l1a_b2": np.ascontiguousarray(inputs["L1_attn_mlp_b2"].reshape(16, 1)),
        "l1a_glt": np.ascontiguousarray(inputs["L1_attn_gate_log_temp"].reshape(1, 4)),
        "l1a_onw": np.ascontiguousarray(inputs["L1_attn_o_norm_w"].reshape(2, 128).T),
        "l1a_wo": np.ascontiguousarray(inputs["L1_attn_Wo"]),
        "c_sel": sel, "c_mlow": mlow, "c_mtriu": mtriu, "c_mup": mup,
        "l1f_wg": cw(inputs["L1_ffn_Wg"], 8),
        "l1f_bg": np.ascontiguousarray(inputs["L1_ffn_bg"].reshape(8, 1)),
        "l1f_w1": np.ascontiguousarray(inputs["L1_ffn_W1"]),
        "l1f_b1": np.ascontiguousarray(inputs["L1_ffn_b1"].reshape(8, 16, 128).transpose(0, 2, 1)),
        "l1f_w2": np.ascontiguousarray(inputs["L1_ffn_W2"]),
        "l1f_b2": np.ascontiguousarray(inputs["L1_ffn_b2"].reshape(1, 8192)),
    }
    if x1 is not None:
        d["x1"] = np.ascontiguousarray(x1)
    d.update(_consts())
    return d


def build_l1_program(dbg=False):
    nc = bass.Bass("TRN2", target_bir_lowering=False, debug=False)
    P = Prog(nc)
    build_layer1(nc, P, dbg=dbg)
    P.finalize()
    return nc, P


def build_fused_program():
    nc = bass.Bass("TRN2", target_bir_lowering=False, debug=False)
    P = Prog(nc)
    C = setup_common(nc, P, need=("masks",))
    P.push_scope()
    build_layer0(nc, P, C=C, out_kind="Internal")
    P.pop_scope()
    P.layer_barrier()
    P.push_scope()
    build_layer1(nc, P, C=C)
    P.pop_scope()
    P.finalize()
    return nc, P


def build_l0_program(dbg=False):
    nc = bass.Bass("TRN2", target_bir_lowering=False, debug=False)
    P = Prog(nc)
    build_layer0(nc, P, dbg=dbg)
    P.finalize()
    return nc, P


def kernel(**inputs):
    nc, _ = build_fused_program()
    in_maps = []
    for c in range(8):
        m = _l0_inputs(inputs, 2 * c)
        m.update(_l1_inputs(inputs))
        in_maps.append(m)
    res = run_bass_kernel_spmd(nc, in_maps, core_ids=list(range(8)))
    return np.concatenate([r["out"] for r in res.results], axis=0)


# version 15
# speedup vs baseline: 1.1885x; 1.1885x over previous
import numpy as np
import ml_dtypes
import concourse.bass as bass
import concourse.mybir as mybir
from concourse.bass_utils import run_bass_kernel_spmd

F32 = mybir.dt.float32
BF16 = mybir.dt.bfloat16
AF = mybir.ActivationFunctionType
ALU = mybir.AluOpType
AX = mybir.AxisListType
BF = ml_dtypes.bfloat16

ENGINES = ("tensor", "vector", "scalar", "gpsimd", "sync")


def _regs(lst):
    out = []
    if lst is None:
        return out
    if isinstance(lst, tuple) and len(lst) == 2 and isinstance(lst[1], (str, int)):
        lst = [lst]
    elif not isinstance(lst, (list, tuple)):
        lst = [lst]
    for item in lst:
        if isinstance(item, tuple):
            t, k = item
        else:
            t, k = item, None
        if not hasattr(t, "name"):
            t = t.tensor
        out.append((t.name, k))
    return out


class Prog:
    def __init__(self, nc, dma_pool=40):
        self.nc = nc
        self.ops = {e: [] for e in ENGINES}
        self._ctx = []
        self.chain = {}
        self.count = {}
        for c in ENGINES:
            self._mk_chain(c)
        self.dma_sems_hw = [self._mk_chain(f"dqh{i}") for i in range(dma_pool)]
        self.dma_sems_sw = [self._mk_chain(f"dqs{i}") for i in range(16)]
        self._dma_rr = {"hw": 0, "sw": 0}
        self.state = {}
        self.tile_keys = {}
        self.covered = {e: {} for e in ENGINES}
        self.n_ops = 0
        self._scope = None

    def _mk_chain(self, name):
        cm = self.nc.semaphore(f"ch_{name}")
        self.chain[name] = cm.__enter__()
        self._ctx.append(cm)
        self.count[name] = 0
        return name

    def sbuf(self, name, shape, dtype):
        cm = self.nc.sbuf_tensor(name, shape, dtype)
        h = cm.__enter__()
        (self._scope if self._scope is not None else self._ctx).append(cm)
        return h

    def psum(self, name, shape, dtype=F32):
        cm = self.nc.psum_tensor(name, shape, dtype)
        h = cm.__enter__()
        (self._scope if self._scope is not None else self._ctx).append(cm)
        return h

    def push_scope(self):
        self._scope = []

    def pop_scope(self):
        for cm in reversed(self._scope):
            cm.__exit__(None, None, None)
        self._scope = None

    def layer_barrier(self):
        for e in ENGINES:
            for c, v in self.count.items():
                if v > 0 and self.covered[e].get(c, 0) < v and c != e:
                    sh = self.chain[c]
                    self.raw(e, lambda eng, sh=sh, v=v: eng.wait_ge(sh, v))
                self.covered[e][c] = max(self.covered[e].get(c, 0), self.count[c])
        self.state = {}
        self.tile_keys = {}

    def _st(self, rk):
        if rk not in self.state:
            self.state[rk] = {"w": None, "r": []}
            self.tile_keys.setdefault(rk[0], set()).add(rk[1])
        return self.state[rk]

    def _dep_read(self, rk):
        deps = []
        name, key = rk
        st = self._st(rk)
        if st["w"]:
            deps.append(st["w"])
        if key is not None:
            stw = self._st((name, None))
            if stw["w"]:
                deps.append(stw["w"])
        else:
            for k in self.tile_keys.get(name, ()):
                if k is None:
                    continue
                stk = self._st((name, k))
                if stk["w"]:
                    deps.append(stk["w"])
        return deps

    def _dep_write(self, rk):
        deps = []
        name, key = rk
        sts = [self._st(rk)]
        if key is not None:
            sts.append(self._st((name, None)))
        else:
            sts += [self._st((name, k)) for k in self.tile_keys.get(name, ())
                    if k is not None]
        for st in sts:
            if st["w"]:
                deps.append(st["w"])
            deps.extend(st["r"])
        return deps

    def _emit(self, eng, fn, rl, wl, mychain, inc_by, extra_waits=(), tag=None,
              strict=False):
        deps = []
        for rk in rl:
            deps += self._dep_read(rk)
        for wk in wl:
            deps += self._dep_write(wk)
        need = {}
        for (c, v) in deps:
            if eng == "tensor" and c == "tensor":
                continue
            if (not strict) and v <= self.covered[eng].get(c, 0):
                continue
            need[c] = max(need.get(c, 0), v)
        waits = list(extra_waits)
        for c, v in sorted(need.items()):
            waits.append((self.chain[c], v))
            self.covered[eng][c] = v
        self.count[mychain] += inc_by
        myval = self.count[mychain]
        mysem = self.chain[mychain]
        self.n_ops += 1

        def thunk(engine, waits=waits, fn=fn, mysem=mysem, inc_by=inc_by, tag=tag):
            for (s, v) in waits:
                engine.wait_ge(s, v)
            instr = fn(engine)
            assert instr is not None, f"op {tag} returned None"
            instr.then_inc(mysem, inc_by)
        self.ops[eng].append(thunk)

        me = (mychain, myval)
        for rk in rl:
            self._st(rk)["r"].append(me)
        for wk in wl:
            name, key = wk
            st = self._st(wk)
            st["w"] = me
            st["r"] = []
            if key is None:
                for k in list(self.tile_keys.get(name, ())):
                    if k is not None:
                        self.state[(name, k)] = {"w": me, "r": []}
        return myval

    def op(self, eng, fn, reads=None, writes=None, tag=None, strict=False):
        return self._emit(eng, fn, _regs(reads), _regs(writes), eng, 1, tag=tag,
                          strict=strict)

    def dma(self, eng, fn, reads=None, writes=None, tag=None):
        pool = "sw" if eng == "gpsimd" else "hw"
        sems = self.dma_sems_sw if pool == "sw" else self.dma_sems_hw
        c = sems[self._dma_rr[pool]]
        self._dma_rr[pool] = (self._dma_rr[pool] + 1) % len(sems)
        extra = []
        if self.count[c] > 0:
            v = self.count[c]
            if self.covered[eng].get(c, 0) < v:
                extra.append((self.chain[c], v))
                self.covered[eng][c] = v
        return self._emit(eng, fn, _regs(reads), _regs(writes), c, 16,
                          extra_waits=extra, tag=tag)

    def raw(self, eng, fn):
        self.ops[eng].append(fn)

    def drain_dmas(self, eng):
        for c in self.dma_sems_hw + self.dma_sems_sw:
            v = self.count[c]
            if v > 0 and self.covered[eng].get(c, 0) < v:
                s = self.chain[c]
                self.raw(eng, lambda e, s=s, v=v: e.wait_ge(s, v))
                self.covered[eng][c] = v

    def finalize(self):
        nc = self.nc
        with nc.Block() as block:
            @block.tensor
            def _(e):
                for f in self.ops["tensor"]:
                    f(e)

            @block.vector
            def _(e):
                for f in self.ops["vector"]:
                    f(e)

            @block.scalar
            def _(e):
                for f in self.ops["scalar"]:
                    f(e)

            @block.gpsimd
            def _(e):
                for f in self.ops["gpsimd"]:
                    f(e)

            @block.sync
            def _(e):
                for f in self.ops["sync"]:
                    f(e)
        for cm in reversed(self._ctx):
            cm.__exit__(None, None, None)
        self._ctx = []


class View:
    def __init__(self, ap, name):
        self._ap = ap
        self.name = name

    def __getitem__(self, idx):
        return self._ap[idx]



def mm(P, out_ap, lhsT_ap, rhs_ap, reads, wkey, start=True, stop=True, tag=None):
    P.op("tensor",
         lambda e: e.matmul(out_ap, lhsT_ap, rhs_ap, start=start, stop=stop,
                            skip_group_check=True),
         reads=reads, writes=[wkey], tag=tag)


def load_weight(P, C, dst, src_dram, nchunks, ncols, tag="w"):
    for k in range(nchunks):
        P.dma("gpsimd",
              lambda e, k=k: e.dma_start(out=dst[:, k, :],
                                         in_=src_dram[k * 128:(k + 1) * 128, :]),
              writes=[(dst, f"k{k}")], tag=f"{tag}{k}")


class Ctx:
    pass


def setup_common(nc, P, need):
    C = Ctx()
    C.nc, C.P = nc, P
    C.idn_d = nc.dram_tensor("c_idn", [128, 128], BF16, kind="ExternalInput").ap()
    C.idnf_d = nc.dram_tensor("c_idnf", [128, 128], F32, kind="ExternalInput").ap()
    C.onescol_d = nc.dram_tensor("c_ones", [128, 128], BF16, kind="ExternalInput").ap()
    C.onesf_d = nc.dram_tensor("c_onesf", [128, 1], F32, kind="ExternalInput").ap()
    C.idn = P.sbuf("idn", [128, 128], BF16)
    C.idnf = P.sbuf("idnf", [128, 128], F32)
    C.onescol = P.sbuf("onescol", [128, 128], BF16)
    C.onesf = P.sbuf("onesf", [128, 1], F32)
    P.dma("sync", lambda e: e.dma_start(out=C.idn[:], in_=C.idn_d[:]), writes=[C.idn])
    P.dma("sync", lambda e: e.dma_start(out=C.idnf[:], in_=C.idnf_d[:]), writes=[C.idnf])
    P.dma("sync", lambda e: e.dma_start(out=C.onescol[:], in_=C.onescol_d[:]), writes=[C.onescol])
    P.dma("sync", lambda e: e.dma_start(out=C.onesf[:], in_=C.onesf_d[:]), writes=[C.onesf])
    if "masks" in need:
        C.mlow_d = nc.dram_tensor("c_mlow", [128, 128], F32, kind="ExternalInput").ap()
        C.mtriu_d = nc.dram_tensor("c_mtriu", [128, 128], BF16, kind="ExternalInput").ap()
        C.mup_d = nc.dram_tensor("c_mup", [128, 128], BF16, kind="ExternalInput").ap()
        C.mlow = P.sbuf("mlow", [128, 128], F32)
        C.mtriu = P.sbuf("mtriu", [128, 128], BF16)
        C.mup = P.sbuf("mup", [128, 128], BF16)
        P.dma("sync", lambda e: e.dma_start(out=C.mlow[:], in_=C.mlow_d[:]), writes=[C.mlow])
        P.dma("sync", lambda e: e.dma_start(out=C.mtriu[:], in_=C.mtriu_d[:]), writes=[C.mtriu])
        P.dma("sync", lambda e: e.dma_start(out=C.mup[:], in_=C.mup_d[:]), writes=[C.mup])
    return C


def emit_ln(P, C, x_dram_b, xn_fm, ps_t, tc, tag):
    xt = C.ln_xt[tc % 2]
    P.dma("sync", lambda e: e.dma_start(out=xt[:], in_=x_dram_b[tc * 128:(tc + 1) * 128, :]),
          writes=[xt], tag=f"lnx{tag}")
    s1, nm, vv, t1, t2, rstd = (C.ln_small[:, i:i + 1] for i in range(6))
    sm = C.ln_small
    P.op("vector", lambda e: e.reduce_sum(s1, xt[:], axis=AX.X), reads=[xt], writes=[(sm, "s1")])
    P.op("vector", lambda e: e.tensor_scalar(nm, s1, -1.0 / 1024, None, ALU.mult),
         reads=[(sm, "s1")], writes=[(sm, "nm")])
    va, vb = C.ln_small[:, 6:7], C.ln_small[:, 7:8]
    pa, pb = C.ln_ps
    P.op("scalar", lambda e: e.activation(pa[:, :], xt[:, 0:512], AF.Square, bias=nm, scale=1.0,
                                          accum_out=va),
         reads=[xt, (sm, "nm")], writes=[(pa, "g"), (sm, "va")])
    P.op("scalar", lambda e: e.activation(pb[:, :], xt[:, 512:1024], AF.Square, bias=nm, scale=1.0,
                                          accum_out=vb),
         reads=[xt, (sm, "nm")], writes=[(pb, "g"), (sm, "vb")])
    P.op("vector", lambda e: e.tensor_tensor(vv, va, vb, ALU.add),
         reads=[(sm, "va"), (sm, "vb")], writes=[(sm, "vv")])
    P.op("vector", lambda e: e.tensor_scalar(t1, vv, 1.0 / 1024, 1e-5, ALU.mult, ALU.add),
         reads=[(sm, "vv")], writes=[(sm, "t1")])
    P.op("scalar", lambda e: e.activation(t2, t1, AF.Sqrt, bias=0.0, scale=1.0),
         reads=[(sm, "t1")], writes=[(sm, "t2")])
    P.op("vector", lambda e: e.reciprocal(rstd, t2), reads=[(sm, "t2")], writes=[(sm, "rstd")])
    P.op("vector", lambda e: e.tensor_scalar(C.ln_xn[:], xt[:], nm, rstd, ALU.add, ALU.mult),
         reads=[xt, (sm, "nm"), (sm, "rstd")], writes=[C.ln_xn])
    for c in range(8):
        pt = ps_t[c % 2]
        P.op("tensor", lambda e, c=c, pt=pt: e.transpose(pt[:, 0:128], C.ln_xn[:, c * 128:(c + 1) * 128], C.idn[:]),
             reads=[C.ln_xn, C.idn], writes=[(pt, "t")])
        P.op("vector", lambda e, c=c, pt=pt: e.tensor_copy(xn_fm[:, c, tc * 128:(tc + 1) * 128], pt[:, 0:128]),
             reads=[(pt, "t")], writes=[(xn_fm, f"c{c}_{tc}")])



def build_layer0(nc, P, C=None, x_kind="ExternalInput", out_kind="ExternalOutput", dbg=False):
    if C is None:
        C = setup_common(nc, P, need=())
    dt = nc.dram_tensor
    x_d = dt("x", [2, 2048, 1024], F32, kind=x_kind).ap()
    x1_d = dt("x1", [2, 2048, 1024], F32, kind=out_kind).ap()
    C.x1_d = x1_d
    s0_d = dt("s0scr", [2, 2048, 1024], BF16, kind="ExternalOutput" if dbg else "Internal").ap()
    if dbg:
        dbg_xn = dt("dbg_xn", [128, 8, 2048], BF16, kind="ExternalOutput").ap()
        dbg_scon = dt("dbg_scon", [128, 16, 8], F32, kind="ExternalOutput").ap()
        dbg_mask = dt("dbg_mask", [128, 16, 8], F32, kind="ExternalOutput").ap()
        dbg_g = dt("dbg_g", [128, 16384], BF16, kind="ExternalOutput").ap()
        dbg_qp = dt("dbg_qp", [128, 2048], BF16, kind="ExternalOutput").ap()
        dbg_kp = dt("dbg_kp", [128, 2048], BF16, kind="ExternalOutput").ap()
        dbg_pt = dt("dbg_pt", [128, 16, 512], BF16, kind="ExternalOutput").ap()
        dbg_ofm = dt("dbg_ofm", [128, 16384], BF16, kind="ExternalOutput").ap()
        dbg_wa = dt("dbg_wa", [128, 8, 1024], BF16, kind="ExternalOutput").ap()
        dbg_bo = dt("dbg_bo", [1, 1024], BF16, kind="ExternalOutput").ap()
        dbg_s0st = dt("dbg_s0st", [128, 1024], BF16, kind="ExternalOutput").ap()

    wq_d = dt("l0a_wq", [1024, 1024], F32, kind="ExternalInput").ap()
    wk_d = dt("l0a_wk", [1024, 1024], F32, kind="ExternalInput").ap()
    wv_d = dt("l0a_wv", [1024, 1024], F32, kind="ExternalInput").ap()
    wo_d = dt("l0a_wo", [1024, 1024], F32, kind="ExternalInput").ap()
    bq_d = dt("l0a_bq", [128, 8], F32, kind="ExternalInput").ap()
    bk_d = dt("l0a_bk", [128, 8], F32, kind="ExternalInput").ap()
    bv_d = dt("l0a_bv", [1, 1024], F32, kind="ExternalInput").ap()
    bo_d = dt("l0a_bo", [1, 1024], F32, kind="ExternalInput").ap()
    fwq_d = dt("l0f_wq", [1024, 1024], F32, kind="ExternalInput").ap()
    fwk_d = dt("l0f_wk", [1024, 1024], F32, kind="ExternalInput").ap()
    fbq_d = dt("l0f_bq", [128, 8], F32, kind="ExternalInput").ap()
    fbk_d = dt("l0f_bk", [128, 8], F32, kind="ExternalInput").ap()
    ewq_d = dt("l0f_ewq", [8, 128, 128], F32, kind="ExternalInput").ap()
    ewk_d = dt("l0f_ewk", [8, 128, 128], F32, kind="ExternalInput").ap()
    ebq_d = dt("l0f_ebq", [128, 8], F32, kind="ExternalInput").ap()
    ebk_d = dt("l0f_ebk", [128, 8], F32, kind="ExternalInput").ap()
    wexp_d = dt("l0f_wexp", [8, 1024, 1024], F32, kind="ExternalInput").ap()
    bexp_d = dt("l0f_bexp", [1, 8192], F32, kind="ExternalInput").ap()

    S = P.sbuf
    WA = S("WA", [128, 8, 1024], BF16)
    WB = S("WB", [128, 8, 1024], BF16)
    WC = S("WC", [128, 8, 1024], BF16)
    xn_fm = S("xn_fm", [128, 8, 2048], BF16)
    SCR1 = S("SCR1", [128, 16384], BF16)
    ofm = SCR1[:, :].rearrange("p (h t) -> p h t", h=8)
    g_acc = SCR1[:, :].rearrange("p (t f) -> p t f", t=16)
    SCR2 = S("SCR2", [128, 16384], BF16)
    qf_fm = SCR2[:, :].rearrange("p (h t) -> p h t", h=8)
    PT = SCR2[:, 0:8192].rearrange("p (j i) -> p j i", j=16)
    SCR3 = S("SCR3", [128, 16384], BF16)
    kf_fm = SCR3[:, :].rearrange("p (h t) -> p h t", h=8)
    qp = SCR3[:, 0:2048]
    kp = SCR3[:, 2048:4096]
    Vp = SCR3[:, 4096:6176].rearrange("p (t h f) -> p t h f", t=16, h=2)
    obt = SCR2[0:64, 8192:10240]
    dnr = SCR2[64:65, 10240:10752]
    dens = S("dens", [128, 512], F32)
    denb = S("denb", [64, 512], BF16)
    s0st = S("s0st", [128, 1024], BF16)
    bias8a = S("bias8a", [128, 8], F32)
    bias8b = S("bias8b", [128, 8], F32)
    bvrow = S("bvrow", [1, 1024], BF16)
    borow = bvrow
    fb_q = S("fb_q", [128, 8], F32)
    fb_k = S("fb_k", [128, 8], F32)
    ewq_s = S("ewq_s", [128, 8, 128], BF16)
    ewk_s = S("ewk_s", [128, 8, 128], BF16)
    ebq_s = S("ebq_s", [128, 8], F32)
    ebk_s = S("ebk_s", [128, 8], F32)
    qe = SCR1[:, 0:2048]
    ke = SCR1[:, 2048:4096]
    Pe = [SCR1[:, 4096:6144], SCR1[:, 6144:8192]]
    accw = S("accw", [128, 2048], F32)
    denp = S("denp", [128, 4], F32)
    dd1 = S("dd1", [128, 2], F32)
    scon = S("scon", [128, 16, 8], F32)
    maskt = S("maskt", [128, 16, 8], F32)
    mxt = S("mxt", [128, 1], F32)
    bexps = S("bexps", [1, 1024], BF16)
    _lnxt = S("ln_xt0", [128, 1024], F32)
    C.ln_xt = [_lnxt, _lnxt]
    C.ln_xn = S("ln_xn", [128, 1024], BF16)
    C.ln_small = S("ln_small", [128, 8], F32)
    C.ln_ps = None
    gs = C.ln_xn
    s0t = s0st
    xt2 = C.ln_xt[0]
    prt = accw[:, 0:1024]
    x1t = accw[:, 1024:2048]

    psAB = P.psum("psAB", [128, 1024])
    psCD = P.psum("psCD", [128, 1024])
    ps0 = View(psAB[:, 0:512], "v_ps0")
    ps1 = View(psAB[:, 512:1024], "v_ps1")
    ps2 = View(psCD[:, 0:512], "v_ps2")
    ps3 = View(psCD[:, 512:1024], "v_ps3")
    ps4 = P.psum("ps4", [128, 512])
    ps5 = P.psum("ps5", [128, 512])
    psT = [P.psum("psTa", [128, 1024], BF16), P.psum("psTb", [128, 1024], BF16)]
    C.ln_ps = (ps4, ps5)

    for t, d in ((bias8a, bq_d), (bias8b, bk_d), (fb_q, fbq_d), (fb_k, fbk_d),
                 (ebq_s, ebq_d), (ebk_s, ebk_d)):
        P.dma("sync", lambda e, t=t, d=d: e.dma_start(out=t[:], in_=d[:]), writes=[t])
    P.dma("gpsimd", lambda e: e.dma_start(out=bvrow[:], in_=bv_d[:]), writes=[bvrow])
    for e_ in range(8):
        P.dma("gpsimd", lambda e, e_=e_: e.dma_start(out=ewq_s[:, e_, :], in_=ewq_d[e_]),
              writes=[(ewq_s, f"e{e_}")])
        P.dma("gpsimd", lambda e, e_=e_: e.dma_start(out=ewk_s[:, e_, :], in_=ewk_d[e_]),
              writes=[(ewk_s, f"e{e_}")])
    SC = 1.0 / np.sqrt(128.0)
    P.op("vector", lambda e: e.tensor_scalar(ebq_s[:], ebq_s[:], SC, None, ALU.mult),
         reads=[ebq_s], writes=[ebq_s])

    for b in range(2):
        xb = x_d[b]
        load_weight(P, C, WA, wq_d, 8, 1024, tag=f"wq{b}")
        load_weight(P, C, WB, wk_d, 8, 1024, tag=f"wk{b}")
        load_weight(P, C, WC, wv_d, 8, 1024, tag=f"wv{b}")
        for tc in range(16):
            emit_ln(P, C, xb, xn_fm, psT, tc, tag=f"{b}_{tc}")
        if dbg and b == 0:
            P.dma("sync", lambda e: e.dma_start(out=dbg_xn[:], in_=xn_fm[:, :, :]),
                  reads=[xn_fm])
        P.op("gpsimd", lambda e: e.memset(SCR1[:, :], 0.0), writes=[(SCR1, None)])
        P.op("gpsimd", lambda e: e.memset(SCR2[:, :], 0.0), writes=[(SCR2, None)])
        P.op("gpsimd", lambda e: e.memset(SCR3[:, :], 0.0), writes=[(SCR3, None)])
        P.op("vector", lambda e: e.memset(Vp[:, :, :, 64:65], 1.0), writes=[(SCR3, "ones")])
        P.dma("gpsimd", lambda e: e.dma_start(out=bvrow[:], in_=bv_d[:]), writes=[bvrow], tag=f"bv{b}")

        for hp in range(8):
            for tt in range(4):
                ps = (ps0, ps1)[tt % 2]
                for k in range(8):
                    mm(P, ps[:, :], WA[:, k, hp * 128:(hp + 1) * 128],
                       xn_fm[:, k, tt * 512:(tt + 1) * 512],
                       reads=[(WA, f"k{k}"), xn_fm], wkey=(ps, "g"),
                       start=(k == 0), stop=(k == 7))
                P.op("vector", lambda e, ps=ps, tt=tt, hp=hp: e.tensor_scalar(
                    qp[:, tt * 512:(tt + 1) * 512], ps[:, :], bias8a[:, hp:hp + 1], 1.0 / 32,
                    ALU.add, ALU.mult),
                    reads=[(ps, "g"), bias8a], writes=[(SCR3, f"qt{tt}")])
                ps_ = (ps2, ps3)[tt % 2]
                for k in range(8):
                    mm(P, ps_[:, :], WB[:, k, hp * 128:(hp + 1) * 128],
                       xn_fm[:, k, tt * 512:(tt + 1) * 512],
                       reads=[(WB, f"k{k}"), xn_fm], wkey=(ps_, "g"),
                       start=(k == 0), stop=(k == 7))
                P.op("vector", lambda e, ps_=ps_, tt=tt, hp=hp: e.tensor_scalar(
                    kp[:, tt * 512:(tt + 1) * 512], ps_[:, :], bias8b[:, hp:hp + 1], None,
                    ALU.add),
                    reads=[(ps_, "g"), bias8b], writes=[(SCR3, f"kt{tt}")])
            for tc2 in range(16):
                ps = (ps0, ps1, ps2, ps3)[tc2 % 4]
                for k in range(8):
                    mm(P, ps[:, 0:128], xn_fm[:, k, tc2 * 128:(tc2 + 1) * 128],
                       WC[:, k, hp * 128:(hp + 1) * 128],
                       reads=[xn_fm, (WC, f"k{k}")], wkey=(ps, "g"),
                       start=(k == 0), stop=False)
                mm(P, ps[:, 0:128], C.onescol[0:1, 0:128],
                   bvrow[0:1, hp * 128:(hp + 1) * 128],
                   reads=[C.onescol, bvrow], wkey=(ps, "g"), start=False, stop=True)
                P.op("vector", lambda e, ps=ps, tc2=tc2: e.tensor_copy(
                    Vp[:, tc2, :, 0:64],
                    ps[:, 0:128].rearrange("p (h f) -> p h f", h=2)),
                    reads=[(ps, "g")], writes=[(SCR3, f"v{tc2}")])
            if dbg and b == 0 and hp == 0:
                P.dma("sync", lambda e: e.dma_start(out=dbg_qp[:], in_=qp[:, :]),
                      reads=[(SCR3, "qt0"), (SCR3, "qt1"), (SCR3, "qt2"), (SCR3, "qt3")])
                P.dma("sync", lambda e: e.dma_start(out=dbg_kp[:], in_=kp[:, :]),
                      reads=[(SCR3, "kt0"), (SCR3, "kt1"), (SCR3, "kt2"), (SCR3, "kt3")])
            for h01 in range(2):
                base = 64 * h01
                for it in range(4):
                    psO = ps4
                    for jc in range(16):
                        ps = (ps0, ps1, ps2, ps3)[jc % 4]
                        mm(P, ps[:, :], kp[base:base + 64, jc * 128:(jc + 1) * 128],
                           qp[base:base + 64, it * 512:(it + 1) * 512],
                           reads=[(SCR3, f"kt{jc // 4}"), (SCR3, f"qt{it}")], wkey=(ps, "g"))
                        if jc % 2 == 1:
                            pr = (psAB, psCD)[(jc % 4) // 2]
                            k0, k1 = ((ps0, "g"), (ps1, "g")) if (jc % 4) == 1 else ((ps2, "g"), (ps3, "g"))
                            P.op("scalar", lambda e, pr=pr, jc=jc: e.activation(
                                SCR2[:, (jc - 1) * 512:(jc + 1) * 512], pr[:, :],
                                AF.Exp, bias=0.0, scale=1.0),
                                reads=[k0, k1],
                                writes=[(SCR2, f"j{jc - 1}"), (SCR2, f"j{jc}")])
                    if dbg and b == 0 and hp == 0 and h01 == 0 and it == 0:
                        P.dma("sync", lambda e: e.dma_start(out=dbg_pt[:], in_=PT[:, :, :]),
                              reads=[(SCR2, f"j{j}") for j in range(16)])
                    for jc in range(16):
                        mm(P, psO[0:65, :], Vp[:, jc, h01, 0:65], PT[:, jc, :],
                           reads=[(SCR3, f"v{jc}"), (SCR3, "ones"), (SCR2, f"j{jc}")],
                           wkey=(psO, "g"), start=(jc == 0), stop=(jc == 15))
                    P.op("vector", lambda e, psO=psO: e.tensor_copy(dens[64:65, :], psO[64:65, :]),
                         reads=[(psO, "g")], writes=[(dens, "d")])
                    P.op("vector", lambda e: e.reciprocal(dens[64:65, :], dens[64:65, :]),
                         reads=[(dens, "d")], writes=[(dens, "r")])
                    P.op("vector", lambda e: e.tensor_copy(dnr, dens[64:65, :]),
                         reads=[(dens, "r")], writes=[(SCR2, "dnr")])
                    pb = ps5
                    mm(P, pb[0:64, :], C.onescol[64:65, 0:64], dnr,
                       reads=[C.onescol, (SCR2, "dnr")], wkey=(pb, "g"))
                    P.op("vector", lambda e, pb=pb: e.tensor_copy(denb[:, :], pb[0:64, :]),
                         reads=[(pb, "g")], writes=[denb])
                    if h01 == 0:
                        dest = ofm[0:64, hp, it * 512:(it + 1) * 512]
                        wk = (SCR1, f"oa{hp}_{it}")
                    else:
                        dest = obt[0:64, it * 512:(it + 1) * 512]
                        wk = (SCR2, f"obt{it}")
                    P.op("vector", lambda e, dest=dest, psO=psO: e.tensor_tensor(
                        dest, psO[0:64, :], denb[:, :], ALU.mult),
                        reads=[(psO, "g"), denb], writes=[wk])
                if h01 == 1:
                    P.dma("sync", lambda e, hp=hp: e.dma_start(
                        out=ofm[64:128, hp, :], in_=obt[:, :]),
                        reads=[(SCR2, "obt0"), (SCR2, "obt1"), (SCR2, "obt2"), (SCR2, "obt3")],
                        writes=[(SCR1, f"ob{hp}")], tag=f"obmv{hp}")
        if dbg and b == 0:
            P.dma("sync", lambda e: e.dma_start(out=dbg_ofm[:], in_=SCR1[:, :]),
                  reads=[(SCR1, None)])
        load_weight(P, C, WA, wo_d, 8, 1024, tag=f"wo{b}")
        P.dma("gpsimd", lambda e: e.dma_start(out=borow[:], in_=bo_d[:]), writes=[bvrow], tag=f"bo{b}")
        if dbg and b == 0:
            P.dma("sync", lambda e: e.dma_start(out=dbg_wa[:], in_=WA[:, :, :]),
                  reads=[(WA, f"k{k}") for k in range(8)])
            P.dma("sync", lambda e: e.dma_start(out=dbg_bo[:], in_=borow[:, :]), reads=[borow])
        for tc in range(16):
            for ft in range(2):
                ps = (ps0, ps1)[ft]
                for pr in range(8):
                    mm(P, ps[:, :], ofm[:, pr, tc * 128:(tc + 1) * 128],
                       WA[:, pr, ft * 512:(ft + 1) * 512],
                       reads=[(SCR1, f"oa{pr}_{tc // 4}"), (SCR1, f"ob{pr}"), (WA, f"k{pr}")],
                       wkey=(ps, "g"), start=(pr == 0), stop=False)
                mm(P, ps[:, :], C.onescol[0:1, 0:128], borow[0:1, ft * 512:(ft + 1) * 512],
                   reads=[C.onescol, borow], wkey=(ps, "g"), start=False, stop=True)
                P.op("vector", lambda e, ps=ps, ft=ft: e.tensor_copy(
                    s0st[:, ft * 512:(ft + 1) * 512], ps[:, :]),
                    reads=[(ps, "g")], writes=[(s0st, f"f{ft}")])
            P.dma("sync", lambda e, tc=tc, b=b: e.dma_start(
                out=s0_d[b, tc * 128:(tc + 1) * 128, :], in_=s0st[:, :]),
                reads=[s0st], writes=[(s0_d, f"{b}_{tc}")], tag=f"s0s{tc}")
            if dbg and b == 0 and tc == 0:
                P.dma("sync", lambda e: e.dma_start(out=dbg_s0st[:], in_=s0st[:, :]),
                      reads=[s0st])

        load_weight(P, C, WA, fwq_d, 8, 1024, tag=f"fwq{b}")
        load_weight(P, C, WB, fwk_d, 8, 1024, tag=f"fwk{b}")
        P.op("gpsimd", lambda e: e.memset(SCR1[:, :], 0.0), writes=[(SCR1, None)])
        P.op("gpsimd", lambda e: e.memset(SCR2[:, :], 0.0), writes=[(SCR2, None)])
        P.op("gpsimd", lambda e: e.memset(SCR3[:, :], 0.0), writes=[(SCR3, None)])
        for mc in range(8):
            for tt in range(4):
                ps = (ps0, ps1)[tt % 2]
                for k in range(8):
                    mm(P, ps[:, :], WA[:, k, mc * 128:(mc + 1) * 128],
                       xn_fm[:, k, tt * 512:(tt + 1) * 512],
                       reads=[(WA, f"k{k}"), xn_fm], wkey=(ps, "g"),
                       start=(k == 0), stop=(k == 7))
                P.op("vector", lambda e, ps=ps, mc=mc, tt=tt: e.tensor_scalar(
                    qf_fm[:, mc, tt * 512:(tt + 1) * 512], ps[:, :],
                    fb_q[:, mc:mc + 1], SC, ALU.add, ALU.mult),
                    reads=[(ps, "g"), fb_q], writes=[(SCR2, f"m{mc}")])
                ps_ = (ps2, ps3)[tt % 2]
                for k in range(8):
                    mm(P, ps_[:, :], WB[:, k, mc * 128:(mc + 1) * 128],
                       xn_fm[:, k, tt * 512:(tt + 1) * 512],
                       reads=[(WB, f"k{k}"), xn_fm], wkey=(ps_, "g"),
                       start=(k == 0), stop=(k == 7))
                P.op("vector", lambda e, ps_=ps_, mc=mc, tt=tt: e.tensor_scalar(
                    kf_fm[:, mc, tt * 512:(tt + 1) * 512], ps_[:, :],
                    fb_k[:, mc:mc + 1], None, ALU.add),
                    reads=[(ps_, "g"), fb_k], writes=[(SCR3, f"m{mc}")])
        for e_ in range(8):
            for h in range(8):
                for tt in range(4):
                    ps = (ps0, ps1)[tt % 2]
                    mm(P, ps[:, :], ewq_s[:, e_, :], qf_fm[:, h, tt * 512:(tt + 1) * 512],
                       reads=[(ewq_s, f"e{e_}"), (SCR2, f"m{h}")], wkey=(ps, "g"))
                    P.op("vector", lambda e, ps=ps, tt=tt, e_=e_: e.tensor_scalar(
                        qe[:, tt * 512:(tt + 1) * 512], ps[:, :], ebq_s[:, e_:e_ + 1],
                        None, ALU.add),
                        reads=[(ps, "g"), ebq_s], writes=[(SCR1, f"qet{tt}")])
                    ps_ = (ps2, ps3)[tt % 2]
                    mm(P, ps_[:, :], ewk_s[:, e_, :], kf_fm[:, h, tt * 512:(tt + 1) * 512],
                       reads=[(ewk_s, f"e{e_}"), (SCR3, f"m{h}")], wkey=(ps_, "g"))
                    P.op("vector", lambda e, ps_=ps_, tt=tt, e_=e_: e.tensor_scalar(
                        ke[:, tt * 512:(tt + 1) * 512], ps_[:, :], ebk_s[:, e_:e_ + 1],
                        None, ALU.add),
                        reads=[(ps_, "g"), ebk_s], writes=[(SCR1, f"ket{tt}")])
                for sc in range(16):
                    pe = Pe[sc % 2]
                    for tt2 in range(4):
                        ps = (ps0, ps1, ps2, ps3)[tt2]
                        mm(P, ps[:, :], qe[:, sc * 128:(sc + 1) * 128],
                           ke[:, tt2 * 512:(tt2 + 1) * 512],
                           reads=[(SCR1, f"qet{sc // 4}"), (SCR1, f"ket{tt2}")], wkey=(ps, "g"))
                        if tt2 % 2 == 1:
                            pr = (psAB, psCD)[tt2 // 2]
                            k0, k1 = ((ps0, "g"), (ps1, "g")) if tt2 == 1 else ((ps2, "g"), (ps3, "g"))
                            P.op("scalar", lambda e, pr=pr, pe=pe, tt2=tt2: e.activation(
                                pe[:, (tt2 - 1) * 512:(tt2 + 1) * 512], pr[:, :], AF.Exp,
                                bias=0.0, scale=1.0, accum_out=denp[:, tt2 // 2:tt2 // 2 + 1]),
                                reads=[k0, k1],
                                writes=[(SCR1, f"pe{sc % 2}"), (denp, f"t{tt2 // 2}")])
                    P.op("vector", lambda e: e.tensor_reduce(dd1[:, 0:1], denp[:, 0:2], AX.X, ALU.add),
                         reads=[(denp, None)], writes=[(dd1, "s")])
                    P.op("vector", lambda e: e.reciprocal(dd1[:, 1:2], dd1[:, 0:1]),
                         reads=[(dd1, "s")], writes=[(dd1, "r")])
                    first = (h == 0 and sc == 0)
                    if first:
                        P.op("vector", lambda e, pe=pe: e.tensor_scalar(
                            accw[:, :], pe[:, :], dd1[:, 1:2], None, ALU.mult),
                            reads=[(SCR1, f"pe{sc % 2}"), (dd1, "r")], writes=[(accw, None)])
                    else:
                        P.op("vector", lambda e, pe=pe: e.scalar_tensor_tensor(
                            accw[:, :], pe[:, :], dd1[:, 1:2], accw[:, :],
                            ALU.mult, ALU.add),
                            reads=[(SCR1, f"pe{sc % 2}"), (dd1, "r"), (accw, None)],
                            writes=[(accw, None)])
            for tc in range(16):
                P.op("tensor", lambda e, tc=tc: e.matmul(
                    ps5[:, 0:1], accw[:, tc * 128:(tc + 1) * 128], C.onesf[:, 0:1],
                    start=True, stop=True, skip_group_check=True),
                    reads=[(accw, None), C.onesf], writes=[(ps5, "g")])
                P.op("vector", lambda e, tc=tc, e_=e_: e.tensor_scalar(
                    scon[:, tc, e_:e_ + 1], ps5[:, 0:1], 1.0 / 16384.0, None, ALU.mult),
                    reads=[(ps5, "g")], writes=[(scon, f"t{tc}e{e_}")])
        for tc in range(16):
            P.op("vector", lambda e, tc=tc: e.tensor_reduce(
                mxt[:, 0:1], scon[:, tc, :], AX.X, ALU.max),
                reads=[(scon, None)], writes=[mxt])
            P.op("vector", lambda e, tc=tc: e.tensor_scalar(
                maskt[:, tc, :], scon[:, tc, :], mxt[:, 0:1], None, ALU.is_equal),
                reads=[(scon, None), mxt], writes=[(maskt, f"t{tc}")])
        if dbg and b == 0:
            P.dma("sync", lambda e: e.dma_start(out=dbg_scon[:], in_=scon[:, :, :]), reads=[(scon, None)])
            P.dma("sync", lambda e: e.dma_start(out=dbg_mask[:], in_=maskt[:, :, :]), reads=[(maskt, None)])
        P.op("gpsimd", lambda e: e.memset(SCR1[:, :], 0.0), writes=[(SCR1, None)])
        for e_ in range(8):
            W = (WA, WB)[e_ % 2]
            load_weight(P, C, W, wexp_d[e_], 8, 1024, tag=f"we{e_}")
            P.dma("gpsimd", lambda e, e_=e_: e.dma_start(
                out=bexps[:, :], in_=bexp_d[0:1, e_ * 1024:(e_ + 1) * 1024]),
                writes=[bexps])
            for tc in range(16):
                for ft in range(2):
                    ps = (ps0, ps1, ps2, ps3)[(tc * 2 + ft) % 4]
                    for k in range(8):
                        mm(P, ps[:, :], xn_fm[:, k, tc * 128:(tc + 1) * 128],
                           W[:, k, ft * 512:(ft + 1) * 512],
                           reads=[xn_fm, (W, f"k{k}")], wkey=(ps, "g"),
                           start=(k == 0), stop=False)
                    mm(P, ps[:, :], C.onescol[0:1, 0:128],
                       bexps[0:1, ft * 512:(ft + 1) * 512],
                       reads=[C.onescol, bexps], wkey=(ps, "g"), start=False, stop=True)
                    P.op("vector", lambda e, ps=ps, tc=tc, ft=ft, e_=e_: e.scalar_tensor_tensor(
                        g_acc[:, tc, ft * 512:(ft + 1) * 512], ps[:, :],
                        maskt[:, tc, e_:e_ + 1], g_acc[:, tc, ft * 512:(ft + 1) * 512],
                        ALU.mult, ALU.add),
                        reads=[(ps, "g"), (maskt, f"t{tc}"), (SCR1, f"g{tc}_{ft}")],
                        writes=[(SCR1, f"g{tc}_{ft}")])
        if dbg and b == 0:
            P.dma("sync", lambda e: e.dma_start(out=dbg_g[:], in_=SCR1[:, :]),
                  reads=[(SCR1, None)])
        for tc in range(16):
            P.op("scalar", lambda e, tc=tc: e.activation(
                gs[:, :], g_acc[:, tc, :], AF.Silu, bias=0.0, scale=1.0),
                reads=[(SCR1, f"g{tc}_0"), (SCR1, f"g{tc}_1")], writes=[gs])
            P.dma("sync", lambda e, tc=tc, b=b: e.dma_start(
                out=s0t[:, :], in_=s0_d[b, tc * 128:(tc + 1) * 128, :]),
                reads=[(s0_d, f"{b}_{tc}")], writes=[s0t], tag=f"s0l{tc}")
            P.dma("scalar", lambda e, tc=tc, xb=xb: e.dma_start(
                out=xt2[:, :], in_=xb[tc * 128:(tc + 1) * 128, :]),
                writes=[xt2], tag=f"xl{tc}")
            P.op("vector", lambda e: e.tensor_tensor(prt, s0t[:, :], gs[:, :], ALU.mult),
                 reads=[s0t, gs], writes=[(accw, "prt")])
            P.op("vector", lambda e: e.tensor_tensor(x1t, xt2[:, :], prt, ALU.add),
                 reads=[xt2, (accw, "prt")], writes=[(accw, "x1t")])
            P.dma("sync", lambda e, tc=tc, b=b: e.dma_start(
                out=x1_d[b, tc * 128:(tc + 1) * 128, :], in_=x1t),
                reads=[(accw, "x1t")], tag=f"x1s{tc}")
    P.drain_dmas("sync")
    P.drain_dmas("gpsimd")
    P.drain_dmas("scalar")
    return C




def build_layer1(nc, P, C=None, x_kind="ExternalInput", out_kind="ExternalOutput", dbg=False):
    if C is None or not hasattr(C, "mlow"):
        C = setup_common(nc, P, need=("masks",)) if C is None else C
        if not hasattr(C, "mlow"):
            raise RuntimeError("fused build needs masks in C")
    dt = nc.dram_tensor
    x1_d = getattr(C, "x1_d", None) if C is not None else None
    if x1_d is None:
        x1_d = dt("x1", [2, 2048, 1024], F32, kind=x_kind).ap()
    out_d = dt("out", [2, 2048, 1024], F32, kind=out_kind).ap()
    q1scr = dt("q1scr", [2, 8, 128, 2048], BF16, kind="Internal").ap()
    k1scr = dt("k1scr", [2, 8, 128, 2048], BF16, kind="Internal").ap()
    fsscr = dt("fsscr", [2, 8, 128, 2048], BF16, kind="ExternalOutput" if dbg else "Internal").ap()
    flscr = dt("flscr", [2, 8, 128, 2048], BF16, kind="ExternalOutput" if dbg else "Internal").ap()
    dlscr = dt("dlscr", [2, 8, 128, 2048], BF16, kind="ExternalOutput" if dbg else "Internal").ap()
    s1scr = dt("s1scr", [2, 2048, 1024], BF16, kind="ExternalOutput" if dbg else "Internal").ap()
    rsd_dram = dt("rsd_dram", [16, 128], F32, kind="Internal").ap()
    if dbg:
        dbg_w = dt("dbg_w", [128, 16, 16], F32, kind="ExternalOutput").ap()
        dbg_comb = dt("dbg_comb", [128, 16, 8], F32, kind="ExternalOutput").ap()
        dbg_stats = dt("dbg_stats", [128, 16, 32], F32, kind="ExternalOutput").ap()
        dbg_lg = dt("dbg_lg", [128, 16, 16], F32, kind="ExternalOutput").ap()
        dbg_h1 = dt("dbg_h1", [128, 512], BF16, kind="ExternalOutput").ap()
        dbg_s32 = dt("dbg_s32", [128, 2048], BF16, kind="ExternalOutput").ap()
        dbg_pre = dt("dbg_pre", [128, 512], F32, kind="ExternalOutput").ap()
        dbg_xn1 = dt("dbg_xn1", [128, 8, 2048], BF16, kind="ExternalOutput").ap()
        dbg_w1 = dt("dbg_w1", [128, 2048], BF16, kind="ExternalOutput").ap()
        dbg_on = dt("dbg_on", [128, 16384], BF16, kind="ExternalOutput").ap()
        dbg_wb = dt("dbg_wb", [128, 512], F32, kind="ExternalOutput").ap()
        dbg_rsd = dt("dbg_rsd", [1, 2048], BF16, kind="ExternalOutput").ap()
        dbg_wfm = dt("dbg_wfm", [16, 2048], BF16, kind="ExternalOutput").ap()
        dbg_sel = dt("dbg_sel", [16, 2048], BF16, kind="ExternalOutput").ap()
        dbg_lgf = dt("dbg_lgf", [16, 2048], BF16, kind="ExternalOutput").ap()
        dbg_rtb = dt("dbg_rtb", [128, 16], F32, kind="ExternalOutput").ap()

    wq_d = dt("l1a_wq", [1024, 1024], F32, kind="ExternalInput").ap()
    wk_d = dt("l1a_wk", [1024, 1024], F32, kind="ExternalInput").ap()
    wv_d = dt("l1a_wv", [1024, 1024], F32, kind="ExternalInput").ap()
    wb_d = dt("l1a_wb", [128, 32], F32, kind="ExternalInput").ap()
    cq_d = dt("l1a_cq", [128, 32], F32, kind="ExternalInput").ap()
    ck_d = dt("l1a_ck", [128, 32], F32, kind="ExternalInput").ap()
    cv_d = dt("l1a_cv", [128, 32], F32, kind="ExternalInput").ap()
    fs_d = dt("l1a_fs", [128, 24], F32, kind="ExternalInput").ap()
    fl_d = dt("l1a_fl", [128, 504], F32, kind="ExternalInput").ap()
    w1_d = dt("l1a_w1", [1024, 2048], F32, kind="ExternalInput").ap()
    w1c9_d = dt("l1a_w1c9", [128, 2048], F32, kind="ExternalInput").ap()
    b1c_d = dt("l1a_b1", [128, 16], F32, kind="ExternalInput").ap()
    w2_d = dt("l1a_w2", [128, 256], F32, kind="ExternalInput").ap()
    b2r_d = dt("l1a_b2", [16, 1], F32, kind="ExternalInput").ap()
    glt_d = dt("l1a_glt", [1, 4], F32, kind="ExternalInput").ap()
    onw_d = dt("l1a_onw", [128, 2], F32, kind="ExternalInput").ap()
    wo_d = dt("l1a_wo", [1024, 1024], F32, kind="ExternalInput").ap()
    sel_d = dt("c_sel", [16, 2048], BF16, kind="ExternalInput").ap()
    wg_d = dt("l1f_wg", [128, 64], F32, kind="ExternalInput").ap()
    bg_d = dt("l1f_bg", [8, 1], F32, kind="ExternalInput").ap()
    fw1_d = dt("l1f_w1", [8, 1024, 2048], F32, kind="ExternalInput").ap()
    fb1_d = dt("l1f_b1", [8, 128, 16], F32, kind="ExternalInput").ap()
    fw2_d = dt("l1f_w2", [8, 2048, 1024], F32, kind="ExternalInput").ap()
    fb2_d = dt("l1f_b2", [1, 8192], F32, kind="ExternalInput").ap()

    S = P.sbuf
    xn1 = S("xn1", [128, 8, 2048], BF16)
    SCRA = S("L1A", [128, 16896], BF16)
    ARW = S("ARW", [128, 16384], BF16)
    ARX = S("ARX", [128, 16384], BF16)
    ARS = S("ARS", [128, 8192], BF16)
    CV = S("CVf", [128, 2051], BF16)
    CA = S("CAf", [128, 2048], BF16)
    OUTF = S("OUTF", [128, 1024], F32)
    qkst = S("qkst", [128, 2048], BF16)
    beta_fm = S("beta_fm", [4, 2048], BF16)
    beta_tm = S("beta_tm", [128, 16, 4], F32)
    stats_tm = S("stats_tm", [128, 16, 32], F32)
    sq_st = S("sq_st", [128, 2048], BF16)
    stats32 = S("stats32", [128, 2048], BF16)
    logits_fm = S("logits_fm", [16, 2048], BF16)
    lg_tm = S("lg_tm", [128, 16, 16], F32)
    w_tm = S("w_tm", [128, 16, 16], F32)
    wfm16 = S("wfm16", [16, 2048], BF16)
    sel_sb = S("sel_sb", [16, 2048], BF16)
    wbst = S("wbst", [128, 512], BF16)
    diag_a = S("diag_a", [128, 128], BF16)
    diag_b = S("diag_b", [128, 128], BF16)
    smallA = S("smallA", [128, 16], F32)
    smallB = S("smallB", [128, 16], F32)
    nrm16 = S("nrm16", [128, 16], F32)
    rsd16 = S("rsd16", [16, 128], F32)
    glt_sb = S("glt_sb", [1, 8], F32)
    rt16 = S("rt16", [1, 24], F32)
    rt16b = S("rt16b", [1, 16], BF16)
    rtb = S("rtb", [128, 16], F32)
    b1c = S("b1c", [128, 16], F32)
    w2_sb = S("w2_sb", [128, 16, 16], BF16)
    b2r = S("b2r", [16, 1], F32)
    onw = S("onw", [128, 2], F32)
    wb_sb = S("wb_sb", [128, 8, 4], BF16)
    cw_sb = S("cw_sb", [128, 8, 12], F32)
    fsw = S("fsw", [128, 8, 3], F32)
    flw = S("flw", [128, 8, 63], F32)
    wg_sb = S("wg_sb", [128, 8, 8], BF16)
    bgc = S("bgc", [8, 1], F32)
    fb1c = S("fb1c", [128, 16], F32)
    fb2r = S("fb2r", [1, 1024], BF16)
    w1c9 = S("w1c9", [128, 2048], BF16)
    comb_tm = S("comb_tm", [128, 16, 8], F32)
    msk1 = S("msk1", [128, 8], F32)
    msk2 = S("msk2", [128, 8], F32)
    lmsk = S("lmsk", [128, 8], F32)
    tmp8 = S("tmp8", [128, 8], F32)
    lnxt = S("l1_lnxt", [128, 1024], F32)
    C.ln_xt = [lnxt, lnxt]
    C.ln_xn = CA[:, 0:1024]
    C.ln_small = S("l1_lnsm", [128, 8], F32)

    rsd_row = beta_fm[0:1, 0:2048]
    def v1s(mc):
        return SCRA[:, mc * 2112:(mc + 1) * 2112]
    g1acc = SCRA[:, 0:16384].rearrange("p (t f) -> p t f", t=16)
    q1h = ARS[:, 0:4096].rearrange("p (c t) -> p c t", c=2)
    k1h = ARS[:, 4096:8192].rearrange("p (c t) -> p c t", c=2)
    dfm = [ARS[:, 0:2048], ARS[:, 2048:4096]]
    h1f = ARS[:, 0:8192].rearrange("p (c t) -> p c t", c=16)
    kn_tm = ARW[:, 0:4096].rearrange("p (c d) -> p c d", c=16)
    kb_tm = ARW[:, 4096:8192].rearrange("p (c d) -> p c d", c=16)
    vb_tm = ARW[:, 8192:12288].rearrange("p (c d) -> p c d", c=16)
    TMPa = ARW[:, 12288:16384].rearrange("p (c d) -> p c d", c=16)
    Ssb = ARW[:, 12288:12800]
    u_sb = ARW[:, 12800:13056]
    u2_sb = ARW[:, 13056:13312]
    wfm_c = ARW[:, 13312:13568]
    Wv_ = [ARW[:, 0:8192].rearrange("p (c f) -> p c f", c=8),
           ARW[:, 8192:16384].rearrange("p (c f) -> p c f", c=8)]
    w1v = ARW[:, 0:16384].rearrange("p (c f) -> p c f", c=8)
    brs = ARW[:, 0:4096].rearrange("p (c t) -> p c t", c=2)
    brl = ARW[:, 4096:8192].rearrange("p (c t) -> p c t", c=2)
    brd = ARW[:, 8192:12288].rearrange("p (c t) -> p c t", c=2)
    brstg = ARW[:, 0:2048]
    wov = ARW[:, 0:8192].rearrange("p (c f) -> p c f", c=8)
    w1e = ARW[:, 0:16384].rearrange("p (c f) -> p c f", c=8)
    qn_fm = ARX[:, 0:4096].rearrange("p (c t) -> p c t", c=2)
    kn_fm = ARX[:, 4096:8192].rearrange("p (c t) -> p c t", c=2)
    kb_fm = ARX[:, 8192:12288].rearrange("p (c t) -> p c t", c=2)
    attnT = ARX[:, 12288:14336].rearrange("p (c t) -> p c t", c=16)
    CHA = ARX[:, 14336:16384]
    def chs(i):
        return CHA[:, i * 128:(i + 1) * 128]
    o_n = ARX[:, 0:16384].rearrange("p (c t) -> p c t", c=8)
    w2e = ARX[:, 0:16384].rearrange("p (c f) -> p c f", c=16)

    ps0 = P.psum("l1ps0", [128, 512])
    ps1 = P.psum("l1ps1", [128, 512])
    ps2 = P.psum("l1ps2", [128, 512])
    ps3 = P.psum("l1ps3", [128, 512])
    ps4 = P.psum("l1ps4", [128, 512])
    ps5 = P.psum("l1ps5", [128, 512])
    psS = P.psum("l1psS", [128, 512])
    psTt = P.psum("l1psT", [128, 1024], BF16)
    psT = [psTt, psTt]
    C.ln_ps = (ps4, ps5)

    def retag(tile, keys, tag="rt"):
        ap1 = tile[0:1, 0:1]
        P.op("vector", lambda e: e.memset(ap1, 0.0),
             writes=[(tile, k) for k in keys], tag=tag, strict=True)

    TMPK = [f"tmp{tc}_{dc}" for tc in range(16) for dc in range(2)]
    SCANK = ["Ssb", "u", "u2", "wfm0", "wfm1"]
    ARSQK = ["q0", "q1", "k0", "k1"]
    ARSDK = [f"d{dvc}_{c}" for dvc in range(2) for c in range(16)]

    P.dma("sync", lambda e: e.dma_start(out=b1c[:], in_=b1c_d[:]), writes=[b1c])
    P.dma("sync", lambda e: e.dma_start(out=b2r[:], in_=b2r_d[:]), writes=[b2r])
    P.dma("sync", lambda e: e.dma_start(out=onw[:], in_=onw_d[:]), writes=[onw])
    P.dma("sync", lambda e: e.dma_start(out=bgc[:], in_=bg_d[:]), writes=[bgc])
    P.dma("sync", lambda e: e.dma_start(out=glt_sb[:, 0:4], in_=glt_d[:]), writes=[(glt_sb, "l")])
    P.dma("gpsimd", lambda e: e.dma_start(out=wb_sb[:], in_=wb_d[:].rearrange("p (c f) -> p c f", c=8)), writes=[wb_sb])
    P.dma("sync", lambda e: e.dma_start(out=cw_sb[:, :, 0:4], in_=cq_d[:].rearrange("p (c f) -> p c f", c=8)), writes=[(cw_sb, "q")])
    P.dma("sync", lambda e: e.dma_start(out=cw_sb[:, :, 4:8], in_=ck_d[:].rearrange("p (c f) -> p c f", c=8)), writes=[(cw_sb, "k")])
    P.dma("sync", lambda e: e.dma_start(out=cw_sb[:, :, 8:12], in_=cv_d[:].rearrange("p (c f) -> p c f", c=8)), writes=[(cw_sb, "v")])
    P.dma("sync", lambda e: e.dma_start(out=fsw[:], in_=fs_d[:].rearrange("p (c f) -> p c f", c=8)), writes=[fsw])
    P.dma("sync", lambda e: e.dma_start(out=flw[:], in_=fl_d[:].rearrange("p (c f) -> p c f", c=8)), writes=[flw])
    P.dma("gpsimd", lambda e: e.dma_start(out=wg_sb[:], in_=wg_d[:].rearrange("p (c f) -> p c f", c=8)), writes=[wg_sb])
    P.dma("gpsimd", lambda e: e.dma_start(out=w2_sb[:], in_=w2_d[:].rearrange("p (c f) -> p c f", c=16)), writes=[w2_sb])
    P.dma("sync", lambda e: e.dma_start(out=sel_sb[:], in_=sel_d[:]), writes=[sel_sb])
    P.dma("gpsimd", lambda e: e.dma_start(out=w1c9[:], in_=w1c9_d[:]), writes=[w1c9])

    g_ = glt_sb
    P.op("scalar", lambda e: e.activation(g_[:, 4:8], g_[:, 0:4], AF.Abs, bias=0.0, scale=1.0),
         reads=[(g_, "l")], writes=[(g_, "na")])
    P.op("scalar", lambda e: e.activation(g_[:, 4:8], g_[:, 4:8], AF.Exp, bias=0.0, scale=-1.0),
         reads=[(g_, "na")], writes=[(g_, "ex")])
    P.op("scalar", lambda e: e.activation(g_[:, 4:8], g_[:, 4:8], AF.Ln, bias=1.0, scale=1.0),
         reads=[(g_, "ex")], writes=[(g_, "ln")])
    P.op("vector", lambda e: e.tensor_scalar_max(rt16[:, 0:4], g_[:, 0:4], 0.0),
         reads=[(g_, "l")], writes=[(rt16, "mx")])
    P.op("vector", lambda e: e.tensor_tensor(rt16[:, 0:4], rt16[:, 0:4], g_[:, 4:8], ALU.add),
         reads=[(rt16, "mx"), (g_, "ln")], writes=[(rt16, "sp")])
    P.op("vector", lambda e: e.tensor_scalar(rt16[:, 0:4], rt16[:, 0:4], 1.0, 1e-4, ALU.mult, ALU.add),
         reads=[(rt16, "sp")], writes=[(rt16, "t")])
    P.op("vector", lambda e: e.reciprocal(rt16[:, 4:8], rt16[:, 0:4]),
         reads=[(rt16, "t")], writes=[(rt16, "r")])
    P.op("vector", lambda e: e.tensor_copy(
        rt16b[:, :].rearrange("p (h w) -> p h w", h=4),
        rt16[:, 4:8].rearrange("p (h w) -> p h w", w=1).broadcast_to((1, 4, 4))),
        reads=[(rt16, "r")], writes=[rt16b])
    mm(P, ps5[:, 0:16], C.onescol[0:1, 0:128], rt16b[0:1, :],
       reads=[C.onescol, rt16b], wkey=(ps5, "g"))
    P.op("vector", lambda e: e.tensor_copy(rtb[:, :], ps5[:, 0:16]),
         reads=[(ps5, "g")], writes=[rtb])

    for b in range(2):
        for tc in range(16):
            emit_ln(P, C, x1_d[b], xn1, psT, tc, tag=f"l1ln{b}_{tc}")
        P.op("gpsimd", lambda e: e.memset(SCRA[:, :], 0.0), writes=[(SCRA, None)])
        P.op("gpsimd", lambda e: e.memset(ARW[:, :], 0.0), writes=[(ARW, None)])
        P.op("gpsimd", lambda e: e.memset(ARX[:, :], 0.0), writes=[(ARX, None)])
        P.op("gpsimd", lambda e: e.memset(ARS[:, :], 0.0), writes=[(ARS, None)])

        for pi, (w_d_, scr, cwoff) in enumerate(((wq_d, q1scr, 0), (wk_d, k1scr, 4), (wv_d, None, 8))):
            W = Wv_[pi % 2]
            for k in range(8):
                P.dma("gpsimd", lambda e, k=k, W=W, w_d_=w_d_: e.dma_start(
                    out=W[:, k, :], in_=w_d_[k * 128:(k + 1) * 128, :]),
                    writes=[(ARW, f"w{pi % 2}_{k}")], tag=f"l1w{pi}_{k}")
            for mc in range(8):
                for tt in range(4):
                    ps = (ps0, ps1)[tt % 2]
                    for k in range(8):
                        mm(P, ps[:, :], W[:, k, mc * 128:(mc + 1) * 128],
                           xn1[:, k, tt * 512:(tt + 1) * 512],
                           reads=[(ARW, f"w{pi % 2}_{k}"), xn1], wkey=(ps, "g"),
                           start=(k == 0), stop=(k == 7))
                    P.op("vector", lambda e, ps=ps, tt=tt: e.tensor_copy(
                        CV[:, 3 + tt * 512:3 + (tt + 1) * 512], ps[:, :]),
                        reads=[(ps, "g")], writes=[(CV, f"t{tt}")])
                P.op("vector", lambda e: e.memset(CV[:, 0:3], 0.0), writes=[(CV, "pad")])
                cvr = [(CV, f"t{t_}") for t_ in range(4)] + [(CV, "pad")]
                for tt in range(4):
                    ps = (ps2, ps3)[tt % 2]
                    for kk in range(4):
                        off = kk + tt * 512
                        dg = (diag_a, diag_b)[kk % 2]
                        dgk = ((diag_a, None) if kk % 2 == 0 else (diag_b, None))
                        P.op("vector", lambda e, dg=dg, mc=mc, kk=kk, co=cwoff: e.tensor_scalar(
                            dg[:, :], C.idn[:], cw_sb[:, mc, co + kk:co + kk + 1], None, ALU.mult),
                            reads=[C.idn, cw_sb], writes=[dgk], tag="dgc")
                        mm(P, ps[:, :], dg[:, :], CV[:, off:off + 512],
                           reads=[dgk] + cvr, wkey=(ps, "g"),
                           start=(kk == 0), stop=(kk == 3))
                    if pi == 2:
                        dst = v1s(mc)[:, 64 + tt * 512:64 + (tt + 1) * 512]
                        dk_ = (SCRA, f"v{mc}")
                    else:
                        dst = qkst[:, tt * 512:(tt + 1) * 512]
                        dk_ = (qkst, f"t{tt}")
                    P.op("scalar", lambda e, dst=dst, ps=ps: e.activation(dst, ps[:, :], AF.Silu, bias=0.0, scale=1.0),
                         reads=[(ps, "g")], writes=[dk_])
                if pi != 2:
                    P.dma("sync", lambda e, scr=scr, b=b, mc=mc: e.dma_start(
                        out=scr[b, mc], in_=qkst[:, :]),
                        reads=[(qkst, f"t{t_}") for t_ in range(4)],
                        writes=[(scr, f"{b}_{mc}")], tag=f"qk{pi}{mc}")
        for tt in range(4):
            ps = (ps2, ps3)[tt % 2]
            for k in range(8):
                mm(P, ps[0:4, :], wb_sb[:, k, :], xn1[:, k, tt * 512:(tt + 1) * 512],
                   reads=[wb_sb, xn1], wkey=(ps, "g"), start=(k == 0), stop=(k == 7))
            P.op("scalar", lambda e, ps=ps, tt=tt: e.activation(
                beta_fm[:, tt * 512:(tt + 1) * 512], ps[0:4, :], AF.Sigmoid, bias=0.0, scale=1.0),
                reads=[(ps, "g")], writes=[(beta_fm, f"t{tt}")])
        for tc in range(16):
            P.op("tensor", lambda e, tc=tc: e.transpose(
                psTt[:, 0:4], beta_fm[0:4, tc * 128:(tc + 1) * 128], C.idn[0:4, 0:4]),
                reads=[(beta_fm, f"t{tc // 4}"), C.idn], writes=[(psTt, "t")])
            P.op("vector", lambda e, tc=tc: e.tensor_copy(beta_tm[:, tc, :], psTt[:, 0:4]),
                 reads=[(psTt, "t")], writes=[(beta_tm, f"t{tc}")])

        for mc in range(8):
            for (taps, wt_, scr, keyp) in ((3, fsw, fsscr, "fs"), (63, flw, flscr, "fl")):
                for tt in range(4):
                    ps = (ps0, ps1)[tt % 2]
                    for kk in range(taps):
                        off = 64 - (taps - 1) + kk + tt * 512
                        dg = (diag_a, diag_b)[kk % 2]
                        dgk = ((diag_a, None) if kk % 2 == 0 else (diag_b, None))
                        P.op("vector", lambda e, dg=dg, wt_=wt_, mc=mc, kk=kk: e.tensor_scalar(
                            dg[:, :], C.idn[:], wt_[:, mc, kk:kk + 1], None, ALU.mult),
                            reads=[C.idn, wt_], writes=[dgk], tag="dgb")
                        mm(P, ps[:, :], dg[:, :], v1s(mc)[:, off:off + 512],
                           reads=[dgk, (SCRA, f"v{mc}")], wkey=(ps, "g"),
                           start=(kk == 0), stop=(kk == taps - 1))
                    P.op("vector", lambda e, ps=ps, tt=tt: e.tensor_copy(
                        qkst[:, tt * 512:(tt + 1) * 512], ps[:, :]),
                        reads=[(ps, "g")], writes=[(qkst, f"f{tt}")])
                P.dma("sync", lambda e, scr=scr, b=b, mc=mc: e.dma_start(
                    out=scr[b, mc], in_=qkst[:, :]),
                    reads=[(qkst, f"f{t_}") for t_ in range(4)],
                    writes=[(scr, f"{b}_{mc}")], tag=f"{keyp}{mc}")
        for h in range(4):
            retag(ARS, ARSDK + ARSQK, tag=f"rtars{h}")
            P.dma("sync", lambda e, b=b, h=h: e.dma_start(out=q1h[:, 0, :], in_=q1scr[b, 2 * h]),
                  reads=[(q1scr, f"{b}_{2 * h}")], writes=[(ARS, "q0")])
            P.dma("sync", lambda e, b=b, h=h: e.dma_start(out=q1h[:, 1, :], in_=q1scr[b, 2 * h + 1]),
                  reads=[(q1scr, f"{b}_{2 * h + 1}")], writes=[(ARS, "q1")])
            P.dma("scalar", lambda e, b=b, h=h: e.dma_start(out=k1h[:, 0, :], in_=k1scr[b, 2 * h]),
                  reads=[(k1scr, f"{b}_{2 * h}")], writes=[(ARS, "k0")])
            P.dma("scalar", lambda e, b=b, h=h: e.dma_start(out=k1h[:, 1, :], in_=k1scr[b, 2 * h + 1]),
                  reads=[(k1scr, f"{b}_{2 * h + 1}")], writes=[(ARS, "k1")])
            retag(ARW, SCANK + TMPK + [f"vb{t}" for t in range(16)] + [f"kn{t}" for t in range(16)] + [f"kb{t}" for t in range(16)], tag=f"rtarw{h}")
            for tc in range(16):
                for dc in range(2):
                    P.op("tensor", lambda e, tc=tc, dc=dc, h=h: e.transpose(
                        psTt[:, 0:128], v1s(2 * h + dc)[:, 64 + tc * 128:64 + (tc + 1) * 128], C.idn[:]),
                        reads=[(SCRA, f"v{2 * h + dc}"), C.idn], writes=[(psTt, "t")])
                    P.op("vector", lambda e, tc=tc, dc=dc: e.tensor_copy(
                        TMPa[:, tc, dc * 128:(dc + 1) * 128], psTt[:, 0:128]),
                        reads=[(psTt, "t")], writes=[(ARW, f"tmp{tc}_{dc}")])
                P.op("vector", lambda e, tc=tc, h=h: e.tensor_scalar(
                    vb_tm[:, tc, :], TMPa[:, tc, :], beta_tm[:, tc, h:h + 1], None, ALU.mult),
                    reads=[(ARW, f"tmp{tc}_0"), (ARW, f"tmp{tc}_1"), (beta_tm, f"t{tc}")],
                    writes=[(ARW, f"vb{tc}")])
            for (which, srcv, sm) in (("k", k1h, smallA), ("q", q1h, smallB)):
                for tc in range(16):
                    for dc in range(2):
                        P.op("tensor", lambda e, tc=tc, dc=dc, srcv=srcv: e.transpose(
                            psTt[:, 0:128], srcv[:, dc, tc * 128:(tc + 1) * 128], C.idn[:]),
                            reads=[(ARS, f"{which}{dc}"), C.idn], writes=[(psTt, "t")])
                        P.op("vector", lambda e, tc=tc, dc=dc: e.tensor_copy(
                            TMPa[:, tc, dc * 128:(dc + 1) * 128], psTt[:, 0:128]),
                            reads=[(psTt, "t")], writes=[(ARW, f"tmp{tc}_{dc}")])
                    P.op("scalar", lambda e, tc=tc, sm=sm: e.activation(
                        ps4[:, 0:256], TMPa[:, tc, :], AF.Square, bias=0.0, scale=1.0,
                        accum_out=sm[:, 0:1]),
                        reads=[(ARW, f"tmp{tc}_0"), (ARW, f"tmp{tc}_1")],
                        writes=[(ps4, "g"), (sm, "n")])
                    P.op("vector", lambda e, sm=sm: e.tensor_scalar(sm[:, 1:2], sm[:, 0:1], 1.0, 1e-6, ALU.mult, ALU.add),
                         reads=[(sm, "n")], writes=[(sm, "n1")])
                    P.op("scalar", lambda e, sm=sm: e.activation(sm[:, 2:3], sm[:, 1:2], AF.Sqrt, bias=0.0, scale=1.0),
                         reads=[(sm, "n1")], writes=[(sm, "n2")])
                    P.op("vector", lambda e, sm=sm: e.reciprocal(sm[:, 3:4], sm[:, 2:3]),
                         reads=[(sm, "n2")], writes=[(sm, "n3")])
                    if which == "k":
                        P.op("vector", lambda e, tc=tc, sm=sm: e.tensor_scalar(
                            kn_tm[:, tc, :], TMPa[:, tc, :], sm[:, 3:4], None, ALU.mult),
                            reads=[(ARW, f"tmp{tc}_0"), (ARW, f"tmp{tc}_1"), (sm, "n3")],
                            writes=[(ARW, f"kn{tc}")])
                        P.op("vector", lambda e, tc=tc, h=h: e.tensor_scalar(
                            kb_tm[:, tc, :], kn_tm[:, tc, :], beta_tm[:, tc, h:h + 1], None, ALU.mult),
                            reads=[(ARW, f"kn{tc}"), (beta_tm, f"t{tc}")], writes=[(ARW, f"kb{tc}")])
                    else:
                        P.op("vector", lambda e, tc=tc, sm=sm: e.tensor_scalar(
                            TMPa[:, tc, :], TMPa[:, tc, :], sm[:, 3:4], None, ALU.mult),
                            reads=[(ARW, f"tmp{tc}_0"), (ARW, f"tmp{tc}_1"), (sm, "n3")],
                            writes=[(ARW, f"tmp{tc}_0"), (ARW, f"tmp{tc}_1")])
                if which == "k":
                    for (src, dstv, kp_, srckeys) in (
                            (kn_tm, kn_fm, "knf", "kn"), (kb_tm, kb_fm, "kbf", "kb")):
                        for tc in range(16):
                            for dc in range(2):
                                P.op("tensor", lambda e, tc=tc, dc=dc, src=src: e.transpose(
                                    psTt[:, 0:128], src[:, tc, dc * 128:(dc + 1) * 128], C.idn[:]),
                                    reads=[(ARW, f"{srckeys}{tc}"), C.idn], writes=[(psTt, "t")])
                                P.op("vector", lambda e, tc=tc, dc=dc, dstv=dstv, kp_=kp_: e.tensor_copy(
                                    dstv[:, dc, tc * 128:(tc + 1) * 128], psTt[:, 0:128]),
                                    reads=[(psTt, "t")], writes=[(ARX, f"{kp_}{dc}_{tc}")])
                else:
                    for tc in range(16):
                        for dc in range(2):
                            P.op("tensor", lambda e, tc=tc, dc=dc: e.transpose(
                                psTt[:, 0:128], TMPa[:, tc, dc * 128:(dc + 1) * 128], C.idn[:]),
                                reads=[(ARW, f"tmp{tc}_0"), (ARW, f"tmp{tc}_1"), C.idn],
                                writes=[(psTt, "t")])
                            P.op("vector", lambda e, tc=tc, dc=dc: e.tensor_copy(
                                qn_fm[:, dc, tc * 128:(tc + 1) * 128], psTt[:, 0:128]),
                                reads=[(psTt, "t")], writes=[(ARX, f"qn{dc}_{tc}")])
            for c in range(16):
                ps = (ps2, ps3)[c % 2]
                for dc in range(2):
                    mm(P, ps[:, 0:128], kn_fm[:, dc, c * 128:(c + 1) * 128],
                       qn_fm[:, dc, c * 128:(c + 1) * 128],
                       reads=[(ARX, f"knf{dc}_{c}"), (ARX, f"qn{dc}_{c}")], wkey=(ps, "g"),
                       start=(dc == 0), stop=(dc == 1))
                P.op("vector", lambda e, ps=ps, c=c: e.tensor_tensor(
                    attnT[:, c, :], ps[:, 0:128], C.mtriu[:], ALU.mult),
                    reads=[(ps, "g"), C.mtriu], writes=[(ARX, f"at{c}")])
            retag(ARW, TMPK + SCANK, tag=f"rtscan{h}")
            retag(ARS, ARSQK + ARSDK, tag=f"rtsdf{h}")
            P.op("vector", lambda e: e.memset(Ssb[:], 0.0), writes=[(ARW, "Ssb")])
            for c in range(16):
                p_ = c % 2
                N_, Q_, qa_, qb_, na_, nb_, R_ = [chs(sl + 8 * p_) for sl in range(7)]
                kN, kQ, kqa, kqb, kna, knb, kR = [
                    f"ch{s}{p_}" for s in ("N", "Q", "qa", "qb", "na", "nb", "R")]
                psN = (ps2, ps3)[c % 2]
                psQ_ = (ps0, ps1)[c % 2]
                for dc in range(2):
                    mm(P, psN[:, 0:128], kb_fm[:, dc, c * 128:(c + 1) * 128],
                       kn_fm[:, dc, c * 128:(c + 1) * 128],
                       reads=[(ARX, f"kbf{dc}_{c}"), (ARX, f"knf{dc}_{c}")], wkey=(psN, "g"),
                       start=(dc == 0), stop=(dc == 1))
                for dc in range(2):
                    mm(P, psQ_[:, 0:128], kn_fm[:, dc, c * 128:(c + 1) * 128],
                       kb_fm[:, dc, c * 128:(c + 1) * 128],
                       reads=[(ARX, f"knf{dc}_{c}"), (ARX, f"kbf{dc}_{c}")], wkey=(psQ_, "g"),
                       start=(dc == 0), stop=(dc == 1))
                P.op("vector", lambda e, psN=psN, N_=N_: e.tensor_tensor(N_, psN[:, 0:128], C.mlow[:], ALU.mult),
                     reads=[(psN, "g"), C.mlow], writes=[(ARX, kN)])
                P.op("vector", lambda e, psQ_=psQ_, Q_=Q_: e.tensor_tensor(Q_, psQ_[:, 0:128], C.mup[:], ALU.mult),
                     reads=[(psQ_, "g"), C.mup], writes=[(ARX, kQ)])
                P.op("vector", lambda e, Q_=Q_, R_=R_: e.tensor_tensor(R_, Q_, C.idn[:], ALU.add),
                     reads=[(ARX, kQ), C.idn], writes=[(ARX, kR)])
                curNk, curQk, curNap, curQap = kN, kQ, N_, Q_
                for i in range(6):
                    psA = (ps2, ps3)[(c + i + 1) % 2]
                    psB = (ps0, ps1)[(c + i + 1) % 2]
                    mm(P, psA[:, 0:128], curNap, curQap,
                       reads=[(ARX, curNk), (ARX, curQk)], wkey=(psA, "g"))
                    mm(P, psB[:, 0:128], curQap, curNap,
                       reads=[(ARX, curQk), (ARX, curNk)], wkey=(psB, "g"))
                    newQ, kq = ((qa_, kqa), (qb_, kqb))[i % 2]
                    newN, kn = ((na_, kna), (nb_, knb))[i % 2]
                    P.op("vector", lambda e, psA=psA, newQ=newQ: e.tensor_copy(newQ, psA[:, 0:128]),
                         reads=[(psA, "g")], writes=[(ARX, kq)])
                    P.op("scalar", lambda e, psB=psB, newN=newN: e.copy(newN, psB[:, 0:128]),
                         reads=[(psB, "g")], writes=[(ARX, kn)])
                    mm(P, psA[:, 0:128], newN, R_,
                       reads=[(ARX, kn), (ARX, kR)], wkey=(psA, "g"),
                       start=True, stop=False)
                    mm(P, psA[:, 0:128], C.idn[:], R_,
                       reads=[C.idn, (ARX, kR)], wkey=(psA, "g"), start=False, stop=True)
                    P.op("vector", lambda e, psA=psA, R_=R_: e.tensor_copy(R_, psA[:, 0:128]),
                         reads=[(psA, "g")], writes=[(ARX, kR)])
                    curNk, curQk, curNap, curQap = kn, kq, newN, newQ
                mm(P, ps4[:, 0:256], R_, vb_tm[:, c, :],
                   reads=[(ARX, kR), (ARW, f"vb{c}")], wkey=(ps4, "g"))
                P.op("vector", lambda e: e.tensor_copy(u_sb[:], ps4[:, 0:256]),
                     reads=[(ps4, "g")], writes=[(ARW, "u")])
                for dc in range(2):
                    mm(P, ps5[:, 0:128], kb_tm[:, c, dc * 128:(dc + 1) * 128], R_,
                       reads=[(ARW, f"kb{c}"), (ARX, kR)], wkey=(ps5, "g"))
                    P.op("vector", lambda e, dc=dc: e.tensor_copy(
                        wfm_c[:, dc * 128:(dc + 1) * 128], ps5[:, 0:128]),
                        reads=[(ps5, "g")], writes=[(ARW, f"wfm{dc}")])
                for dc in range(2):
                    mm(P, ps4[:, 0:256], wfm_c[:, dc * 128:(dc + 1) * 128],
                       Ssb[:, dc * 256:(dc + 1) * 256],
                       reads=[(ARW, f"wfm{dc}"), (ARW, "Ssb")], wkey=(ps4, "g"),
                       start=(dc == 0), stop=(dc == 1))
                P.op("vector", lambda e: e.tensor_tensor(u2_sb[:], u_sb[:], ps4[:, 0:256], ALU.subtract),
                     reads=[(ARW, "u"), (ps4, "g")], writes=[(ARW, "u2")])
                for dvc in range(2):
                    ps_ = (ps4, ps5)[dvc]
                    for dc in range(2):
                        mm(P, ps_[:, 0:128], Ssb[:, dc * 256 + dvc * 128:dc * 256 + (dvc + 1) * 128],
                           qn_fm[:, dc, c * 128:(c + 1) * 128],
                           reads=[(ARW, "Ssb"), (ARX, f"qn{dc}_{c}")], wkey=(ps_, "g"),
                           start=(dc == 0), stop=False)
                    mm(P, ps_[:, 0:128], u2_sb[:, dvc * 128:(dvc + 1) * 128], attnT[:, c, :],
                       reads=[(ARW, "u2"), (ARX, f"at{c}")], wkey=(ps_, "g"),
                       start=False, stop=True)
                    P.op("vector", lambda e, ps_=ps_, c=c, dvc=dvc: e.tensor_copy(
                        dfm[dvc][:, c * 128:(c + 1) * 128], ps_[:, 0:128]),
                        reads=[(ps_, "g")], writes=[(ARS, f"d{dvc}_{c}")])
                for dc in range(2):
                    mm(P, psS[:, dc * 256:(dc + 1) * 256], kn_tm[:, c, dc * 128:(dc + 1) * 128],
                       u2_sb[:, 0:256],
                       reads=[(ARW, f"kn{c}"), (ARW, "u2")], wkey=(psS, "S"),
                       start=(c == 0 and dc == 0), stop=(c == 15 and dc == 1))
                P.op("vector", lambda e: e.tensor_copy(Ssb[:], psS[:, :]),
                     reads=[(psS, "S")], writes=[(ARW, "Ssb")])
            for dvc in range(2):
                P.dma("sync", lambda e, b=b, h=h, dvc=dvc: e.dma_start(
                    out=dlscr[b, 2 * h + dvc], in_=dfm[dvc][:, :]),
                    reads=[(ARS, f"d{dvc}_{c_}") for c_ in range(16)],
                    writes=[(dlscr, f"{b}_{2 * h + dvc}")], tag=f"dl{h}{dvc}")

        retag(ARW, [f"w{i}_{k}" for i in range(2) for k in range(8)] + ["brstg"] + [f"g1w{k}" for k in range(8)], tag="rtstat")
        for br_i, scr in enumerate((fsscr, flscr, dlscr, None)):
            for h in range(4):
                psA, psB = (ps0, ps1) if (h % 2 == 0) else (ps2, ps3)
                for dc in range(2):
                    ch_ = 2 * h + dc
                    if scr is not None:
                        P.dma("sync", lambda e, scr=scr, b=b, ch_=ch_: e.dma_start(
                            out=brstg[:], in_=scr[b, ch_]),
                            reads=[(scr, f"{b}_{ch_}")], writes=[(ARW, "brstg")], tag="stl")
                        src_ap = brstg
                        src_k = (ARW, "brstg")
                    else:
                        src_ap = v1s(ch_)[:, 64:2112]
                        src_k = (SCRA, f"v{ch_}")
                    P.op("scalar", lambda e, src_ap=src_ap: e.activation(
                        sq_st[:, :], src_ap, AF.Square, bias=0.0, scale=1.0),
                        reads=[src_k], writes=[(sq_st, None)])
                    for tc in range(16):
                        mm(P, psA[:, tc:tc + 1], src_ap[:, tc * 128:(tc + 1) * 128],
                           C.onescol[:, 0:1],
                           reads=[src_k, C.onescol], wkey=(psA, "g"),
                           start=(dc == 0 and tc == 0), stop=(dc == 1 and tc == 15))
                        mm(P, psB[:, tc:tc + 1], sq_st[:, tc * 128:(tc + 1) * 128],
                           C.onescol[:, 0:1],
                           reads=[(sq_st, None), C.onescol], wkey=(psB, "g"),
                           start=(dc == 0 and tc == 0), stop=(dc == 1 and tc == 15))
                cm = br_i * 8 + h
                cs = br_i * 8 + 4 + h
                P.op("vector", lambda e, psA=psA, cm=cm: e.tensor_scalar(
                    stats_tm[:, :, cm:cm + 1], psA[:, 0:16].rearrange("p (a one) -> p a one", one=1),
                    1.0 / 256, None, ALU.mult),
                    reads=[(psA, "g")], writes=[(stats_tm, f"c{cm}")])
                P.op("vector", lambda e, psB=psB, cs=cs: e.tensor_scalar(
                    stats_tm[:, :, cs:cs + 1], psB[:, 0:16].rearrange("p (a one) -> p a one", one=1),
                    1.0 / 256, None, ALU.mult),
                    reads=[(psB, "g")], writes=[(stats_tm, f"c{cs}")])
        STK = [(stats_tm, f"c{i}") for i in range(32)]
        P.op("gpsimd", lambda e: e.memset(stats32[:, :], 0.0),
             writes=[(stats32, f"t{t_}") for t_ in range(16)] + [(stats32, "z")])
        for tc in range(16):
            mview = stats_tm[:, tc, :].rearrange("p (br k) -> p br k", k=8)[:, :, 0:4]
            svview = stats_tm[:, tc, :].rearrange("p (br k) -> p br k", k=8)[:, :, 4:8]
            tview = smallA[:, 0:16].rearrange("p (br k) -> p br k", k=4)
            P.op("vector", lambda e, mview=mview, tview=tview: e.tensor_tensor(
                tview, mview, mview, ALU.mult), reads=STK, writes=[(smallA, "m2")])
            P.op("vector", lambda e, svview=svview, tview=tview: e.tensor_tensor(
                svview, svview, tview, ALU.subtract),
                reads=STK + [(smallA, "m2")], writes=STK)
            P.op("vector", lambda e, svview=svview: e.tensor_scalar_max(
                svview, svview, 1e-6), reads=STK, writes=STK)
            P.op("scalar", lambda e, svview=svview: e.activation(
                svview, svview, AF.Sqrt, bias=0.0, scale=1.0), reads=STK, writes=STK)
            P.op("tensor", lambda e, tc=tc: e.transpose(
                ps5[0:32, 0:128], stats_tm[:, tc, :], C.idnf[:]),
                reads=STK, writes=[(ps5, "g")])
            P.op("vector", lambda e, tc=tc: e.tensor_copy(
                stats32[0:32, tc * 128:(tc + 1) * 128], ps5[0:32, 0:128]),
                reads=[(ps5, "g")], writes=[(stats32, f"t{tc}")])
        if dbg and b == 0:
            P.dma("sync", lambda e: e.dma_start(out=dbg_stats[:], in_=stats_tm[:, :, :]),
                  reads=STK)

        if dbg and b == 0:
            P.dma("sync", lambda e: e.dma_start(out=dbg_xn1[:], in_=xn1[:, :, :]), reads=[xn1])
        retag(ARW, ["brstg"] + [f"g1w{k}" for k in range(8)], tag="rtgate")
        h1g = qkst
        retag(qkst, [None, "s0", "s1", "s1l"] + [f"t{t_}" for t_ in range(4)], tag="rth1g")
        for k in range(8):
            P.dma("gpsimd", lambda e, k=k: e.dma_start(
                out=w1v[:, k, :], in_=w1_d[k * 128:(k + 1) * 128, :]),
                writes=[(ARW, f"g1w{k}")], tag=f"w1l{k}")
        if dbg and b == 0:
            P.dma("sync", lambda e: e.dma_start(out=dbg_w1[:], in_=w1v[:, 0, :]),
                  reads=[(ARW, "g1w0")])
        for mh in range(16):
            for tt in range(4):
                ps = (ps0, ps1, ps4, ps2)[tt]
                for k in range(8):
                    mm(P, ps[:, :], w1v[:, k, mh * 128:(mh + 1) * 128],
                       xn1[:, k, tt * 512:(tt + 1) * 512],
                       reads=[(ARW, f"g1w{k}"), xn1], wkey=(ps, "g"),
                       start=(k == 0), stop=False)
                mm(P, ps[:, :], w1c9[:, mh * 128:(mh + 1) * 128],
                   stats32[:, tt * 512:(tt + 1) * 512],
                   reads=[w1c9, (stats32, "z")] + [(stats32, f"t{t_}") for t_ in range(4 * tt, 4 * tt + 4)],
                   wkey=(ps, "g"), start=False, stop=True)
                if dbg and b == 0 and mh == 0 and tt == 0:
                    P.op("vector", lambda e, ps=ps: e.tensor_copy(OUTF[:, 0:512], ps[:, 0:512]),
                         reads=[(ps, "g")], writes=[OUTF])
                    P.dma("sync", lambda e: e.dma_start(out=dbg_pre[:], in_=OUTF[:, 0:512]),
                          reads=[OUTF])
                P.op("scalar", lambda e, ps=ps, tt=tt, mh=mh: e.activation(
                    h1g[:, tt * 512:(tt + 1) * 512], ps[:, :], AF.Gelu, bias=b1c[:, mh:mh + 1], scale=1.0),
                    reads=[(ps, "g"), b1c], writes=[(qkst, f"t{tt}")])
                if dbg and b == 0 and mh == 0 and tt == 0:
                    P.dma("sync", lambda e: e.dma_start(out=dbg_h1[:], in_=h1g[:, 0:512]),
                          reads=[(qkst, "t0")])
                mm(P, ps5[0:16, 0:512], w2_sb[:, mh, :], h1g[:, tt * 512:(tt + 1) * 512],
                   reads=[w2_sb, (qkst, f"t{tt}")], wkey=(ps5, "g"))
                if mh == 0:
                    P.op("vector", lambda e, tt=tt: e.tensor_copy(
                        logits_fm[:, tt * 512:(tt + 1) * 512], ps5[0:16, 0:512]),
                        reads=[(ps5, "g")], writes=[(logits_fm, f"t{tt}")])
                else:
                    P.op("vector", lambda e, tt=tt: e.tensor_tensor(
                        logits_fm[:, tt * 512:(tt + 1) * 512],
                        logits_fm[:, tt * 512:(tt + 1) * 512], ps5[0:16, 0:512], ALU.add),
                        reads=[(ps5, "g"), (logits_fm, f"t{tt}")], writes=[(logits_fm, f"t{tt}")])
        P.op("vector", lambda e: e.tensor_scalar(logits_fm[:, :], logits_fm[:, :], b2r[:, 0:1], None, ALU.add),
             reads=[b2r] + [(logits_fm, f"t{t_}") for t_ in range(4)], writes=[(logits_fm, None)])
        if dbg and b == 0:
            P.dma("sync", lambda e: e.dma_start(out=dbg_lgf[:], in_=logits_fm[:, :]),
                  reads=[(logits_fm, None)])
            P.dma("sync", lambda e: e.dma_start(out=dbg_s32[:], in_=stats32[:, :]),
                  reads=[(stats32, "z")] + [(stats32, f"t{t_}") for t_ in range(16)])
        for tc in range(16):
            P.op("tensor", lambda e, tc=tc: e.transpose(
                psTt[:, 0:16], logits_fm[0:16, tc * 128:(tc + 1) * 128], C.idn[0:16, 0:16]),
                reads=[(logits_fm, None), C.idn], writes=[(psTt, "t")])
            P.op("vector", lambda e, tc=tc: e.tensor_tensor(
                lg_tm[:, tc, :], psTt[:, 0:16], rtb[:, :], ALU.mult),
                reads=[(psTt, "t"), rtb], writes=[(lg_tm, f"t{tc}")])
            l4 = lg_tm[:, tc, :].rearrange("p (h w) -> p h w", h=4)
            P.op("vector", lambda e, l4=l4: e.tensor_reduce(
                smallA[:, 0:4].rearrange("p (h w) -> p h w", h=4), l4, AX.X, ALU.max),
                reads=[(lg_tm, f"t{tc}")], writes=[(smallA, "gm")])
            P.op("vector", lambda e, l4=l4: e.tensor_tensor(
                l4, l4, smallA[:, 0:4].rearrange("p (h w) -> p h w", w=1).broadcast_to((128, 4, 4)),
                ALU.subtract),
                reads=[(lg_tm, f"t{tc}"), (smallA, "gm")], writes=[(lg_tm, f"t{tc}")])
            P.op("scalar", lambda e, tc=tc: e.activation(
                w_tm[:, tc, :], lg_tm[:, tc, :], AF.Exp, bias=0.0, scale=1.0),
                reads=[(lg_tm, f"t{tc}")], writes=[(w_tm, f"t{tc}")])
            w4 = w_tm[:, tc, :].rearrange("p (h w) -> p h w", h=4)
            P.op("vector", lambda e, w4=w4: e.tensor_reduce(
                smallB[:, 0:4].rearrange("p (h w) -> p h w", h=4), w4, AX.X, ALU.add),
                reads=[(w_tm, f"t{tc}")], writes=[(smallB, "gs")])
            P.op("vector", lambda e: e.reciprocal(smallB[:, 4:8], smallB[:, 0:4]),
                 reads=[(smallB, "gs")], writes=[(smallB, "gr")])
            P.op("vector", lambda e, w4=w4: e.tensor_tensor(
                w4, w4, smallB[:, 4:8].rearrange("p (h w) -> p h w", w=1).broadcast_to((128, 4, 4)),
                ALU.mult),
                reads=[(w_tm, f"t{tc}"), (smallB, "gr")], writes=[(w_tm, f"t{tc}")])
            P.op("tensor", lambda e, tc=tc: e.transpose(
                ps5[0:16, 0:128], w_tm[:, tc, :], C.idnf[:]),
                reads=[(w_tm, f"t{tc}")], writes=[(ps5, "g")])
            P.op("vector", lambda e, tc=tc: e.tensor_copy(
                wfm16[0:16, tc * 128:(tc + 1) * 128], ps5[0:16, 0:128]),
                reads=[(ps5, "g")], writes=[(wfm16, f"t{tc}")])
        if dbg and b == 0:
            P.dma("sync", lambda e: e.dma_start(out=dbg_w[:], in_=w_tm[:, :, :]),
                  reads=[(w_tm, f"t{t_}") for t_ in range(16)])
            P.dma("sync", lambda e: e.dma_start(out=dbg_lg[:], in_=lg_tm[:, :, :]),
                  reads=[(lg_tm, f"t{t_}") for t_ in range(16)])
            P.dma("sync", lambda e: e.dma_start(out=dbg_rtb[:], in_=rtb[:, :]), reads=[rtb])

        if dbg and b == 0:
            P.dma("scalar", lambda e: e.dma_start(out=dbg_wfm[:], in_=wfm16[:, :]),
                  reads=[(wfm16, f"t{t_}") for t_ in range(16)])
            P.dma("scalar", lambda e: e.dma_start(out=dbg_sel[:], in_=sel_sb[:, :]),
                  reads=[sel_sb])
        retag(ARX, [f"ch{s_}{p_}" for s_ in ("N", "Q", "qa", "qb", "na", "nb", "R") for p_ in range(2)]
              + [f"at{c}" for c in range(16)]
              + [f"{kp}{dc}_{tc}" for kp in ("qn", "knf", "kbf") for dc in range(2) for tc in range(16)]
              + [f"on{ch}" for ch in range(8)], tag="rtcomb")
        retag(ARW, ["brstg", "brs0", "brs1", "brl0", "brl1", "brd0", "brd1"]
              + [f"g1w{k}" for k in range(8)], tag="rtcomb2")
        wrow = rsd_row
        for h in range(4):
            for (vv, scr, kk_) in ((brs, fsscr, "brs"), (brl, flscr, "brl"), (brd, dlscr, "brd")):
                for dc in range(2):
                    P.dma("sync", lambda e, scr=scr, b=b, h=h, dc=dc, vv=vv: e.dma_start(
                        out=vv[:, dc, :], in_=scr[b, 2 * h + dc]),
                        reads=[(scr, f"{b}_{2 * h + dc}")],
                        writes=[(ARW, f"{kk_}{dc}")], tag=f"cb{kk_}{dc}")
            for br in range(4):
                r = h * 4 + br
                P.dma("scalar", lambda e, r=r: e.dma_start(out=wrow, in_=wfm16[r:r + 1, :]),
                      reads=[(wfm16, f"t{t_}") for t_ in range(16)],
                      writes=[(beta_fm, "row")], tag=f"wrow{r}")
                for dc in range(2):
                    ch_ = 2 * h + dc
                    for tt in range(4):
                        sl = slice(tt * 512, (tt + 1) * 512)
                        bt, bk_ = ((brs[:, dc, sl], (ARW, f"brs{dc}")),
                                   (brl[:, dc, sl], (ARW, f"brl{dc}")),
                                   (brd[:, dc, sl], (ARW, f"brd{dc}")),
                                   (v1s(ch_)[:, 64 + tt * 512:64 + (tt + 1) * 512],
                                    (SCRA, f"v{ch_}")))[br]
                        ps_ = (ps0, ps1)[(dc * 4 + tt) % 2]
                        mm(P, ps_[:, :], C.onescol[0:1, 0:128], wrow[0:1, sl],
                           reads=[C.onescol, (beta_fm, "row")], wkey=(ps_, "wb"))
                        P.op("vector", lambda e, ps_=ps_: e.tensor_copy(wbst[:, :], ps_[:, :]),
                             reads=[(ps_, "wb")], writes=[wbst])
                        if br == 0:
                            P.op("vector", lambda e, bt=bt, ch_=ch_, sl=sl: e.tensor_tensor(
                                o_n[:, ch_, sl], bt, wbst[:, :], ALU.mult),
                                reads=[bk_, wbst], writes=[(ARX, f"on{ch_}")])
                        else:
                            P.op("vector", lambda e, bt=bt: e.tensor_tensor(
                                sq_st[:, 0:512], bt, wbst[:, :], ALU.mult),
                                reads=[bk_, wbst], writes=[(sq_st, "cb")])
                            P.op("vector", lambda e, ch_=ch_, sl=sl: e.tensor_tensor(
                                o_n[:, ch_, sl], o_n[:, ch_, sl], sq_st[:, 0:512], ALU.add),
                                reads=[(sq_st, "cb"), (ARX, f"on{ch_}")], writes=[(ARX, f"on{ch_}")])
            for dc in range(2):
                ch_ = 2 * h + dc
                P.op("scalar", lambda e, ch_=ch_: e.activation(
                    sq_st[:, :], o_n[:, ch_, :], AF.Square, bias=0.0, scale=1.0),
                    reads=[(ARX, f"on{ch_}")], writes=[(sq_st, None)])
                for tc in range(16):
                    mm(P, ps4[:, tc:tc + 1], sq_st[:, tc * 128:(tc + 1) * 128], C.onescol[:, 0:1],
                       reads=[(sq_st, None), C.onescol], wkey=(ps4, "g"),
                       start=(dc == 0 and tc == 0), stop=(dc == 1 and tc == 15))
            P.op("vector", lambda e: e.tensor_scalar(nrm16[:, :], ps4[:, 0:16], 1.0 / 256, 1e-5, ALU.mult, ALU.add),
                 reads=[(ps4, "g")], writes=[(nrm16, "a")])
            P.op("scalar", lambda e: e.activation(nrm16[:, :], nrm16[:, :], AF.Sqrt, bias=0.0, scale=1.0),
                 reads=[(nrm16, "a")], writes=[(nrm16, "b")])
            P.op("vector", lambda e: e.reciprocal(nrm16[:, :], nrm16[:, :]),
                 reads=[(nrm16, "b")], writes=[(nrm16, "c")])
            P.op("tensor", lambda e: e.transpose(ps5[0:16, 0:128], nrm16[:, :], C.idnf[:]),
                 reads=[(nrm16, "c")], writes=[(ps5, "g")])
            P.op("vector", lambda e: e.tensor_copy(rsd16[:, :], ps5[0:16, 0:128]),
                 reads=[(ps5, "g")], writes=[rsd16])
            P.dma("sync", lambda e: e.dma_start(out=rsd_dram[:, :], in_=rsd16[:, :]),
                  reads=[rsd16], writes=[(rsd_dram, "d")], tag="rsro")
            P.dma("gpsimd", lambda e: e.dma_start(out=rsd_row, in_=rsd_dram[:].rearrange("a f -> (a f)")[None, :]),
                  reads=[(rsd_dram, "d")], writes=[(beta_fm, "row")], tag="rsrow")
            if dbg and b == 0 and h == 0:
                P.dma("scalar", lambda e: e.dma_start(out=dbg_rsd[:], in_=rsd_row),
                      reads=[(beta_fm, "row")])
            for tt in range(4):
                mm(P, ps0[:, :], C.onescol[0:1, 0:128], rsd_row[0:1, tt * 512:(tt + 1) * 512],
                   reads=[C.onescol, (beta_fm, "row")], wkey=(ps0, "g"))
                for dc in range(2):
                    ch_ = 2 * h + dc
                    sl = slice(tt * 512, (tt + 1) * 512)
                    P.op("vector", lambda e, ch_=ch_, sl=sl, dc=dc: e.scalar_tensor_tensor(
                        o_n[:, ch_, sl], o_n[:, ch_, sl], onw[:, dc:dc + 1], ps0[:, :],
                        ALU.mult, ALU.mult),
                        reads=[(ARX, f"on{ch_}"), onw, (ps0, "g")], writes=[(ARX, f"on{ch_}")])

        if dbg and b == 0:
            P.dma("sync", lambda e: e.dma_start(out=dbg_on[:], in_=ARX[:, 0:16384]),
                  reads=[(ARX, f"on{ch}") for ch in range(8)])
        retag(ARW, [f"wo{k}" for k in range(8)] + ["brs0", "brs1", "brl0", "brl1", "brd0", "brd1", "brstg"], tag="rtwo")
        for k in range(8):
            P.dma("gpsimd", lambda e, k=k: e.dma_start(
                out=wov[:, k, :], in_=wo_d[k * 128:(k + 1) * 128, :]),
                writes=[(ARW, f"wo{k}")], tag=f"wol{k}")
        for tc in range(16):
            for ft in range(2):
                ps = (ps0, ps1)[ft]
                for ch_ in range(8):
                    mm(P, ps[:, :], o_n[:, ch_, tc * 128:(tc + 1) * 128],
                       wov[:, ch_, ft * 512:(ft + 1) * 512],
                       reads=[(ARX, f"on{ch_}"), (ARW, f"wo{ch_}")], wkey=(ps, "g"),
                       start=(ch_ == 0), stop=(ch_ == 7))
                P.op("vector", lambda e, ps=ps, ft=ft: e.tensor_copy(
                    qkst[:, ft * 512 + 0:ft * 512 + 512] if False else qkst[:, ft * 512:(ft + 1) * 512],
                    ps[:, :]),
                    reads=[(ps, "g")], writes=[(qkst, f"s{ft}")])
            P.dma("sync", lambda e, b=b, tc=tc: e.dma_start(
                out=s1scr[b, tc * 128:(tc + 1) * 128, :], in_=qkst[:, 0:1024]),
                reads=[(qkst, "s0"), (qkst, "s1")],
                writes=[(s1scr, f"{b}_{tc}")], tag=f"s1s{tc}")

        logf = logits_fm[0:8, :]
        retag(logits_fm, [f"t{t_}" for t_ in range(4)] + [None] + [f"lf{t_}" for t_ in range(4)], tag="rtlf")
        for tt in range(4):
            ps = (ps2, ps3)[tt % 2]
            for k in range(8):
                mm(P, ps[0:8, :], wg_sb[:, k, :], xn1[:, k, tt * 512:(tt + 1) * 512],
                   reads=[wg_sb, xn1], wkey=(ps, "g"), start=(k == 0), stop=(k == 7))
            P.op("vector", lambda e, ps=ps, tt=tt: e.tensor_scalar(
                logf[:, tt * 512:(tt + 1) * 512], ps[0:8, :], bgc[:, 0:1], None, ALU.add),
                reads=[(ps, "g"), bgc], writes=[(logits_fm, f"lf{tt}")])
        for tc in range(16):
            P.op("tensor", lambda e, tc=tc: e.transpose(
                psTt[:, 0:8], logf[0:8, tc * 128:(tc + 1) * 128], C.idn[0:8, 0:8]),
                reads=[(logits_fm, f"lf{tc // 4}"), C.idn], writes=[(psTt, "t")])
            P.op("vector", lambda e, tc=tc: e.tensor_copy(smallA[:, 0:8], psTt[:, 0:8]),
                 reads=[(psTt, "t")], writes=[(smallA, "lg")])
            P.op("vector", lambda e: e.reduce_max(smallB[:, 0:1], smallA[:, 0:8], axis=AX.X),
                 reads=[(smallA, "lg")], writes=[(smallB, "m1")])
            P.op("vector", lambda e: e.tensor_scalar(msk1[:, :], smallA[:, 0:8], smallB[:, 0:1], None, ALU.is_equal),
                 reads=[(smallA, "lg"), (smallB, "m1")], writes=[msk1])
            P.op("vector", lambda e: e.tensor_scalar(tmp8[:, :], msk1[:, :], -1e30, None, ALU.mult),
                 reads=[msk1], writes=[(tmp8, "a")])
            P.op("vector", lambda e: e.tensor_tensor(lmsk[:, :], smallA[:, 0:8], tmp8[:, :], ALU.add),
                 reads=[(smallA, "lg"), (tmp8, "a")], writes=[lmsk])
            P.op("vector", lambda e: e.reduce_max(smallB[:, 1:2], lmsk[:, :], axis=AX.X),
                 reads=[lmsk], writes=[(smallB, "m2")])
            P.op("vector", lambda e: e.tensor_scalar(msk2[:, :], lmsk[:, :], smallB[:, 1:2], None, ALU.is_equal),
                 reads=[lmsk, (smallB, "m2")], writes=[msk2])
            P.op("vector", lambda e: e.tensor_tensor(smallB[:, 2:3], smallB[:, 0:1], smallB[:, 1:2], ALU.subtract),
                 reads=[(smallB, "m1"), (smallB, "m2")], writes=[(smallB, "d")])
            P.op("scalar", lambda e: e.activation(smallB[:, 3:4], smallB[:, 2:3], AF.Sigmoid, bias=0.0, scale=1.0),
                 reads=[(smallB, "d")], writes=[(smallB, "w1")])
            P.op("vector", lambda e: e.tensor_scalar(smallB[:, 4:5], smallB[:, 3:4], -1.0, 1.0, ALU.mult, ALU.add),
                 reads=[(smallB, "w1")], writes=[(smallB, "w2")])
            P.op("vector", lambda e: e.tensor_scalar(tmp8[:, :], msk2[:, :], smallB[:, 4:5], None, ALU.mult),
                 reads=[msk2, (smallB, "w2")], writes=[(tmp8, "b")])
            P.op("vector", lambda e, tc=tc: e.scalar_tensor_tensor(
                comb_tm[:, tc, :], msk1[:, :], smallB[:, 3:4], tmp8[:, :], ALU.mult, ALU.add),
                reads=[msk1, (smallB, "w1"), (tmp8, "b")], writes=[(comb_tm, f"t{tc}")])
        if dbg and b == 0:
            P.dma("sync", lambda e: e.dma_start(out=dbg_comb[:], in_=comb_tm[:, :, :]),
                  reads=[(comb_tm, f"t{t_}") for t_ in range(16)])
        retag(SCRA, [f"v{mc}" for mc in range(8)] + [f"ga{tc}_{ft}" for tc in range(16) for ft in range(2)], tag="rtg1")
        retag(ARX, [f"on{ch}" for ch in range(8)] + [f"fw2_{k}" for k in range(16)], tag="rtw2")
        retag(ARW, [f"wo{k}" for k in range(8)] + [f"fw1_{k}" for k in range(8)], tag="rtw1")
        retag(ARS, ARSQK + ARSDK + [f"h1_{k}" for k in range(16)], tag="rth1")
        P.op("gpsimd", lambda e: e.memset(SCRA[:, 0:16384], 0.0),
             writes=[(SCRA, f"ga{tc}_{ft}") for tc in range(16) for ft in range(2)])
        for e_ in range(8):
            for k in range(8):
                P.dma("gpsimd", lambda e, k=k, e_=e_: e.dma_start(
                    out=w1e[:, k, :], in_=fw1_d[e_, k * 128:(k + 1) * 128, :]),
                    writes=[(ARW, f"fw1_{k}")], tag=f"fw1l{k}")
            for k in range(16):
                P.dma("gpsimd", lambda e, k=k, e_=e_: e.dma_start(
                    out=w2e[:, k, :], in_=fw2_d[e_, k * 128:(k + 1) * 128, :]),
                    writes=[(ARX, f"fw2_{k}")], tag=f"fw2l{k}")
            P.dma("sync", lambda e, e_=e_: e.dma_start(out=fb1c[:], in_=fb1_d[e_]),
                  writes=[fb1c], tag="fb1l")
            P.dma("gpsimd", lambda e, e_=e_: e.dma_start(
                out=fb2r[:, :], in_=fb2_d[0:1, e_ * 1024:(e_ + 1) * 1024]),
                writes=[fb2r], tag="fb2l")
            for tt in range(4):
                for mh in range(16):
                    ps = (ps0, ps1, ps4)[mh % 3]
                    for k in range(8):
                        mm(P, ps[:, :], w1e[:, k, mh * 128:(mh + 1) * 128],
                           xn1[:, k, tt * 512:(tt + 1) * 512],
                           reads=[(ARW, f"fw1_{k}"), xn1], wkey=(ps, "g"),
                           start=(k == 0), stop=(k == 7))
                    P.op("scalar", lambda e, ps=ps, mh=mh: e.activation(
                        h1f[:, mh, :], ps[:, :], AF.Relu, bias=fb1c[:, mh:mh + 1], scale=1.0),
                        reads=[(ps, "g"), fb1c], writes=[(ARS, f"h1_{mh}")])
                for tc2 in range(4):
                    tc = tt * 4 + tc2
                    for ft in range(2):
                        ps = (ps2, ps3, ps5)[(tc2 * 2 + ft) % 3]
                        for k in range(16):
                            mm(P, ps[:, :], h1f[:, k, tc2 * 128:(tc2 + 1) * 128],
                               w2e[:, k, ft * 512:(ft + 1) * 512],
                               reads=[(ARS, f"h1_{k}"), (ARX, f"fw2_{k}")], wkey=(ps, "g"),
                               start=(k == 0), stop=False)
                        mm(P, ps[:, :], C.onescol[0:1, 0:128], fb2r[0:1, ft * 512:(ft + 1) * 512],
                           reads=[C.onescol, fb2r], wkey=(ps, "g"), start=False, stop=True)
                        P.op("vector", lambda e, ps=ps, tc=tc, ft=ft, e_=e_: e.scalar_tensor_tensor(
                            g1acc[:, tc, ft * 512:(ft + 1) * 512], ps[:, :],
                            comb_tm[:, tc, e_:e_ + 1], g1acc[:, tc, ft * 512:(ft + 1) * 512],
                            ALU.mult, ALU.add),
                            reads=[(ps, "g"), (comb_tm, f"t{tc}"), (SCRA, f"ga{tc}_{ft}")],
                            writes=[(SCRA, f"ga{tc}_{ft}")])
        for tc in range(16):
            P.op("scalar", lambda e, tc=tc: e.activation(
                C.ln_xn[:, :], g1acc[:, tc, :], AF.Silu, bias=0.0, scale=1.0),
                reads=[(SCRA, f"ga{tc}_0"), (SCRA, f"ga{tc}_1")], writes=[C.ln_xn])
            P.dma("sync", lambda e, b=b, tc=tc: e.dma_start(
                out=qkst[:, 0:1024], in_=s1scr[b, tc * 128:(tc + 1) * 128, :]),
                reads=[(s1scr, f"{b}_{tc}")], writes=[(qkst, "s1l")], tag=f"s1l{tc}")
            P.dma("scalar", lambda e, b=b, tc=tc: e.dma_start(
                out=lnxt[:, :], in_=x1_d[b, tc * 128:(tc + 1) * 128, :]),
                writes=[lnxt], tag=f"x1l{tc}")
            P.op("vector", lambda e: e.tensor_tensor(sq_st[:, 0:1024], qkst[:, 0:1024], C.ln_xn[:, :], ALU.mult),
                 reads=[(qkst, "s1l"), C.ln_xn], writes=[(sq_st, "pr")])
            P.op("vector", lambda e: e.tensor_tensor(OUTF[:, :], lnxt[:, :], sq_st[:, 0:1024], ALU.add),
                 reads=[lnxt, (sq_st, "pr")], writes=[OUTF])
            P.dma("sync", lambda e, b=b, tc=tc: e.dma_start(
                out=out_d[b, tc * 128:(tc + 1) * 128, :], in_=OUTF[:, :]),
                reads=[OUTF], tag=f"outs{tc}")
    P.drain_dmas("sync")
    P.drain_dmas("gpsimd")
    P.drain_dmas("scalar")
    return C



def _consts():
    idn = np.eye(128, dtype=BF)
    onescol = np.ones((128, 128), dtype=BF)
    onesf = np.ones((128, 1), np.float32)
    return {"c_idn": idn, "c_ones": onescol, "c_onesf": onesf,
            "c_idnf": np.eye(128, dtype=np.float32)}


def _l0_inputs(inputs, b0):
    t = lambda a: np.ascontiguousarray(np.asarray(a).T)
    col8 = lambda a: np.ascontiguousarray(np.asarray(a).reshape(8, 128).T)
    d = {
        "x": np.ascontiguousarray(inputs["L0_attn_x"][b0:b0 + 2]),
        "l0a_wq": t(inputs["L0_attn_Wq"]), "l0a_wk": t(inputs["L0_attn_Wk"]),
        "l0a_wv": t(inputs["L0_attn_Wv"]), "l0a_wo": t(inputs["L0_attn_Wo"]),
        "l0a_bq": col8(inputs["L0_attn_bq"]), "l0a_bk": col8(inputs["L0_attn_bk"]),
        "l0a_bv": np.ascontiguousarray(inputs["L0_attn_bv"].reshape(1, 1024)),
        "l0a_bo": np.ascontiguousarray(inputs["L0_attn_bo"].reshape(1, 1024)),
        "l0f_wq": t(inputs["L0_ffn_Wq"]), "l0f_wk": t(inputs["L0_ffn_Wk"]),
        "l0f_bq": col8(inputs["L0_ffn_bq"]), "l0f_bk": col8(inputs["L0_ffn_bk"]),
        "l0f_ewq": np.ascontiguousarray(inputs["L0_ffn_eWq"].transpose(0, 2, 1)),
        "l0f_ewk": np.ascontiguousarray(inputs["L0_ffn_eWk"].transpose(0, 2, 1)),
        "l0f_ebq": np.ascontiguousarray(inputs["L0_ffn_ebq"].T),
        "l0f_ebk": np.ascontiguousarray(inputs["L0_ffn_ebk"].T),
        "l0f_wexp": np.ascontiguousarray(inputs["L0_ffn_Wexp"]),
        "l0f_bexp": np.ascontiguousarray(inputs["L0_ffn_bexp"].reshape(1, 8192)),
    }
    d.update(_consts())
    return d


def _l1_inputs(inputs, x1=None):
    cw = lambda a, g: np.ascontiguousarray(np.asarray(a).reshape(g, 128, -1).transpose(1, 0, 2).reshape(128, -1))
    sel = np.zeros((16, 2048), dtype=BF)
    for r in range(16):
        sel[r, r * 128:(r + 1) * 128] = 1
    mlow = -np.tril(np.ones((128, 128), np.float32), -1)
    mtriu = np.triu(np.ones((128, 128), np.float32), 0).astype(BF)
    mup = (-np.triu(np.ones((128, 128), np.float32), 1)).astype(BF)
    d = {
        "l1a_wq": np.ascontiguousarray(inputs["L1_attn_Wq"]),
        "l1a_wk": np.ascontiguousarray(inputs["L1_attn_Wk"]),
        "l1a_wv": np.ascontiguousarray(inputs["L1_attn_Wv"]),
        "l1a_wb": cw(inputs["L1_attn_Wb"], 8),
        "l1a_cq": cw(inputs["L1_attn_conv_q_w"], 8),
        "l1a_ck": cw(inputs["L1_attn_conv_k_w"], 8),
        "l1a_cv": cw(inputs["L1_attn_conv_v_w"], 8),
        "l1a_fs": cw(inputs["L1_attn_fir_short_w"].reshape(1024, 3), 8),
        "l1a_fl": cw(inputs["L1_attn_fir_long_w"].reshape(1024, 63), 8),
        "l1a_w1": np.ascontiguousarray(inputs["L1_attn_mlp_w1"][0:1024]),
        "l1a_w1c9": np.ascontiguousarray(np.pad(inputs["L1_attn_mlp_w1"][1024:1056], ((0, 96), (0, 0)))),
        "l1a_b1": np.ascontiguousarray(inputs["L1_attn_mlp_b1"].reshape(16, 128).T),
        "l1a_w2": cw(inputs["L1_attn_mlp_w2"], 16),
        "l1a_b2": np.ascontiguousarray(inputs["L1_attn_mlp_b2"].reshape(16, 1)),
        "l1a_glt": np.ascontiguousarray(inputs["L1_attn_gate_log_temp"].reshape(1, 4)),
        "l1a_onw": np.ascontiguousarray(inputs["L1_attn_o_norm_w"].reshape(2, 128).T),
        "l1a_wo": np.ascontiguousarray(inputs["L1_attn_Wo"]),
        "c_sel": sel, "c_mlow": mlow, "c_mtriu": mtriu, "c_mup": mup,
        "l1f_wg": cw(inputs["L1_ffn_Wg"], 8),
        "l1f_bg": np.ascontiguousarray(inputs["L1_ffn_bg"].reshape(8, 1)),
        "l1f_w1": np.ascontiguousarray(inputs["L1_ffn_W1"]),
        "l1f_b1": np.ascontiguousarray(inputs["L1_ffn_b1"].reshape(8, 16, 128).transpose(0, 2, 1)),
        "l1f_w2": np.ascontiguousarray(inputs["L1_ffn_W2"]),
        "l1f_b2": np.ascontiguousarray(inputs["L1_ffn_b2"].reshape(1, 8192)),
    }
    if x1 is not None:
        d["x1"] = np.ascontiguousarray(x1)
    d.update(_consts())
    return d


def build_l1_program(dbg=False):
    nc = bass.Bass("TRN2", target_bir_lowering=False, debug=False)
    P = Prog(nc)
    build_layer1(nc, P, dbg=dbg)
    P.finalize()
    return nc, P


def build_fused_program():
    nc = bass.Bass("TRN2", target_bir_lowering=False, debug=False)
    P = Prog(nc)
    C = setup_common(nc, P, need=("masks",))
    P.push_scope()
    build_layer0(nc, P, C=C, out_kind="Internal")
    P.pop_scope()
    P.layer_barrier()
    P.push_scope()
    build_layer1(nc, P, C=C)
    P.pop_scope()
    P.finalize()
    return nc, P


def build_l0_program(dbg=False):
    nc = bass.Bass("TRN2", target_bir_lowering=False, debug=False)
    P = Prog(nc)
    build_layer0(nc, P, dbg=dbg)
    P.finalize()
    return nc, P


def kernel(**inputs):
    nc, _ = build_fused_program()
    in_maps = []
    for c in range(8):
        m = _l0_inputs(inputs, 2 * c)
        m.update(_l1_inputs(inputs))
        in_maps.append(m)
    res = run_bass_kernel_spmd(nc, in_maps, core_ids=list(range(8)))
    return np.concatenate([r["out"] for r in res.results], axis=0)


# version 16 (final)
# speedup vs baseline: 1.1930x; 1.0038x over previous
import numpy as np
import ml_dtypes
import concourse.bass as bass
import concourse.mybir as mybir
from concourse.bass_utils import run_bass_kernel_spmd

F32 = mybir.dt.float32
BF16 = mybir.dt.bfloat16
AF = mybir.ActivationFunctionType
ALU = mybir.AluOpType
AX = mybir.AxisListType
BF = ml_dtypes.bfloat16

ENGINES = ("tensor", "vector", "scalar", "gpsimd", "sync")


def _regs(lst):
    out = []
    if lst is None:
        return out
    if isinstance(lst, tuple) and len(lst) == 2 and isinstance(lst[1], (str, int)):
        lst = [lst]
    elif not isinstance(lst, (list, tuple)):
        lst = [lst]
    for item in lst:
        if isinstance(item, tuple):
            t, k = item
        else:
            t, k = item, None
        if not hasattr(t, "name"):
            t = t.tensor
        out.append((t.name, k))
    return out


class Prog:
    def __init__(self, nc, dma_pool=40):
        self.nc = nc
        self.ops = {e: [] for e in ENGINES}
        self._ctx = []
        self.chain = {}
        self.count = {}
        for c in ENGINES:
            self._mk_chain(c)
        self.dma_sems_hw = [self._mk_chain(f"dqh{i}") for i in range(dma_pool)]
        self.dma_sems_sw = [self._mk_chain(f"dqs{i}") for i in range(16)]
        self._dma_rr = {"hw": 0, "sw": 0}
        self.state = {}
        self.tile_keys = {}
        self.covered = {e: {} for e in ENGINES}
        self.n_ops = 0
        self._scope = None

    def _mk_chain(self, name):
        cm = self.nc.semaphore(f"ch_{name}")
        self.chain[name] = cm.__enter__()
        self._ctx.append(cm)
        self.count[name] = 0
        return name

    def sbuf(self, name, shape, dtype):
        cm = self.nc.sbuf_tensor(name, shape, dtype)
        h = cm.__enter__()
        (self._scope if self._scope is not None else self._ctx).append(cm)
        return h

    def psum(self, name, shape, dtype=F32):
        cm = self.nc.psum_tensor(name, shape, dtype)
        h = cm.__enter__()
        (self._scope if self._scope is not None else self._ctx).append(cm)
        return h

    def push_scope(self):
        self._scope = []

    def pop_scope(self):
        for cm in reversed(self._scope):
            cm.__exit__(None, None, None)
        self._scope = None

    def layer_barrier(self):
        for e in ENGINES:
            for c, v in self.count.items():
                if v > 0 and self.covered[e].get(c, 0) < v and c != e:
                    sh = self.chain[c]
                    self.raw(e, lambda eng, sh=sh, v=v: eng.wait_ge(sh, v))
                self.covered[e][c] = max(self.covered[e].get(c, 0), self.count[c])
        self.state = {}
        self.tile_keys = {}

    def _st(self, rk):
        if rk not in self.state:
            self.state[rk] = {"w": None, "r": []}
            self.tile_keys.setdefault(rk[0], set()).add(rk[1])
        return self.state[rk]

    def _dep_read(self, rk):
        deps = []
        name, key = rk
        st = self._st(rk)
        if st["w"]:
            deps.append(st["w"])
        if key is not None:
            stw = self._st((name, None))
            if stw["w"]:
                deps.append(stw["w"])
        else:
            for k in self.tile_keys.get(name, ()):
                if k is None:
                    continue
                stk = self._st((name, k))
                if stk["w"]:
                    deps.append(stk["w"])
        return deps

    def _dep_write(self, rk):
        deps = []
        name, key = rk
        sts = [self._st(rk)]
        if key is not None:
            sts.append(self._st((name, None)))
        else:
            sts += [self._st((name, k)) for k in self.tile_keys.get(name, ())
                    if k is not None]
        for st in sts:
            if st["w"]:
                deps.append(st["w"])
            deps.extend(st["r"])
        return deps

    def _emit(self, eng, fn, rl, wl, mychain, inc_by, extra_waits=(), tag=None,
              strict=False):
        deps = []
        for rk in rl:
            deps += self._dep_read(rk)
        for wk in wl:
            deps += self._dep_write(wk)
        need = {}
        for (c, v) in deps:
            if eng == "tensor" and c == "tensor":
                continue
            if (not strict) and v <= self.covered[eng].get(c, 0):
                continue
            need[c] = max(need.get(c, 0), v)
        waits = list(extra_waits)
        for c, v in sorted(need.items()):
            waits.append((self.chain[c], v))
            self.covered[eng][c] = v
        self.count[mychain] += inc_by
        myval = self.count[mychain]
        mysem = self.chain[mychain]
        self.n_ops += 1

        def thunk(engine, waits=waits, fn=fn, mysem=mysem, inc_by=inc_by, tag=tag):
            for (s, v) in waits:
                engine.wait_ge(s, v)
            instr = fn(engine)
            assert instr is not None, f"op {tag} returned None"
            instr.then_inc(mysem, inc_by)
        self.ops[eng].append(thunk)

        me = (mychain, myval)
        for rk in rl:
            self._st(rk)["r"].append(me)
        for wk in wl:
            name, key = wk
            st = self._st(wk)
            st["w"] = me
            st["r"] = []
            if key is None:
                for k in list(self.tile_keys.get(name, ())):
                    if k is not None:
                        self.state[(name, k)] = {"w": me, "r": []}
        return myval

    def op(self, eng, fn, reads=None, writes=None, tag=None, strict=False):
        return self._emit(eng, fn, _regs(reads), _regs(writes), eng, 1, tag=tag,
                          strict=strict)

    def dma(self, eng, fn, reads=None, writes=None, tag=None):
        pool = "sw" if eng == "gpsimd" else "hw"
        sems = self.dma_sems_sw if pool == "sw" else self.dma_sems_hw
        c = sems[self._dma_rr[pool]]
        self._dma_rr[pool] = (self._dma_rr[pool] + 1) % len(sems)
        extra = []
        if self.count[c] > 0:
            v = self.count[c]
            if self.covered[eng].get(c, 0) < v:
                extra.append((self.chain[c], v))
                self.covered[eng][c] = v
        return self._emit(eng, fn, _regs(reads), _regs(writes), c, 16,
                          extra_waits=extra, tag=tag)

    def raw(self, eng, fn):
        self.ops[eng].append(fn)

    def drain_dmas(self, eng):
        for c in self.dma_sems_hw + self.dma_sems_sw:
            v = self.count[c]
            if v > 0 and self.covered[eng].get(c, 0) < v:
                s = self.chain[c]
                self.raw(eng, lambda e, s=s, v=v: e.wait_ge(s, v))
                self.covered[eng][c] = v

    def finalize(self):
        nc = self.nc
        with nc.Block() as block:
            @block.tensor
            def _(e):
                for f in self.ops["tensor"]:
                    f(e)

            @block.vector
            def _(e):
                for f in self.ops["vector"]:
                    f(e)

            @block.scalar
            def _(e):
                for f in self.ops["scalar"]:
                    f(e)

            @block.gpsimd
            def _(e):
                for f in self.ops["gpsimd"]:
                    f(e)

            @block.sync
            def _(e):
                for f in self.ops["sync"]:
                    f(e)
        for cm in reversed(self._ctx):
            cm.__exit__(None, None, None)
        self._ctx = []


class View:
    def __init__(self, ap, name):
        self._ap = ap
        self.name = name

    def __getitem__(self, idx):
        return self._ap[idx]



def mm(P, out_ap, lhsT_ap, rhs_ap, reads, wkey, start=True, stop=True, tag=None):
    P.op("tensor",
         lambda e: e.matmul(out_ap, lhsT_ap, rhs_ap, start=start, stop=stop,
                            skip_group_check=True),
         reads=reads, writes=[wkey], tag=tag)


def load_weight(P, C, dst, src_dram, nchunks, ncols, tag="w"):
    for k in range(nchunks):
        P.dma("gpsimd",
              lambda e, k=k: e.dma_start(out=dst[:, k, :],
                                         in_=src_dram[k * 128:(k + 1) * 128, :]),
              writes=[(dst, f"k{k}")], tag=f"{tag}{k}")


class Ctx:
    pass


def setup_common(nc, P, need):
    C = Ctx()
    C.nc, C.P = nc, P
    C.idn_d = nc.dram_tensor("c_idn", [128, 128], BF16, kind="ExternalInput").ap()
    C.idnf_d = nc.dram_tensor("c_idnf", [128, 128], F32, kind="ExternalInput").ap()
    C.onescol_d = nc.dram_tensor("c_ones", [128, 128], BF16, kind="ExternalInput").ap()
    C.onesf_d = nc.dram_tensor("c_onesf", [128, 1], F32, kind="ExternalInput").ap()
    C.idn = P.sbuf("idn", [128, 128], BF16)
    C.idnf = P.sbuf("idnf", [128, 128], F32)
    C.onescol = P.sbuf("onescol", [128, 128], BF16)
    C.onesf = P.sbuf("onesf", [128, 1], F32)
    P.dma("sync", lambda e: e.dma_start(out=C.idn[:], in_=C.idn_d[:]), writes=[C.idn])
    P.dma("sync", lambda e: e.dma_start(out=C.idnf[:], in_=C.idnf_d[:]), writes=[C.idnf])
    P.dma("sync", lambda e: e.dma_start(out=C.onescol[:], in_=C.onescol_d[:]), writes=[C.onescol])
    P.dma("sync", lambda e: e.dma_start(out=C.onesf[:], in_=C.onesf_d[:]), writes=[C.onesf])
    if "masks" in need:
        C.mlow_d = nc.dram_tensor("c_mlow", [128, 128], F32, kind="ExternalInput").ap()
        C.mtriu_d = nc.dram_tensor("c_mtriu", [128, 128], BF16, kind="ExternalInput").ap()
        C.mup_d = nc.dram_tensor("c_mup", [128, 128], BF16, kind="ExternalInput").ap()
        C.mlow = P.sbuf("mlow", [128, 128], F32)
        C.mtriu = P.sbuf("mtriu", [128, 128], BF16)
        C.mup = P.sbuf("mup", [128, 128], BF16)
        P.dma("sync", lambda e: e.dma_start(out=C.mlow[:], in_=C.mlow_d[:]), writes=[C.mlow])
        P.dma("sync", lambda e: e.dma_start(out=C.mtriu[:], in_=C.mtriu_d[:]), writes=[C.mtriu])
        P.dma("sync", lambda e: e.dma_start(out=C.mup[:], in_=C.mup_d[:]), writes=[C.mup])
    return C


def emit_ln(P, C, x_dram_b, xn_fm, ps_t, tc, tag):
    xt = C.ln_xt[tc % 2]
    P.dma("sync", lambda e: e.dma_start(out=xt[:], in_=x_dram_b[tc * 128:(tc + 1) * 128, :]),
          writes=[xt], tag=f"lnx{tag}")
    s1, nm, vv, t1, t2, rstd = (C.ln_small[:, i:i + 1] for i in range(6))
    sm = C.ln_small
    P.op("vector", lambda e: e.reduce_sum(s1, xt[:], axis=AX.X), reads=[xt], writes=[(sm, "s1")])
    P.op("vector", lambda e: e.tensor_scalar(nm, s1, -1.0 / 1024, None, ALU.mult),
         reads=[(sm, "s1")], writes=[(sm, "nm")])
    va, vb = C.ln_small[:, 6:7], C.ln_small[:, 7:8]
    pa, pb = C.ln_ps
    P.op("scalar", lambda e: e.activation(pa[:, :], xt[:, 0:512], AF.Square, bias=nm, scale=1.0,
                                          accum_out=va),
         reads=[xt, (sm, "nm")], writes=[(pa, "g"), (sm, "va")])
    P.op("scalar", lambda e: e.activation(pb[:, :], xt[:, 512:1024], AF.Square, bias=nm, scale=1.0,
                                          accum_out=vb),
         reads=[xt, (sm, "nm")], writes=[(pb, "g"), (sm, "vb")])
    P.op("vector", lambda e: e.tensor_tensor(vv, va, vb, ALU.add),
         reads=[(sm, "va"), (sm, "vb")], writes=[(sm, "vv")])
    P.op("vector", lambda e: e.tensor_scalar(t1, vv, 1.0 / 1024, 1e-5, ALU.mult, ALU.add),
         reads=[(sm, "vv")], writes=[(sm, "t1")])
    P.op("scalar", lambda e: e.activation(t2, t1, AF.Sqrt, bias=0.0, scale=1.0),
         reads=[(sm, "t1")], writes=[(sm, "t2")])
    P.op("vector", lambda e: e.reciprocal(rstd, t2), reads=[(sm, "t2")], writes=[(sm, "rstd")])
    P.op("vector", lambda e: e.tensor_scalar(C.ln_xn[:], xt[:], nm, rstd, ALU.add, ALU.mult),
         reads=[xt, (sm, "nm"), (sm, "rstd")], writes=[C.ln_xn])
    for c in range(8):
        pt = ps_t[c % 2]
        P.op("tensor", lambda e, c=c, pt=pt: e.transpose(pt[:, 0:128], C.ln_xn[:, c * 128:(c + 1) * 128], C.idn[:]),
             reads=[C.ln_xn, C.idn], writes=[(pt, "t")])
        P.op("vector", lambda e, c=c, pt=pt: e.tensor_copy(xn_fm[:, c, tc * 128:(tc + 1) * 128], pt[:, 0:128]),
             reads=[(pt, "t")], writes=[(xn_fm, f"c{c}_{tc}")])



def build_layer0(nc, P, C=None, x_kind="ExternalInput", out_kind="ExternalOutput", dbg=False):
    if C is None:
        C = setup_common(nc, P, need=())
    dt = nc.dram_tensor
    x_d = dt("x", [2, 2048, 1024], F32, kind=x_kind).ap()
    x1_d = dt("x1", [2, 2048, 1024], F32, kind=out_kind).ap()
    C.x1_d = x1_d
    s0_d = dt("s0scr", [2, 2048, 1024], BF16, kind="ExternalOutput" if dbg else "Internal").ap()
    if dbg:
        dbg_xn = dt("dbg_xn", [128, 8, 2048], BF16, kind="ExternalOutput").ap()
        dbg_scon = dt("dbg_scon", [128, 16, 8], F32, kind="ExternalOutput").ap()
        dbg_mask = dt("dbg_mask", [128, 16, 8], F32, kind="ExternalOutput").ap()
        dbg_g = dt("dbg_g", [128, 16384], BF16, kind="ExternalOutput").ap()
        dbg_qp = dt("dbg_qp", [128, 2048], BF16, kind="ExternalOutput").ap()
        dbg_kp = dt("dbg_kp", [128, 2048], BF16, kind="ExternalOutput").ap()
        dbg_pt = dt("dbg_pt", [128, 16, 512], BF16, kind="ExternalOutput").ap()
        dbg_ofm = dt("dbg_ofm", [128, 16384], BF16, kind="ExternalOutput").ap()
        dbg_wa = dt("dbg_wa", [128, 8, 1024], BF16, kind="ExternalOutput").ap()
        dbg_bo = dt("dbg_bo", [1, 1024], BF16, kind="ExternalOutput").ap()
        dbg_s0st = dt("dbg_s0st", [128, 1024], BF16, kind="ExternalOutput").ap()

    wq_d = dt("l0a_wq", [1024, 1024], F32, kind="ExternalInput").ap()
    wk_d = dt("l0a_wk", [1024, 1024], F32, kind="ExternalInput").ap()
    wv_d = dt("l0a_wv", [1024, 1024], F32, kind="ExternalInput").ap()
    wo_d = dt("l0a_wo", [1024, 1024], F32, kind="ExternalInput").ap()
    bq_d = dt("l0a_bq", [128, 8], F32, kind="ExternalInput").ap()
    bk_d = dt("l0a_bk", [128, 8], F32, kind="ExternalInput").ap()
    bv_d = dt("l0a_bv", [1, 1024], F32, kind="ExternalInput").ap()
    bo_d = dt("l0a_bo", [1, 1024], F32, kind="ExternalInput").ap()
    fwq_d = dt("l0f_wq", [1024, 1024], F32, kind="ExternalInput").ap()
    fwk_d = dt("l0f_wk", [1024, 1024], F32, kind="ExternalInput").ap()
    fbq_d = dt("l0f_bq", [128, 8], F32, kind="ExternalInput").ap()
    fbk_d = dt("l0f_bk", [128, 8], F32, kind="ExternalInput").ap()
    ewq_d = dt("l0f_ewq", [8, 128, 128], F32, kind="ExternalInput").ap()
    ewk_d = dt("l0f_ewk", [8, 128, 128], F32, kind="ExternalInput").ap()
    ebq_d = dt("l0f_ebq", [128, 8], F32, kind="ExternalInput").ap()
    ebk_d = dt("l0f_ebk", [128, 8], F32, kind="ExternalInput").ap()
    wexp_d = dt("l0f_wexp", [8, 1024, 1024], F32, kind="ExternalInput").ap()
    bexp_d = dt("l0f_bexp", [1, 8192], F32, kind="ExternalInput").ap()

    S = P.sbuf
    WA = S("WA", [128, 8, 1024], BF16)
    WB = S("WB", [128, 8, 1024], BF16)
    WC = S("WC", [128, 8, 1024], BF16)
    xn_fm = S("xn_fm", [128, 8, 2048], BF16)
    SCR1 = S("SCR1", [128, 16384], BF16)
    ofm = SCR1[:, :].rearrange("p (h t) -> p h t", h=8)
    g_acc = SCR1[:, :].rearrange("p (t f) -> p t f", t=16)
    SCR2 = S("SCR2", [128, 16384], BF16)
    qf_fm = SCR2[:, :].rearrange("p (h t) -> p h t", h=8)
    PT = SCR2[:, 0:8192].rearrange("p (j i) -> p j i", j=16)
    SCR3 = S("SCR3", [128, 16384], BF16)
    kf_fm = SCR3[:, :].rearrange("p (h t) -> p h t", h=8)
    qp = SCR3[:, 0:2048]
    kp = SCR3[:, 2048:4096]
    Vp = SCR3[:, 4096:6176].rearrange("p (t h f) -> p t h f", t=16, h=2)
    obt = SCR2[0:64, 8192:10240]
    dnr = SCR2[64:65, 10240:10752]
    dens = S("dens", [128, 512], F32)
    denb = S("denb", [64, 512], BF16)
    s0st = S("s0st", [128, 1024], BF16)
    bias8a = S("bias8a", [128, 8], F32)
    bias8b = S("bias8b", [128, 8], F32)
    bvrow = S("bvrow", [1, 1024], BF16)
    borow = bvrow
    fb_q = S("fb_q", [128, 8], F32)
    fb_k = S("fb_k", [128, 8], F32)
    ewq_s = S("ewq_s", [128, 8, 128], BF16)
    ewk_s = S("ewk_s", [128, 8, 128], BF16)
    ebq_s = S("ebq_s", [128, 8], F32)
    ebk_s = S("ebk_s", [128, 8], F32)
    qe = SCR1[:, 0:2048]
    ke = SCR1[:, 2048:4096]
    Pe = [SCR1[:, 4096:6144], SCR1[:, 6144:8192]]
    accw = S("accw", [128, 2048], F32)
    denp = S("denp", [128, 4], F32)
    dd1 = S("dd1", [128, 2], F32)
    scon = S("scon", [128, 16, 8], F32)
    maskt = S("maskt", [128, 16, 8], F32)
    mxt = S("mxt", [128, 1], F32)
    bexps = S("bexps", [1, 1024], BF16)
    _lnxt = S("ln_xt0", [128, 1024], F32)
    C.ln_xt = [_lnxt, _lnxt]
    C.ln_xn = S("ln_xn", [128, 1024], BF16)
    C.ln_small = S("ln_small", [128, 8], F32)
    C.ln_ps = None
    gs = C.ln_xn
    s0t = s0st
    xt2 = C.ln_xt[0]
    prt = accw[:, 0:1024]
    x1t = accw[:, 1024:2048]

    psAB = P.psum("psAB", [128, 1024])
    psCD = P.psum("psCD", [128, 1024])
    ps0 = View(psAB[:, 0:512], "v_ps0")
    ps1 = View(psAB[:, 512:1024], "v_ps1")
    ps2 = View(psCD[:, 0:512], "v_ps2")
    ps3 = View(psCD[:, 512:1024], "v_ps3")
    ps4 = P.psum("ps4", [128, 512])
    ps5 = P.psum("ps5", [128, 512])
    psT = [P.psum("psTa", [128, 1024], BF16), P.psum("psTb", [128, 1024], BF16)]
    C.ln_ps = (ps4, ps5)

    for t, d in ((bias8a, bq_d), (bias8b, bk_d), (fb_q, fbq_d), (fb_k, fbk_d),
                 (ebq_s, ebq_d), (ebk_s, ebk_d)):
        P.dma("sync", lambda e, t=t, d=d: e.dma_start(out=t[:], in_=d[:]), writes=[t])
    P.dma("gpsimd", lambda e: e.dma_start(out=bvrow[:], in_=bv_d[:]), writes=[bvrow])
    for e_ in range(8):
        P.dma("gpsimd", lambda e, e_=e_: e.dma_start(out=ewq_s[:, e_, :], in_=ewq_d[e_]),
              writes=[(ewq_s, f"e{e_}")])
        P.dma("gpsimd", lambda e, e_=e_: e.dma_start(out=ewk_s[:, e_, :], in_=ewk_d[e_]),
              writes=[(ewk_s, f"e{e_}")])
    SC = 1.0 / np.sqrt(128.0)
    P.op("vector", lambda e: e.tensor_scalar(ebq_s[:], ebq_s[:], SC, None, ALU.mult),
         reads=[ebq_s], writes=[ebq_s])

    for b in range(2):
        xb = x_d[b]
        load_weight(P, C, WA, wq_d, 8, 1024, tag=f"wq{b}")
        load_weight(P, C, WB, wk_d, 8, 1024, tag=f"wk{b}")
        load_weight(P, C, WC, wv_d, 8, 1024, tag=f"wv{b}")
        for tc in range(16):
            emit_ln(P, C, xb, xn_fm, psT, tc, tag=f"{b}_{tc}")
        if dbg and b == 0:
            P.dma("sync", lambda e: e.dma_start(out=dbg_xn[:], in_=xn_fm[:, :, :]),
                  reads=[xn_fm])
        P.op("gpsimd", lambda e: e.memset(SCR1[:, :], 0.0), writes=[(SCR1, None)])
        P.op("gpsimd", lambda e: e.memset(SCR2[:, :], 0.0), writes=[(SCR2, None)])
        P.op("gpsimd", lambda e: e.memset(SCR3[:, :], 0.0), writes=[(SCR3, None)])
        P.op("vector", lambda e: e.memset(Vp[:, :, :, 64:65], 1.0), writes=[(SCR3, "ones")])
        P.dma("gpsimd", lambda e: e.dma_start(out=bvrow[:], in_=bv_d[:]), writes=[bvrow], tag=f"bv{b}")

        for hp in range(8):
            for tt in range(4):
                ps = (ps0, ps1)[tt % 2]
                for k in range(8):
                    mm(P, ps[:, :], WA[:, k, hp * 128:(hp + 1) * 128],
                       xn_fm[:, k, tt * 512:(tt + 1) * 512],
                       reads=[(WA, f"k{k}"), xn_fm], wkey=(ps, "g"),
                       start=(k == 0), stop=(k == 7))
                P.op("vector", lambda e, ps=ps, tt=tt, hp=hp: e.tensor_scalar(
                    qp[:, tt * 512:(tt + 1) * 512], ps[:, :], bias8a[:, hp:hp + 1], 1.0 / 32,
                    ALU.add, ALU.mult),
                    reads=[(ps, "g"), bias8a], writes=[(SCR3, f"qt{tt}")])
                ps_ = (ps2, ps3)[tt % 2]
                for k in range(8):
                    mm(P, ps_[:, :], WB[:, k, hp * 128:(hp + 1) * 128],
                       xn_fm[:, k, tt * 512:(tt + 1) * 512],
                       reads=[(WB, f"k{k}"), xn_fm], wkey=(ps_, "g"),
                       start=(k == 0), stop=(k == 7))
                P.op("vector", lambda e, ps_=ps_, tt=tt, hp=hp: e.tensor_scalar(
                    kp[:, tt * 512:(tt + 1) * 512], ps_[:, :], bias8b[:, hp:hp + 1], None,
                    ALU.add),
                    reads=[(ps_, "g"), bias8b], writes=[(SCR3, f"kt{tt}")])
            for tc2 in range(16):
                ps = (ps0, ps1, ps2, ps3)[tc2 % 4]
                for k in range(8):
                    mm(P, ps[:, 0:128], xn_fm[:, k, tc2 * 128:(tc2 + 1) * 128],
                       WC[:, k, hp * 128:(hp + 1) * 128],
                       reads=[xn_fm, (WC, f"k{k}")], wkey=(ps, "g"),
                       start=(k == 0), stop=False)
                mm(P, ps[:, 0:128], C.onescol[0:1, 0:128],
                   bvrow[0:1, hp * 128:(hp + 1) * 128],
                   reads=[C.onescol, bvrow], wkey=(ps, "g"), start=False, stop=True)
                P.op("vector", lambda e, ps=ps, tc2=tc2: e.tensor_copy(
                    Vp[:, tc2, :, 0:64],
                    ps[:, 0:128].rearrange("p (h f) -> p h f", h=2)),
                    reads=[(ps, "g")], writes=[(SCR3, f"v{tc2}")])
            if dbg and b == 0 and hp == 0:
                P.dma("sync", lambda e: e.dma_start(out=dbg_qp[:], in_=qp[:, :]),
                      reads=[(SCR3, "qt0"), (SCR3, "qt1"), (SCR3, "qt2"), (SCR3, "qt3")])
                P.dma("sync", lambda e: e.dma_start(out=dbg_kp[:], in_=kp[:, :]),
                      reads=[(SCR3, "kt0"), (SCR3, "kt1"), (SCR3, "kt2"), (SCR3, "kt3")])
            for h01 in range(2):
                base = 64 * h01
                for it in range(4):
                    psO = ps4
                    for jc in range(16):
                        ps = (ps0, ps1, ps2, ps3)[jc % 4]
                        mm(P, ps[:, :], kp[base:base + 64, jc * 128:(jc + 1) * 128],
                           qp[base:base + 64, it * 512:(it + 1) * 512],
                           reads=[(SCR3, f"kt{jc // 4}"), (SCR3, f"qt{it}")], wkey=(ps, "g"))
                        if jc % 2 == 1:
                            pr = (psAB, psCD)[(jc % 4) // 2]
                            k0, k1 = ((ps0, "g"), (ps1, "g")) if (jc % 4) == 1 else ((ps2, "g"), (ps3, "g"))
                            P.op("scalar", lambda e, pr=pr, jc=jc: e.activation(
                                SCR2[:, (jc - 1) * 512:(jc + 1) * 512], pr[:, :],
                                AF.Exp, bias=0.0, scale=1.0),
                                reads=[k0, k1],
                                writes=[(SCR2, f"j{jc - 1}"), (SCR2, f"j{jc}")])
                    if dbg and b == 0 and hp == 0 and h01 == 0 and it == 0:
                        P.dma("sync", lambda e: e.dma_start(out=dbg_pt[:], in_=PT[:, :, :]),
                              reads=[(SCR2, f"j{j}") for j in range(16)])
                    for jc in range(16):
                        mm(P, psO[0:65, :], Vp[:, jc, h01, 0:65], PT[:, jc, :],
                           reads=[(SCR3, f"v{jc}"), (SCR3, "ones"), (SCR2, f"j{jc}")],
                           wkey=(psO, "g"), start=(jc == 0), stop=(jc == 15))
                    P.op("vector", lambda e, psO=psO: e.tensor_copy(dens[64:65, :], psO[64:65, :]),
                         reads=[(psO, "g")], writes=[(dens, "d")])
                    P.op("vector", lambda e: e.reciprocal(dens[64:65, :], dens[64:65, :]),
                         reads=[(dens, "d")], writes=[(dens, "r")])
                    P.op("vector", lambda e: e.tensor_copy(dnr, dens[64:65, :]),
                         reads=[(dens, "r")], writes=[(SCR2, "dnr")])
                    pb = ps5
                    mm(P, pb[0:64, :], C.onescol[64:65, 0:64], dnr,
                       reads=[C.onescol, (SCR2, "dnr")], wkey=(pb, "g"))
                    P.op("vector", lambda e, pb=pb: e.tensor_copy(denb[:, :], pb[0:64, :]),
                         reads=[(pb, "g")], writes=[denb])
                    if h01 == 0:
                        dest = ofm[0:64, hp, it * 512:(it + 1) * 512]
                        wk = (SCR1, f"oa{hp}_{it}")
                    else:
                        dest = obt[0:64, it * 512:(it + 1) * 512]
                        wk = (SCR2, f"obt{it}")
                    P.op("vector", lambda e, dest=dest, psO=psO: e.tensor_tensor(
                        dest, psO[0:64, :], denb[:, :], ALU.mult),
                        reads=[(psO, "g"), denb], writes=[wk])
                if h01 == 1:
                    P.dma("sync", lambda e, hp=hp: e.dma_start(
                        out=ofm[64:128, hp, :], in_=obt[:, :]),
                        reads=[(SCR2, "obt0"), (SCR2, "obt1"), (SCR2, "obt2"), (SCR2, "obt3")],
                        writes=[(SCR1, f"ob{hp}")], tag=f"obmv{hp}")
        if dbg and b == 0:
            P.dma("sync", lambda e: e.dma_start(out=dbg_ofm[:], in_=SCR1[:, :]),
                  reads=[(SCR1, None)])
        load_weight(P, C, WA, wo_d, 8, 1024, tag=f"wo{b}")
        P.dma("gpsimd", lambda e: e.dma_start(out=borow[:], in_=bo_d[:]), writes=[bvrow], tag=f"bo{b}")
        if dbg and b == 0:
            P.dma("sync", lambda e: e.dma_start(out=dbg_wa[:], in_=WA[:, :, :]),
                  reads=[(WA, f"k{k}") for k in range(8)])
            P.dma("sync", lambda e: e.dma_start(out=dbg_bo[:], in_=borow[:, :]), reads=[borow])
        for tc in range(16):
            for ft in range(2):
                ps = (ps0, ps1)[ft]
                for pr in range(8):
                    mm(P, ps[:, :], ofm[:, pr, tc * 128:(tc + 1) * 128],
                       WA[:, pr, ft * 512:(ft + 1) * 512],
                       reads=[(SCR1, f"oa{pr}_{tc // 4}"), (SCR1, f"ob{pr}"), (WA, f"k{pr}")],
                       wkey=(ps, "g"), start=(pr == 0), stop=False)
                mm(P, ps[:, :], C.onescol[0:1, 0:128], borow[0:1, ft * 512:(ft + 1) * 512],
                   reads=[C.onescol, borow], wkey=(ps, "g"), start=False, stop=True)
                P.op("vector", lambda e, ps=ps, ft=ft: e.tensor_copy(
                    s0st[:, ft * 512:(ft + 1) * 512], ps[:, :]),
                    reads=[(ps, "g")], writes=[(s0st, f"f{ft}")])
            P.dma("sync", lambda e, tc=tc, b=b: e.dma_start(
                out=s0_d[b, tc * 128:(tc + 1) * 128, :], in_=s0st[:, :]),
                reads=[s0st], writes=[(s0_d, f"{b}_{tc}")], tag=f"s0s{tc}")
            if dbg and b == 0 and tc == 0:
                P.dma("sync", lambda e: e.dma_start(out=dbg_s0st[:], in_=s0st[:, :]),
                      reads=[s0st])

        load_weight(P, C, WA, fwq_d, 8, 1024, tag=f"fwq{b}")
        load_weight(P, C, WB, fwk_d, 8, 1024, tag=f"fwk{b}")
        P.op("gpsimd", lambda e: e.memset(SCR1[:, :], 0.0), writes=[(SCR1, None)])
        P.op("gpsimd", lambda e: e.memset(SCR2[:, :], 0.0), writes=[(SCR2, None)])
        P.op("gpsimd", lambda e: e.memset(SCR3[:, :], 0.0), writes=[(SCR3, None)])
        for mc in range(8):
            for tt in range(4):
                ps = (ps0, ps1)[tt % 2]
                for k in range(8):
                    mm(P, ps[:, :], WA[:, k, mc * 128:(mc + 1) * 128],
                       xn_fm[:, k, tt * 512:(tt + 1) * 512],
                       reads=[(WA, f"k{k}"), xn_fm], wkey=(ps, "g"),
                       start=(k == 0), stop=(k == 7))
                P.op("vector", lambda e, ps=ps, mc=mc, tt=tt: e.tensor_scalar(
                    qf_fm[:, mc, tt * 512:(tt + 1) * 512], ps[:, :],
                    fb_q[:, mc:mc + 1], SC, ALU.add, ALU.mult),
                    reads=[(ps, "g"), fb_q], writes=[(SCR2, f"m{mc}")])
                ps_ = (ps2, ps3)[tt % 2]
                for k in range(8):
                    mm(P, ps_[:, :], WB[:, k, mc * 128:(mc + 1) * 128],
                       xn_fm[:, k, tt * 512:(tt + 1) * 512],
                       reads=[(WB, f"k{k}"), xn_fm], wkey=(ps_, "g"),
                       start=(k == 0), stop=(k == 7))
                P.op("vector", lambda e, ps_=ps_, mc=mc, tt=tt: e.tensor_scalar(
                    kf_fm[:, mc, tt * 512:(tt + 1) * 512], ps_[:, :],
                    fb_k[:, mc:mc + 1], None, ALU.add),
                    reads=[(ps_, "g"), fb_k], writes=[(SCR3, f"m{mc}")])
        for e_ in range(8):
            for h in range(8):
                for tt in range(4):
                    ps = (ps0, ps1)[tt % 2]
                    mm(P, ps[:, :], ewq_s[:, e_, :], qf_fm[:, h, tt * 512:(tt + 1) * 512],
                       reads=[(ewq_s, f"e{e_}"), (SCR2, f"m{h}")], wkey=(ps, "g"))
                    P.op("vector", lambda e, ps=ps, tt=tt, e_=e_: e.tensor_scalar(
                        qe[:, tt * 512:(tt + 1) * 512], ps[:, :], ebq_s[:, e_:e_ + 1],
                        None, ALU.add),
                        reads=[(ps, "g"), ebq_s], writes=[(SCR1, f"qet{tt}")])
                    ps_ = (ps2, ps3)[tt % 2]
                    mm(P, ps_[:, :], ewk_s[:, e_, :], kf_fm[:, h, tt * 512:(tt + 1) * 512],
                       reads=[(ewk_s, f"e{e_}"), (SCR3, f"m{h}")], wkey=(ps_, "g"))
                    P.op("vector", lambda e, ps_=ps_, tt=tt, e_=e_: e.tensor_scalar(
                        ke[:, tt * 512:(tt + 1) * 512], ps_[:, :], ebk_s[:, e_:e_ + 1],
                        None, ALU.add),
                        reads=[(ps_, "g"), ebk_s], writes=[(SCR1, f"ket{tt}")])
                for sc in range(16):
                    pe = Pe[sc % 2]
                    for tt2 in range(4):
                        ps = (ps0, ps1, ps2, ps3)[tt2]
                        mm(P, ps[:, :], qe[:, sc * 128:(sc + 1) * 128],
                           ke[:, tt2 * 512:(tt2 + 1) * 512],
                           reads=[(SCR1, f"qet{sc // 4}"), (SCR1, f"ket{tt2}")], wkey=(ps, "g"))
                        if tt2 % 2 == 1:
                            pr = (psAB, psCD)[tt2 // 2]
                            k0, k1 = ((ps0, "g"), (ps1, "g")) if tt2 == 1 else ((ps2, "g"), (ps3, "g"))
                            P.op("scalar", lambda e, pr=pr, pe=pe, tt2=tt2: e.activation(
                                pe[:, (tt2 - 1) * 512:(tt2 + 1) * 512], pr[:, :], AF.Exp,
                                bias=0.0, scale=1.0, accum_out=denp[:, tt2 // 2:tt2 // 2 + 1]),
                                reads=[k0, k1],
                                writes=[(SCR1, f"pe{sc % 2}"), (denp, f"t{tt2 // 2}")])
                    P.op("vector", lambda e: e.tensor_reduce(dd1[:, 0:1], denp[:, 0:2], AX.X, ALU.add),
                         reads=[(denp, None)], writes=[(dd1, "s")])
                    P.op("vector", lambda e: e.reciprocal(dd1[:, 1:2], dd1[:, 0:1]),
                         reads=[(dd1, "s")], writes=[(dd1, "r")])
                    first = (h == 0 and sc == 0)
                    if first:
                        P.op("vector", lambda e, pe=pe: e.tensor_scalar(
                            accw[:, :], pe[:, :], dd1[:, 1:2], None, ALU.mult),
                            reads=[(SCR1, f"pe{sc % 2}"), (dd1, "r")], writes=[(accw, None)])
                    else:
                        P.op("vector", lambda e, pe=pe: e.scalar_tensor_tensor(
                            accw[:, :], pe[:, :], dd1[:, 1:2], accw[:, :],
                            ALU.mult, ALU.add),
                            reads=[(SCR1, f"pe{sc % 2}"), (dd1, "r"), (accw, None)],
                            writes=[(accw, None)])
            for tc in range(16):
                P.op("tensor", lambda e, tc=tc: e.matmul(
                    ps5[:, 0:1], accw[:, tc * 128:(tc + 1) * 128], C.onesf[:, 0:1],
                    start=True, stop=True, skip_group_check=True),
                    reads=[(accw, None), C.onesf], writes=[(ps5, "g")])
                P.op("vector", lambda e, tc=tc, e_=e_: e.tensor_scalar(
                    scon[:, tc, e_:e_ + 1], ps5[:, 0:1], 1.0 / 16384.0, None, ALU.mult),
                    reads=[(ps5, "g")], writes=[(scon, f"t{tc}e{e_}")])
        for tc in range(16):
            P.op("vector", lambda e, tc=tc: e.tensor_reduce(
                mxt[:, 0:1], scon[:, tc, :], AX.X, ALU.max),
                reads=[(scon, None)], writes=[mxt])
            P.op("vector", lambda e, tc=tc: e.tensor_scalar(
                maskt[:, tc, :], scon[:, tc, :], mxt[:, 0:1], None, ALU.is_equal),
                reads=[(scon, None), mxt], writes=[(maskt, f"t{tc}")])
        if dbg and b == 0:
            P.dma("sync", lambda e: e.dma_start(out=dbg_scon[:], in_=scon[:, :, :]), reads=[(scon, None)])
            P.dma("sync", lambda e: e.dma_start(out=dbg_mask[:], in_=maskt[:, :, :]), reads=[(maskt, None)])
        P.op("gpsimd", lambda e: e.memset(SCR1[:, :], 0.0), writes=[(SCR1, None)])
        for e_ in range(8):
            W = (WA, WB)[e_ % 2]
            load_weight(P, C, W, wexp_d[e_], 8, 1024, tag=f"we{e_}")
            P.dma("gpsimd", lambda e, e_=e_: e.dma_start(
                out=bexps[:, :], in_=bexp_d[0:1, e_ * 1024:(e_ + 1) * 1024]),
                writes=[bexps])
            for tc in range(16):
                for ft in range(2):
                    ps = (ps0, ps1, ps2, ps3)[(tc * 2 + ft) % 4]
                    for k in range(8):
                        mm(P, ps[:, :], xn_fm[:, k, tc * 128:(tc + 1) * 128],
                           W[:, k, ft * 512:(ft + 1) * 512],
                           reads=[xn_fm, (W, f"k{k}")], wkey=(ps, "g"),
                           start=(k == 0), stop=False)
                    mm(P, ps[:, :], C.onescol[0:1, 0:128],
                       bexps[0:1, ft * 512:(ft + 1) * 512],
                       reads=[C.onescol, bexps], wkey=(ps, "g"), start=False, stop=True)
                    P.op("vector", lambda e, ps=ps, tc=tc, ft=ft, e_=e_: e.scalar_tensor_tensor(
                        g_acc[:, tc, ft * 512:(ft + 1) * 512], ps[:, :],
                        maskt[:, tc, e_:e_ + 1], g_acc[:, tc, ft * 512:(ft + 1) * 512],
                        ALU.mult, ALU.add),
                        reads=[(ps, "g"), (maskt, f"t{tc}"), (SCR1, f"g{tc}_{ft}")],
                        writes=[(SCR1, f"g{tc}_{ft}")])
        if dbg and b == 0:
            P.dma("sync", lambda e: e.dma_start(out=dbg_g[:], in_=SCR1[:, :]),
                  reads=[(SCR1, None)])
        for tc in range(16):
            P.op("scalar", lambda e, tc=tc: e.activation(
                gs[:, :], g_acc[:, tc, :], AF.Silu, bias=0.0, scale=1.0),
                reads=[(SCR1, f"g{tc}_0"), (SCR1, f"g{tc}_1")], writes=[gs])
            P.dma("sync", lambda e, tc=tc, b=b: e.dma_start(
                out=s0t[:, :], in_=s0_d[b, tc * 128:(tc + 1) * 128, :]),
                reads=[(s0_d, f"{b}_{tc}")], writes=[s0t], tag=f"s0l{tc}")
            P.dma("scalar", lambda e, tc=tc, xb=xb: e.dma_start(
                out=xt2[:, :], in_=xb[tc * 128:(tc + 1) * 128, :]),
                writes=[xt2], tag=f"xl{tc}")
            P.op("vector", lambda e: e.tensor_tensor(prt, s0t[:, :], gs[:, :], ALU.mult),
                 reads=[s0t, gs], writes=[(accw, "prt")])
            P.op("vector", lambda e: e.tensor_tensor(x1t, xt2[:, :], prt, ALU.add),
                 reads=[xt2, (accw, "prt")], writes=[(accw, "x1t")])
            P.dma("sync", lambda e, tc=tc, b=b: e.dma_start(
                out=x1_d[b, tc * 128:(tc + 1) * 128, :], in_=x1t),
                reads=[(accw, "x1t")], tag=f"x1s{tc}")
    P.drain_dmas("sync")
    P.drain_dmas("gpsimd")
    P.drain_dmas("scalar")
    return C




def build_layer1(nc, P, C=None, x_kind="ExternalInput", out_kind="ExternalOutput", dbg=False):
    if C is None or not hasattr(C, "mlow"):
        C = setup_common(nc, P, need=("masks",)) if C is None else C
        if not hasattr(C, "mlow"):
            raise RuntimeError("fused build needs masks in C")
    dt = nc.dram_tensor
    x1_d = getattr(C, "x1_d", None) if C is not None else None
    if x1_d is None:
        x1_d = dt("x1", [2, 2048, 1024], F32, kind=x_kind).ap()
    out_d = dt("out", [2, 2048, 1024], F32, kind=out_kind).ap()
    q1scr = dt("q1scr", [2, 8, 128, 2048], BF16, kind="Internal").ap()
    k1scr = dt("k1scr", [2, 8, 128, 2048], BF16, kind="Internal").ap()
    fsscr = dt("fsscr", [2, 8, 128, 2048], BF16, kind="ExternalOutput" if dbg else "Internal").ap()
    flscr = dt("flscr", [2, 8, 128, 2048], BF16, kind="ExternalOutput" if dbg else "Internal").ap()
    dlscr = dt("dlscr", [2, 8, 128, 2048], BF16, kind="ExternalOutput" if dbg else "Internal").ap()
    s1scr = dt("s1scr", [2, 2048, 1024], BF16, kind="ExternalOutput" if dbg else "Internal").ap()
    rsd_dram = dt("rsd_dram", [16, 128], F32, kind="Internal").ap()
    if dbg:
        dbg_w = dt("dbg_w", [128, 16, 16], F32, kind="ExternalOutput").ap()
        dbg_comb = dt("dbg_comb", [128, 16, 8], F32, kind="ExternalOutput").ap()
        dbg_stats = dt("dbg_stats", [128, 16, 32], F32, kind="ExternalOutput").ap()
        dbg_lg = dt("dbg_lg", [128, 16, 16], F32, kind="ExternalOutput").ap()
        dbg_h1 = dt("dbg_h1", [128, 512], BF16, kind="ExternalOutput").ap()
        dbg_s32 = dt("dbg_s32", [128, 2048], BF16, kind="ExternalOutput").ap()
        dbg_pre = dt("dbg_pre", [128, 512], F32, kind="ExternalOutput").ap()
        dbg_xn1 = dt("dbg_xn1", [128, 8, 2048], BF16, kind="ExternalOutput").ap()
        dbg_w1 = dt("dbg_w1", [128, 2048], BF16, kind="ExternalOutput").ap()
        dbg_on = dt("dbg_on", [128, 16384], BF16, kind="ExternalOutput").ap()
        dbg_wb = dt("dbg_wb", [128, 512], F32, kind="ExternalOutput").ap()
        dbg_rsd = dt("dbg_rsd", [1, 2048], BF16, kind="ExternalOutput").ap()
        dbg_wfm = dt("dbg_wfm", [16, 2048], BF16, kind="ExternalOutput").ap()
        dbg_sel = dt("dbg_sel", [16, 2048], BF16, kind="ExternalOutput").ap()
        dbg_lgf = dt("dbg_lgf", [16, 2048], BF16, kind="ExternalOutput").ap()
        dbg_rtb = dt("dbg_rtb", [128, 16], F32, kind="ExternalOutput").ap()

    wq_d = dt("l1a_wq", [1024, 1024], F32, kind="ExternalInput").ap()
    wk_d = dt("l1a_wk", [1024, 1024], F32, kind="ExternalInput").ap()
    wv_d = dt("l1a_wv", [1024, 1024], F32, kind="ExternalInput").ap()
    wb_d = dt("l1a_wb", [128, 32], F32, kind="ExternalInput").ap()
    cq_d = dt("l1a_cq", [128, 32], F32, kind="ExternalInput").ap()
    ck_d = dt("l1a_ck", [128, 32], F32, kind="ExternalInput").ap()
    cv_d = dt("l1a_cv", [128, 32], F32, kind="ExternalInput").ap()
    fs_d = dt("l1a_fs", [128, 24], F32, kind="ExternalInput").ap()
    fl_d = dt("l1a_fl", [128, 504], F32, kind="ExternalInput").ap()
    w1_d = dt("l1a_w1", [1024, 2048], F32, kind="ExternalInput").ap()
    w1c9_d = dt("l1a_w1c9", [128, 2048], F32, kind="ExternalInput").ap()
    b1c_d = dt("l1a_b1", [128, 16], F32, kind="ExternalInput").ap()
    w2_d = dt("l1a_w2", [128, 256], F32, kind="ExternalInput").ap()
    b2r_d = dt("l1a_b2", [16, 1], F32, kind="ExternalInput").ap()
    glt_d = dt("l1a_glt", [1, 4], F32, kind="ExternalInput").ap()
    onw_d = dt("l1a_onw", [128, 2], F32, kind="ExternalInput").ap()
    wo_d = dt("l1a_wo", [1024, 1024], F32, kind="ExternalInput").ap()
    sel_d = dt("c_sel", [16, 2048], BF16, kind="ExternalInput").ap()
    wg_d = dt("l1f_wg", [128, 64], F32, kind="ExternalInput").ap()
    bg_d = dt("l1f_bg", [8, 1], F32, kind="ExternalInput").ap()
    fw1_d = dt("l1f_w1", [8, 1024, 2048], F32, kind="ExternalInput").ap()
    fb1_d = dt("l1f_b1", [8, 128, 16], F32, kind="ExternalInput").ap()
    fw2_d = dt("l1f_w2", [8, 2048, 1024], F32, kind="ExternalInput").ap()
    fb2_d = dt("l1f_b2", [1, 8192], F32, kind="ExternalInput").ap()

    S = P.sbuf
    xn1 = S("xn1", [128, 8, 2048], BF16)
    SCRA = S("L1A", [128, 16896], BF16)
    ARW = S("ARW", [128, 16384], BF16)
    ARX = S("ARX", [128, 16384], BF16)
    ARS = S("ARS", [128, 8192], BF16)
    CV = S("CVf", [128, 2051], BF16)
    CA = S("CAf", [128, 2048], BF16)
    OUTF = S("OUTF", [128, 1024], F32)
    qkst = S("qkst", [128, 2048], BF16)
    beta_fm = S("beta_fm", [4, 2048], BF16)
    beta_tm = S("beta_tm", [128, 16, 4], F32)
    stats_tm = S("stats_tm", [128, 16, 32], F32)
    sq_st = S("sq_st", [128, 2048], BF16)
    stats32 = S("stats32", [128, 2048], BF16)
    logits_fm = S("logits_fm", [16, 2048], BF16)
    lg_tm = S("lg_tm", [128, 16, 16], F32)
    w_tm = S("w_tm", [128, 16, 16], F32)
    wfm16 = S("wfm16", [16, 2048], BF16)
    sel_sb = S("sel_sb", [16, 2048], BF16)
    wbst = S("wbst", [128, 512], BF16)
    diag_a = S("diag_a", [128, 128], BF16)
    diag_b = S("diag_b", [128, 128], BF16)
    smallA = S("smallA", [128, 16], F32)
    smallB = S("smallB", [128, 16], F32)
    nrm16 = S("nrm16", [128, 16], F32)
    rsd16 = S("rsd16", [16, 128], F32)
    glt_sb = S("glt_sb", [1, 8], F32)
    rt16 = S("rt16", [1, 24], F32)
    rt16b = S("rt16b", [1, 16], BF16)
    rtb = S("rtb", [128, 16], F32)
    b1c = S("b1c", [128, 16], F32)
    w2_sb = S("w2_sb", [128, 16, 16], BF16)
    b2r = S("b2r", [16, 1], F32)
    onw = S("onw", [128, 2], F32)
    wb_sb = S("wb_sb", [128, 8, 4], BF16)
    cw_sb = S("cw_sb", [128, 8, 12], F32)
    fsw = S("fsw", [128, 8, 3], F32)
    flw = S("flw", [128, 8, 63], F32)
    wg_sb = S("wg_sb", [128, 8, 8], BF16)
    bgc = S("bgc", [8, 1], F32)
    fb1c = S("fb1c", [128, 16], F32)
    fb2r = S("fb2r", [1, 1024], BF16)
    w1c9 = S("w1c9", [128, 2048], BF16)
    comb_tm = S("comb_tm", [128, 16, 8], F32)
    msk1 = S("msk1", [128, 8], F32)
    msk2 = S("msk2", [128, 8], F32)
    lmsk = S("lmsk", [128, 8], F32)
    tmp8 = S("tmp8", [128, 8], F32)
    lnxt = S("l1_lnxt", [128, 1024], F32)
    C.ln_xt = [lnxt, lnxt]
    C.ln_xn = CA[:, 0:1024]
    C.ln_small = S("l1_lnsm", [128, 8], F32)

    rsd_row = beta_fm[0:1, 0:2048]
    def v1s(mc):
        return SCRA[:, mc * 2112:(mc + 1) * 2112]
    g1acc = SCRA[:, 0:16384].rearrange("p (t f) -> p t f", t=16)
    q1h = ARS[:, 0:4096].rearrange("p (c t) -> p c t", c=2)
    k1h = ARS[:, 4096:8192].rearrange("p (c t) -> p c t", c=2)
    dfm = [ARS[:, 0:2048], ARS[:, 2048:4096]]
    h1f = ARS[:, 0:8192].rearrange("p (c t) -> p c t", c=16)
    kn_tm = ARW[:, 0:4096].rearrange("p (c d) -> p c d", c=16)
    kb_tm = ARW[:, 4096:8192].rearrange("p (c d) -> p c d", c=16)
    vb_tm = ARW[:, 8192:12288].rearrange("p (c d) -> p c d", c=16)
    TMPa = ARW[:, 12288:16384].rearrange("p (c d) -> p c d", c=16)
    Ssb = ARW[:, 12288:12800]
    u_sb = ARW[:, 12800:13056]
    u2_sb = ARW[:, 13056:13312]
    wfm_c = ARW[:, 13312:13568]
    Wv_ = [ARW[:, 0:8192].rearrange("p (c f) -> p c f", c=8),
           ARW[:, 8192:16384].rearrange("p (c f) -> p c f", c=8)]
    w1v = ARW[:, 0:16384].rearrange("p (c f) -> p c f", c=8)
    brs = ARW[:, 0:4096].rearrange("p (c t) -> p c t", c=2)
    brl = ARW[:, 4096:8192].rearrange("p (c t) -> p c t", c=2)
    brd = ARW[:, 8192:12288].rearrange("p (c t) -> p c t", c=2)
    brstg = ARW[:, 0:2048]
    wov = ARW[:, 0:8192].rearrange("p (c f) -> p c f", c=8)
    w1e = ARW[:, 0:16384].rearrange("p (c f) -> p c f", c=8)
    qn_fm = ARX[:, 0:4096].rearrange("p (c t) -> p c t", c=2)
    kn_fm = ARX[:, 4096:8192].rearrange("p (c t) -> p c t", c=2)
    kb_fm = ARX[:, 8192:12288].rearrange("p (c t) -> p c t", c=2)
    attnT = ARX[:, 12288:14336].rearrange("p (c t) -> p c t", c=16)
    CHA = ARX[:, 14336:16384]
    def chs(i):
        return CHA[:, i * 128:(i + 1) * 128]
    o_n = ARX[:, 0:16384].rearrange("p (c t) -> p c t", c=8)
    w2e = ARX[:, 0:16384].rearrange("p (c f) -> p c f", c=16)

    ps0 = P.psum("l1ps0", [128, 512])
    ps1 = P.psum("l1ps1", [128, 512])
    ps2 = P.psum("l1ps2", [128, 512])
    ps3 = P.psum("l1ps3", [128, 512])
    ps4 = P.psum("l1ps4", [128, 512])
    ps5 = P.psum("l1ps5", [128, 512])
    psS = P.psum("l1psS", [128, 512])
    psTt = P.psum("l1psT", [128, 1024], BF16)
    psT = [psTt, psTt]
    C.ln_ps = (ps4, ps5)

    def retag(tile, keys, tag="rt"):
        ap1 = tile[0:1, 0:1]
        P.op("vector", lambda e: e.memset(ap1, 0.0),
             writes=[(tile, k) for k in keys], tag=tag, strict=True)

    TMPK = [f"tmp{tc}_{dc}" for tc in range(16) for dc in range(2)]
    SCANK = ["Ssb", "u", "u2", "wfm0", "wfm1"]
    ARSQK = ["q0", "q1", "k0", "k1"]
    ARSDK = [f"d{dvc}_{c}" for dvc in range(2) for c in range(16)]

    P.dma("sync", lambda e: e.dma_start(out=b1c[:], in_=b1c_d[:]), writes=[b1c])
    P.dma("sync", lambda e: e.dma_start(out=b2r[:], in_=b2r_d[:]), writes=[b2r])
    P.dma("sync", lambda e: e.dma_start(out=onw[:], in_=onw_d[:]), writes=[onw])
    P.dma("sync", lambda e: e.dma_start(out=bgc[:], in_=bg_d[:]), writes=[bgc])
    P.dma("sync", lambda e: e.dma_start(out=glt_sb[:, 0:4], in_=glt_d[:]), writes=[(glt_sb, "l")])
    P.dma("gpsimd", lambda e: e.dma_start(out=wb_sb[:], in_=wb_d[:].rearrange("p (c f) -> p c f", c=8)), writes=[wb_sb])
    P.dma("sync", lambda e: e.dma_start(out=cw_sb[:, :, 0:4], in_=cq_d[:].rearrange("p (c f) -> p c f", c=8)), writes=[(cw_sb, "q")])
    P.dma("sync", lambda e: e.dma_start(out=cw_sb[:, :, 4:8], in_=ck_d[:].rearrange("p (c f) -> p c f", c=8)), writes=[(cw_sb, "k")])
    P.dma("sync", lambda e: e.dma_start(out=cw_sb[:, :, 8:12], in_=cv_d[:].rearrange("p (c f) -> p c f", c=8)), writes=[(cw_sb, "v")])
    P.dma("sync", lambda e: e.dma_start(out=fsw[:], in_=fs_d[:].rearrange("p (c f) -> p c f", c=8)), writes=[fsw])
    P.dma("sync", lambda e: e.dma_start(out=flw[:], in_=fl_d[:].rearrange("p (c f) -> p c f", c=8)), writes=[flw])
    P.dma("gpsimd", lambda e: e.dma_start(out=wg_sb[:], in_=wg_d[:].rearrange("p (c f) -> p c f", c=8)), writes=[wg_sb])
    P.dma("gpsimd", lambda e: e.dma_start(out=w2_sb[:], in_=w2_d[:].rearrange("p (c f) -> p c f", c=16)), writes=[w2_sb])
    P.dma("sync", lambda e: e.dma_start(out=sel_sb[:], in_=sel_d[:]), writes=[sel_sb])
    P.dma("gpsimd", lambda e: e.dma_start(out=w1c9[:], in_=w1c9_d[:]), writes=[w1c9])

    g_ = glt_sb
    P.op("scalar", lambda e: e.activation(g_[:, 4:8], g_[:, 0:4], AF.Abs, bias=0.0, scale=1.0),
         reads=[(g_, "l")], writes=[(g_, "na")])
    P.op("scalar", lambda e: e.activation(g_[:, 4:8], g_[:, 4:8], AF.Exp, bias=0.0, scale=-1.0),
         reads=[(g_, "na")], writes=[(g_, "ex")])
    P.op("scalar", lambda e: e.activation(g_[:, 4:8], g_[:, 4:8], AF.Ln, bias=1.0, scale=1.0),
         reads=[(g_, "ex")], writes=[(g_, "ln")])
    P.op("vector", lambda e: e.tensor_scalar_max(rt16[:, 0:4], g_[:, 0:4], 0.0),
         reads=[(g_, "l")], writes=[(rt16, "mx")])
    P.op("vector", lambda e: e.tensor_tensor(rt16[:, 0:4], rt16[:, 0:4], g_[:, 4:8], ALU.add),
         reads=[(rt16, "mx"), (g_, "ln")], writes=[(rt16, "sp")])
    P.op("vector", lambda e: e.tensor_scalar(rt16[:, 0:4], rt16[:, 0:4], 1.0, 1e-4, ALU.mult, ALU.add),
         reads=[(rt16, "sp")], writes=[(rt16, "t")])
    P.op("vector", lambda e: e.reciprocal(rt16[:, 4:8], rt16[:, 0:4]),
         reads=[(rt16, "t")], writes=[(rt16, "r")])
    P.op("vector", lambda e: e.tensor_copy(
        rt16b[:, :].rearrange("p (h w) -> p h w", h=4),
        rt16[:, 4:8].rearrange("p (h w) -> p h w", w=1).broadcast_to((1, 4, 4))),
        reads=[(rt16, "r")], writes=[rt16b])
    mm(P, ps5[:, 0:16], C.onescol[0:1, 0:128], rt16b[0:1, :],
       reads=[C.onescol, rt16b], wkey=(ps5, "g"))
    P.op("vector", lambda e: e.tensor_copy(rtb[:, :], ps5[:, 0:16]),
         reads=[(ps5, "g")], writes=[rtb])

    for b in range(2):
        for tc in range(16):
            emit_ln(P, C, x1_d[b], xn1, psT, tc, tag=f"l1ln{b}_{tc}")
        P.op("gpsimd", lambda e: e.memset(SCRA[:, :], 0.0), writes=[(SCRA, None)])
        P.op("gpsimd", lambda e: e.memset(ARW[:, :], 0.0), writes=[(ARW, None)])
        P.op("gpsimd", lambda e: e.memset(ARX[:, :], 0.0), writes=[(ARX, None)])
        P.op("gpsimd", lambda e: e.memset(ARS[:, :], 0.0), writes=[(ARS, None)])

        for pi, (w_d_, scr, cwoff) in enumerate(((wq_d, q1scr, 0), (wk_d, k1scr, 4), (wv_d, None, 8))):
            W = Wv_[pi % 2]
            for k in range(8):
                P.dma("gpsimd", lambda e, k=k, W=W, w_d_=w_d_: e.dma_start(
                    out=W[:, k, :], in_=w_d_[k * 128:(k + 1) * 128, :]),
                    writes=[(ARW, f"w{pi % 2}_{k}")], tag=f"l1w{pi}_{k}")
            for mc in range(8):
                for tt in range(4):
                    ps = (ps0, ps1)[tt % 2]
                    for k in range(8):
                        mm(P, ps[:, :], W[:, k, mc * 128:(mc + 1) * 128],
                           xn1[:, k, tt * 512:(tt + 1) * 512],
                           reads=[(ARW, f"w{pi % 2}_{k}"), xn1], wkey=(ps, "g"),
                           start=(k == 0), stop=(k == 7))
                    P.op("vector", lambda e, ps=ps, tt=tt: e.tensor_copy(
                        CV[:, 3 + tt * 512:3 + (tt + 1) * 512], ps[:, :]),
                        reads=[(ps, "g")], writes=[(CV, f"t{tt}")])
                P.op("vector", lambda e: e.memset(CV[:, 0:3], 0.0), writes=[(CV, "pad")])
                cvr = [(CV, f"t{t_}") for t_ in range(4)] + [(CV, "pad")]
                for tt in range(4):
                    ps = (ps2, ps3)[tt % 2]
                    for kk in range(4):
                        off = kk + tt * 512
                        dg = (diag_a, diag_b)[kk % 2]
                        dgk = ((diag_a, None) if kk % 2 == 0 else (diag_b, None))
                        P.op("vector", lambda e, dg=dg, mc=mc, kk=kk, co=cwoff: e.tensor_scalar(
                            dg[:, :], C.idn[:], cw_sb[:, mc, co + kk:co + kk + 1], None, ALU.mult),
                            reads=[C.idn, cw_sb], writes=[dgk], tag="dgc")
                        mm(P, ps[:, :], dg[:, :], CV[:, off:off + 512],
                           reads=[dgk] + cvr, wkey=(ps, "g"),
                           start=(kk == 0), stop=(kk == 3))
                    if pi == 2:
                        dst = v1s(mc)[:, 64 + tt * 512:64 + (tt + 1) * 512]
                        dk_ = (SCRA, f"v{mc}")
                    else:
                        dst = qkst[:, tt * 512:(tt + 1) * 512]
                        dk_ = (qkst, f"t{tt}")
                    P.op("scalar", lambda e, dst=dst, ps=ps: e.activation(dst, ps[:, :], AF.Silu, bias=0.0, scale=1.0),
                         reads=[(ps, "g")], writes=[dk_])
                if pi != 2:
                    P.dma("sync", lambda e, scr=scr, b=b, mc=mc: e.dma_start(
                        out=scr[b, mc], in_=qkst[:, :]),
                        reads=[(qkst, f"t{t_}") for t_ in range(4)],
                        writes=[(scr, f"{b}_{mc}")], tag=f"qk{pi}{mc}")
        for tt in range(4):
            ps = (ps2, ps3)[tt % 2]
            for k in range(8):
                mm(P, ps[0:4, :], wb_sb[:, k, :], xn1[:, k, tt * 512:(tt + 1) * 512],
                   reads=[wb_sb, xn1], wkey=(ps, "g"), start=(k == 0), stop=(k == 7))
            P.op("scalar", lambda e, ps=ps, tt=tt: e.activation(
                beta_fm[:, tt * 512:(tt + 1) * 512], ps[0:4, :], AF.Sigmoid, bias=0.0, scale=1.0),
                reads=[(ps, "g")], writes=[(beta_fm, f"t{tt}")])
        for tc in range(16):
            P.op("tensor", lambda e, tc=tc: e.transpose(
                psTt[:, 0:4], beta_fm[0:4, tc * 128:(tc + 1) * 128], C.idn[0:4, 0:4]),
                reads=[(beta_fm, f"t{tc // 4}"), C.idn], writes=[(psTt, "t")])
            P.op("vector", lambda e, tc=tc: e.tensor_copy(beta_tm[:, tc, :], psTt[:, 0:4]),
                 reads=[(psTt, "t")], writes=[(beta_tm, f"t{tc}")])

        for mc in range(8):
            for (taps, wt_, scr, keyp) in ((3, fsw, fsscr, "fs"), (63, flw, flscr, "fl")):
                pe_taps = taps if taps == 3 else 36
                for kk in range(pe_taps):
                    dg = (diag_a, diag_b)[kk % 2]
                    dgk = ((diag_a, None) if kk % 2 == 0 else (diag_b, None))
                    P.op("vector", lambda e, dg=dg, wt_=wt_, mc=mc, kk=kk: e.tensor_scalar(
                        dg[:, :], C.idn[:], wt_[:, mc, kk:kk + 1], None, ALU.mult),
                        reads=[C.idn, wt_], writes=[dgk], tag="dgb")
                    for tt in range(4):
                        ps = (ps0, ps1, ps2, ps3)[tt]
                        off = 64 - (taps - 1) + kk + tt * 512
                        mm(P, ps[:, :], dg[:, :], v1s(mc)[:, off:off + 512],
                           reads=[dgk, (SCRA, f"v{mc}")], wkey=(ps, "g"),
                           start=(kk == 0), stop=(kk == pe_taps - 1))
                for tt in range(4):
                    ps = (ps0, ps1, ps2, ps3)[tt]
                    if taps == 3:
                        P.op("vector", lambda e, ps=ps, tt=tt: e.tensor_copy(
                            qkst[:, tt * 512:(tt + 1) * 512], ps[:, :]),
                            reads=[(ps, "g")], writes=[(qkst, f"f{tt}")])
                        continue
                    psD = (psS, ps4)[tt % 2]
                    kD = (psD, "S" if tt % 2 == 0 else "g")
                    for j, kk in enumerate(range(pe_taps, taps)):
                        off = 64 - (taps - 1) + kk + tt * 512
                        vsl = v1s(mc)[:, off:off + 512]
                        wap = wt_[:, mc, kk:kk + 1]
                        if j == 0:
                            P.op("vector", lambda e, psD=psD, vsl=vsl, wap=wap: e.tensor_scalar(
                                psD[:, :], vsl, wap, None, ALU.mult),
                                reads=[(SCRA, f"v{mc}"), wt_], writes=[kD], tag="fdv0")
                        else:
                            P.op("vector", lambda e, psD=psD, vsl=vsl, wap=wap: e.scalar_tensor_tensor(
                                psD[:, :], vsl, wap, psD[:, :], ALU.mult, ALU.add),
                                reads=[(SCRA, f"v{mc}"), wt_, kD], writes=[kD], tag="fdv")
                    P.op("vector", lambda e, ps=ps, tt=tt: e.tensor_copy(
                        qkst[:, tt * 512:(tt + 1) * 512], ps[:, :]),
                        reads=[(ps, "g")], writes=[(qkst, f"f{tt}")])
                    P.op("vector", lambda e, psD=psD, tt=tt: e.tensor_tensor(
                        qkst[:, tt * 512:(tt + 1) * 512], qkst[:, tt * 512:(tt + 1) * 512],
                        psD[:, :], ALU.add),
                        reads=[(qkst, f"f{tt}"), kD], writes=[(qkst, f"f{tt}")])
                P.dma("sync", lambda e, scr=scr, b=b, mc=mc: e.dma_start(
                    out=scr[b, mc], in_=qkst[:, :]),
                    reads=[(qkst, f"f{t_}") for t_ in range(4)],
                    writes=[(scr, f"{b}_{mc}")], tag=f"{keyp}{mc}")
        for h in range(4):
            retag(ARS, ARSDK + ARSQK, tag=f"rtars{h}")
            P.dma("sync", lambda e, b=b, h=h: e.dma_start(out=q1h[:, 0, :], in_=q1scr[b, 2 * h]),
                  reads=[(q1scr, f"{b}_{2 * h}")], writes=[(ARS, "q0")])
            P.dma("sync", lambda e, b=b, h=h: e.dma_start(out=q1h[:, 1, :], in_=q1scr[b, 2 * h + 1]),
                  reads=[(q1scr, f"{b}_{2 * h + 1}")], writes=[(ARS, "q1")])
            P.dma("scalar", lambda e, b=b, h=h: e.dma_start(out=k1h[:, 0, :], in_=k1scr[b, 2 * h]),
                  reads=[(k1scr, f"{b}_{2 * h}")], writes=[(ARS, "k0")])
            P.dma("scalar", lambda e, b=b, h=h: e.dma_start(out=k1h[:, 1, :], in_=k1scr[b, 2 * h + 1]),
                  reads=[(k1scr, f"{b}_{2 * h + 1}")], writes=[(ARS, "k1")])
            retag(ARW, SCANK + TMPK + [f"vb{t}" for t in range(16)] + [f"kn{t}" for t in range(16)] + [f"kb{t}" for t in range(16)], tag=f"rtarw{h}")
            for tc in range(16):
                for dc in range(2):
                    P.op("tensor", lambda e, tc=tc, dc=dc, h=h: e.transpose(
                        psTt[:, 0:128], v1s(2 * h + dc)[:, 64 + tc * 128:64 + (tc + 1) * 128], C.idn[:]),
                        reads=[(SCRA, f"v{2 * h + dc}"), C.idn], writes=[(psTt, "t")])
                    P.op("vector", lambda e, tc=tc, dc=dc: e.tensor_copy(
                        TMPa[:, tc, dc * 128:(dc + 1) * 128], psTt[:, 0:128]),
                        reads=[(psTt, "t")], writes=[(ARW, f"tmp{tc}_{dc}")])
                P.op("vector", lambda e, tc=tc, h=h: e.tensor_scalar(
                    vb_tm[:, tc, :], TMPa[:, tc, :], beta_tm[:, tc, h:h + 1], None, ALU.mult),
                    reads=[(ARW, f"tmp{tc}_0"), (ARW, f"tmp{tc}_1"), (beta_tm, f"t{tc}")],
                    writes=[(ARW, f"vb{tc}")])
            for (which, srcv, sm) in (("k", k1h, smallA), ("q", q1h, smallB)):
                for tc in range(16):
                    for dc in range(2):
                        P.op("tensor", lambda e, tc=tc, dc=dc, srcv=srcv: e.transpose(
                            psTt[:, 0:128], srcv[:, dc, tc * 128:(tc + 1) * 128], C.idn[:]),
                            reads=[(ARS, f"{which}{dc}"), C.idn], writes=[(psTt, "t")])
                        P.op("vector", lambda e, tc=tc, dc=dc: e.tensor_copy(
                            TMPa[:, tc, dc * 128:(dc + 1) * 128], psTt[:, 0:128]),
                            reads=[(psTt, "t")], writes=[(ARW, f"tmp{tc}_{dc}")])
                    P.op("scalar", lambda e, tc=tc, sm=sm: e.activation(
                        ps4[:, 0:256], TMPa[:, tc, :], AF.Square, bias=0.0, scale=1.0,
                        accum_out=sm[:, 0:1]),
                        reads=[(ARW, f"tmp{tc}_0"), (ARW, f"tmp{tc}_1")],
                        writes=[(ps4, "g"), (sm, "n")])
                    P.op("vector", lambda e, sm=sm: e.tensor_scalar(sm[:, 1:2], sm[:, 0:1], 1.0, 1e-6, ALU.mult, ALU.add),
                         reads=[(sm, "n")], writes=[(sm, "n1")])
                    P.op("scalar", lambda e, sm=sm: e.activation(sm[:, 2:3], sm[:, 1:2], AF.Sqrt, bias=0.0, scale=1.0),
                         reads=[(sm, "n1")], writes=[(sm, "n2")])
                    P.op("vector", lambda e, sm=sm: e.reciprocal(sm[:, 3:4], sm[:, 2:3]),
                         reads=[(sm, "n2")], writes=[(sm, "n3")])
                    if which == "k":
                        P.op("vector", lambda e, tc=tc, sm=sm: e.tensor_scalar(
                            kn_tm[:, tc, :], TMPa[:, tc, :], sm[:, 3:4], None, ALU.mult),
                            reads=[(ARW, f"tmp{tc}_0"), (ARW, f"tmp{tc}_1"), (sm, "n3")],
                            writes=[(ARW, f"kn{tc}")])
                        P.op("vector", lambda e, tc=tc, h=h: e.tensor_scalar(
                            kb_tm[:, tc, :], kn_tm[:, tc, :], beta_tm[:, tc, h:h + 1], None, ALU.mult),
                            reads=[(ARW, f"kn{tc}"), (beta_tm, f"t{tc}")], writes=[(ARW, f"kb{tc}")])
                    else:
                        P.op("vector", lambda e, tc=tc, sm=sm: e.tensor_scalar(
                            TMPa[:, tc, :], TMPa[:, tc, :], sm[:, 3:4], None, ALU.mult),
                            reads=[(ARW, f"tmp{tc}_0"), (ARW, f"tmp{tc}_1"), (sm, "n3")],
                            writes=[(ARW, f"tmp{tc}_0"), (ARW, f"tmp{tc}_1")])
                if which == "k":
                    for (src, dstv, kp_, srckeys) in (
                            (kn_tm, kn_fm, "knf", "kn"), (kb_tm, kb_fm, "kbf", "kb")):
                        for tc in range(16):
                            for dc in range(2):
                                P.op("tensor", lambda e, tc=tc, dc=dc, src=src: e.transpose(
                                    psTt[:, 0:128], src[:, tc, dc * 128:(dc + 1) * 128], C.idn[:]),
                                    reads=[(ARW, f"{srckeys}{tc}"), C.idn], writes=[(psTt, "t")])
                                P.op("vector", lambda e, tc=tc, dc=dc, dstv=dstv, kp_=kp_: e.tensor_copy(
                                    dstv[:, dc, tc * 128:(tc + 1) * 128], psTt[:, 0:128]),
                                    reads=[(psTt, "t")], writes=[(ARX, f"{kp_}{dc}_{tc}")])
                else:
                    for tc in range(16):
                        for dc in range(2):
                            P.op("tensor", lambda e, tc=tc, dc=dc: e.transpose(
                                psTt[:, 0:128], TMPa[:, tc, dc * 128:(dc + 1) * 128], C.idn[:]),
                                reads=[(ARW, f"tmp{tc}_0"), (ARW, f"tmp{tc}_1"), C.idn],
                                writes=[(psTt, "t")])
                            P.op("vector", lambda e, tc=tc, dc=dc: e.tensor_copy(
                                qn_fm[:, dc, tc * 128:(tc + 1) * 128], psTt[:, 0:128]),
                                reads=[(psTt, "t")], writes=[(ARX, f"qn{dc}_{tc}")])
            for c in range(16):
                ps = (ps2, ps3)[c % 2]
                for dc in range(2):
                    mm(P, ps[:, 0:128], kn_fm[:, dc, c * 128:(c + 1) * 128],
                       qn_fm[:, dc, c * 128:(c + 1) * 128],
                       reads=[(ARX, f"knf{dc}_{c}"), (ARX, f"qn{dc}_{c}")], wkey=(ps, "g"),
                       start=(dc == 0), stop=(dc == 1))
                P.op("vector", lambda e, ps=ps, c=c: e.tensor_tensor(
                    attnT[:, c, :], ps[:, 0:128], C.mtriu[:], ALU.mult),
                    reads=[(ps, "g"), C.mtriu], writes=[(ARX, f"at{c}")])
            retag(ARW, TMPK + SCANK, tag=f"rtscan{h}")
            retag(ARS, ARSQK + ARSDK, tag=f"rtsdf{h}")
            P.op("vector", lambda e: e.memset(Ssb[:], 0.0), writes=[(ARW, "Ssb")])
            for c in range(16):
                p_ = c % 2
                N_, Q_, qa_, qb_, na_, nb_, R_ = [chs(sl + 8 * p_) for sl in range(7)]
                kN, kQ, kqa, kqb, kna, knb, kR = [
                    f"ch{s}{p_}" for s in ("N", "Q", "qa", "qb", "na", "nb", "R")]
                psN = (ps2, ps3)[c % 2]
                psQ_ = (ps0, ps1)[c % 2]
                for dc in range(2):
                    mm(P, psN[:, 0:128], kb_fm[:, dc, c * 128:(c + 1) * 128],
                       kn_fm[:, dc, c * 128:(c + 1) * 128],
                       reads=[(ARX, f"kbf{dc}_{c}"), (ARX, f"knf{dc}_{c}")], wkey=(psN, "g"),
                       start=(dc == 0), stop=(dc == 1))
                for dc in range(2):
                    mm(P, psQ_[:, 0:128], kn_fm[:, dc, c * 128:(c + 1) * 128],
                       kb_fm[:, dc, c * 128:(c + 1) * 128],
                       reads=[(ARX, f"knf{dc}_{c}"), (ARX, f"kbf{dc}_{c}")], wkey=(psQ_, "g"),
                       start=(dc == 0), stop=(dc == 1))
                P.op("vector", lambda e, psN=psN, N_=N_: e.tensor_tensor(N_, psN[:, 0:128], C.mlow[:], ALU.mult),
                     reads=[(psN, "g"), C.mlow], writes=[(ARX, kN)])
                P.op("vector", lambda e, psQ_=psQ_, Q_=Q_: e.tensor_tensor(Q_, psQ_[:, 0:128], C.mup[:], ALU.mult),
                     reads=[(psQ_, "g"), C.mup], writes=[(ARX, kQ)])
                P.op("vector", lambda e, Q_=Q_, R_=R_: e.tensor_tensor(R_, Q_, C.idn[:], ALU.add),
                     reads=[(ARX, kQ), C.idn], writes=[(ARX, kR)])
                curNk, curQk, curNap, curQap = kN, kQ, N_, Q_
                for i in range(6):
                    psA = (ps2, ps3)[(c + i + 1) % 2]
                    psB = (ps0, ps1)[(c + i + 1) % 2]
                    mm(P, psA[:, 0:128], curNap, curQap,
                       reads=[(ARX, curNk), (ARX, curQk)], wkey=(psA, "g"))
                    mm(P, psB[:, 0:128], curQap, curNap,
                       reads=[(ARX, curQk), (ARX, curNk)], wkey=(psB, "g"))
                    newQ, kq = ((qa_, kqa), (qb_, kqb))[i % 2]
                    newN, kn = ((na_, kna), (nb_, knb))[i % 2]
                    P.op("vector", lambda e, psA=psA, newQ=newQ: e.tensor_copy(newQ, psA[:, 0:128]),
                         reads=[(psA, "g")], writes=[(ARX, kq)])
                    P.op("scalar", lambda e, psB=psB, newN=newN: e.copy(newN, psB[:, 0:128]),
                         reads=[(psB, "g")], writes=[(ARX, kn)])
                    mm(P, psA[:, 0:128], newN, R_,
                       reads=[(ARX, kn), (ARX, kR)], wkey=(psA, "g"),
                       start=True, stop=False)
                    mm(P, psA[:, 0:128], C.idn[:], R_,
                       reads=[C.idn, (ARX, kR)], wkey=(psA, "g"), start=False, stop=True)
                    P.op("vector", lambda e, psA=psA, R_=R_: e.tensor_copy(R_, psA[:, 0:128]),
                         reads=[(psA, "g")], writes=[(ARX, kR)])
                    curNk, curQk, curNap, curQap = kn, kq, newN, newQ
                mm(P, ps0[:, 0:256], R_, vb_tm[:, c, :],
                   reads=[(ARX, kR), (ARW, f"vb{c}")], wkey=(ps0, "g"))
                P.op("vector", lambda e: e.tensor_copy(u_sb[:], ps0[:, 0:256]),
                     reads=[(ps0, "g")], writes=[(ARW, "u")])
                for dc in range(2):
                    mm(P, ps1[:, 0:128], kb_tm[:, c, dc * 128:(dc + 1) * 128], R_,
                       reads=[(ARW, f"kb{c}"), (ARX, kR)], wkey=(ps1, "g"))
                    P.op("vector", lambda e, dc=dc: e.tensor_copy(
                        wfm_c[:, dc * 128:(dc + 1) * 128], ps1[:, 0:128]),
                        reads=[(ps1, "g")], writes=[(ARW, f"wfm{dc}")])
                for dc in range(2):
                    mm(P, ps0[:, 0:256], wfm_c[:, dc * 128:(dc + 1) * 128],
                       Ssb[:, dc * 256:(dc + 1) * 256],
                       reads=[(ARW, f"wfm{dc}"), (ARW, "Ssb")], wkey=(ps0, "g"),
                       start=(dc == 0), stop=(dc == 1))
                P.op("vector", lambda e: e.tensor_tensor(u2_sb[:], u_sb[:], ps0[:, 0:256], ALU.subtract),
                     reads=[(ARW, "u"), (ps0, "g")], writes=[(ARW, "u2")])
                for dvc in range(2):
                    ps_ = (ps2, ps3)[dvc]
                    for dc in range(2):
                        mm(P, ps_[:, 0:128], Ssb[:, dc * 256 + dvc * 128:dc * 256 + (dvc + 1) * 128],
                           qn_fm[:, dc, c * 128:(c + 1) * 128],
                           reads=[(ARW, "Ssb"), (ARX, f"qn{dc}_{c}")], wkey=(ps_, "g"),
                           start=(dc == 0), stop=False)
                    mm(P, ps_[:, 0:128], u2_sb[:, dvc * 128:(dvc + 1) * 128], attnT[:, c, :],
                       reads=[(ARW, "u2"), (ARX, f"at{c}")], wkey=(ps_, "g"),
                       start=False, stop=True)
                    P.op("vector", lambda e, ps_=ps_, c=c, dvc=dvc: e.tensor_copy(
                        dfm[dvc][:, c * 128:(c + 1) * 128], ps_[:, 0:128]),
                        reads=[(ps_, "g")], writes=[(ARS, f"d{dvc}_{c}")])
                for dc in range(2):
                    mm(P, psS[:, dc * 256:(dc + 1) * 256], kn_tm[:, c, dc * 128:(dc + 1) * 128],
                       u2_sb[:, 0:256],
                       reads=[(ARW, f"kn{c}"), (ARW, "u2")], wkey=(psS, "S"),
                       start=(c == 0 and dc == 0), stop=(c == 15 and dc == 1))
                P.op("vector", lambda e: e.tensor_copy(Ssb[:], psS[:, :]),
                     reads=[(psS, "S")], writes=[(ARW, "Ssb")])
            for dvc in range(2):
                P.dma("sync", lambda e, b=b, h=h, dvc=dvc: e.dma_start(
                    out=dlscr[b, 2 * h + dvc], in_=dfm[dvc][:, :]),
                    reads=[(ARS, f"d{dvc}_{c_}") for c_ in range(16)],
                    writes=[(dlscr, f"{b}_{2 * h + dvc}")], tag=f"dl{h}{dvc}")

        retag(ARW, [f"w{i}_{k}" for i in range(2) for k in range(8)] + ["brstg"] + [f"g1w{k}" for k in range(8)], tag="rtstat")
        for br_i, scr in enumerate((fsscr, flscr, dlscr, None)):
            for h in range(4):
                psA, psB = (ps0, ps1) if (h % 2 == 0) else (ps2, ps3)
                for dc in range(2):
                    ch_ = 2 * h + dc
                    if scr is not None:
                        P.dma("sync", lambda e, scr=scr, b=b, ch_=ch_: e.dma_start(
                            out=brstg[:], in_=scr[b, ch_]),
                            reads=[(scr, f"{b}_{ch_}")], writes=[(ARW, "brstg")], tag="stl")
                        src_ap = brstg
                        src_k = (ARW, "brstg")
                    else:
                        src_ap = v1s(ch_)[:, 64:2112]
                        src_k = (SCRA, f"v{ch_}")
                    P.op("scalar", lambda e, src_ap=src_ap: e.activation(
                        sq_st[:, :], src_ap, AF.Square, bias=0.0, scale=1.0),
                        reads=[src_k], writes=[(sq_st, None)])
                    for tc in range(16):
                        mm(P, psA[:, tc:tc + 1], src_ap[:, tc * 128:(tc + 1) * 128],
                           C.onescol[:, 0:1],
                           reads=[src_k, C.onescol], wkey=(psA, "g"),
                           start=(dc == 0 and tc == 0), stop=(dc == 1 and tc == 15))
                        mm(P, psB[:, tc:tc + 1], sq_st[:, tc * 128:(tc + 1) * 128],
                           C.onescol[:, 0:1],
                           reads=[(sq_st, None), C.onescol], wkey=(psB, "g"),
                           start=(dc == 0 and tc == 0), stop=(dc == 1 and tc == 15))
                cm = br_i * 8 + h
                cs = br_i * 8 + 4 + h
                P.op("vector", lambda e, psA=psA, cm=cm: e.tensor_scalar(
                    stats_tm[:, :, cm:cm + 1], psA[:, 0:16].rearrange("p (a one) -> p a one", one=1),
                    1.0 / 256, None, ALU.mult),
                    reads=[(psA, "g")], writes=[(stats_tm, f"c{cm}")])
                P.op("vector", lambda e, psB=psB, cs=cs: e.tensor_scalar(
                    stats_tm[:, :, cs:cs + 1], psB[:, 0:16].rearrange("p (a one) -> p a one", one=1),
                    1.0 / 256, None, ALU.mult),
                    reads=[(psB, "g")], writes=[(stats_tm, f"c{cs}")])
        STK = [(stats_tm, f"c{i}") for i in range(32)]
        P.op("gpsimd", lambda e: e.memset(stats32[:, :], 0.0),
             writes=[(stats32, f"t{t_}") for t_ in range(16)] + [(stats32, "z")])
        for tc in range(16):
            mview = stats_tm[:, tc, :].rearrange("p (br k) -> p br k", k=8)[:, :, 0:4]
            svview = stats_tm[:, tc, :].rearrange("p (br k) -> p br k", k=8)[:, :, 4:8]
            tview = smallA[:, 0:16].rearrange("p (br k) -> p br k", k=4)
            P.op("vector", lambda e, mview=mview, tview=tview: e.tensor_tensor(
                tview, mview, mview, ALU.mult), reads=STK, writes=[(smallA, "m2")])
            P.op("vector", lambda e, svview=svview, tview=tview: e.tensor_tensor(
                svview, svview, tview, ALU.subtract),
                reads=STK + [(smallA, "m2")], writes=STK)
            P.op("vector", lambda e, svview=svview: e.tensor_scalar_max(
                svview, svview, 1e-6), reads=STK, writes=STK)
            P.op("scalar", lambda e, svview=svview: e.activation(
                svview, svview, AF.Sqrt, bias=0.0, scale=1.0), reads=STK, writes=STK)
            P.op("tensor", lambda e, tc=tc: e.transpose(
                ps5[0:32, 0:128], stats_tm[:, tc, :], C.idnf[:]),
                reads=STK, writes=[(ps5, "g")])
            P.op("vector", lambda e, tc=tc: e.tensor_copy(
                stats32[0:32, tc * 128:(tc + 1) * 128], ps5[0:32, 0:128]),
                reads=[(ps5, "g")], writes=[(stats32, f"t{tc}")])
        if dbg and b == 0:
            P.dma("sync", lambda e: e.dma_start(out=dbg_stats[:], in_=stats_tm[:, :, :]),
                  reads=STK)

        if dbg and b == 0:
            P.dma("sync", lambda e: e.dma_start(out=dbg_xn1[:], in_=xn1[:, :, :]), reads=[xn1])
        retag(ARW, ["brstg"] + [f"g1w{k}" for k in range(8)], tag="rtgate")
        h1g = qkst
        retag(qkst, [None, "s0", "s1", "s1l"] + [f"t{t_}" for t_ in range(4)], tag="rth1g")
        for k in range(8):
            P.dma("gpsimd", lambda e, k=k: e.dma_start(
                out=w1v[:, k, :], in_=w1_d[k * 128:(k + 1) * 128, :]),
                writes=[(ARW, f"g1w{k}")], tag=f"w1l{k}")
        if dbg and b == 0:
            P.dma("sync", lambda e: e.dma_start(out=dbg_w1[:], in_=w1v[:, 0, :]),
                  reads=[(ARW, "g1w0")])
        for mh in range(16):
            for tt in range(4):
                ps = (ps0, ps1, ps4, ps2)[tt]
                for k in range(8):
                    mm(P, ps[:, :], w1v[:, k, mh * 128:(mh + 1) * 128],
                       xn1[:, k, tt * 512:(tt + 1) * 512],
                       reads=[(ARW, f"g1w{k}"), xn1], wkey=(ps, "g"),
                       start=(k == 0), stop=False)
                mm(P, ps[:, :], w1c9[:, mh * 128:(mh + 1) * 128],
                   stats32[:, tt * 512:(tt + 1) * 512],
                   reads=[w1c9, (stats32, "z")] + [(stats32, f"t{t_}") for t_ in range(4 * tt, 4 * tt + 4)],
                   wkey=(ps, "g"), start=False, stop=True)
                if dbg and b == 0 and mh == 0 and tt == 0:
                    P.op("vector", lambda e, ps=ps: e.tensor_copy(OUTF[:, 0:512], ps[:, 0:512]),
                         reads=[(ps, "g")], writes=[OUTF])
                    P.dma("sync", lambda e: e.dma_start(out=dbg_pre[:], in_=OUTF[:, 0:512]),
                          reads=[OUTF])
                P.op("scalar", lambda e, ps=ps, tt=tt, mh=mh: e.activation(
                    h1g[:, tt * 512:(tt + 1) * 512], ps[:, :], AF.Gelu, bias=b1c[:, mh:mh + 1], scale=1.0),
                    reads=[(ps, "g"), b1c], writes=[(qkst, f"t{tt}")])
                if dbg and b == 0 and mh == 0 and tt == 0:
                    P.dma("sync", lambda e: e.dma_start(out=dbg_h1[:], in_=h1g[:, 0:512]),
                          reads=[(qkst, "t0")])
                mm(P, ps5[0:16, 0:512], w2_sb[:, mh, :], h1g[:, tt * 512:(tt + 1) * 512],
                   reads=[w2_sb, (qkst, f"t{tt}")], wkey=(ps5, "g"))
                if mh == 0:
                    P.op("vector", lambda e, tt=tt: e.tensor_copy(
                        logits_fm[:, tt * 512:(tt + 1) * 512], ps5[0:16, 0:512]),
                        reads=[(ps5, "g")], writes=[(logits_fm, f"t{tt}")])
                else:
                    P.op("vector", lambda e, tt=tt: e.tensor_tensor(
                        logits_fm[:, tt * 512:(tt + 1) * 512],
                        logits_fm[:, tt * 512:(tt + 1) * 512], ps5[0:16, 0:512], ALU.add),
                        reads=[(ps5, "g"), (logits_fm, f"t{tt}")], writes=[(logits_fm, f"t{tt}")])
        P.op("vector", lambda e: e.tensor_scalar(logits_fm[:, :], logits_fm[:, :], b2r[:, 0:1], None, ALU.add),
             reads=[b2r] + [(logits_fm, f"t{t_}") for t_ in range(4)], writes=[(logits_fm, None)])
        if dbg and b == 0:
            P.dma("sync", lambda e: e.dma_start(out=dbg_lgf[:], in_=logits_fm[:, :]),
                  reads=[(logits_fm, None)])
            P.dma("sync", lambda e: e.dma_start(out=dbg_s32[:], in_=stats32[:, :]),
                  reads=[(stats32, "z")] + [(stats32, f"t{t_}") for t_ in range(16)])
        for tc in range(16):
            P.op("tensor", lambda e, tc=tc: e.transpose(
                psTt[:, 0:16], logits_fm[0:16, tc * 128:(tc + 1) * 128], C.idn[0:16, 0:16]),
                reads=[(logits_fm, None), C.idn], writes=[(psTt, "t")])
            P.op("vector", lambda e, tc=tc: e.tensor_tensor(
                lg_tm[:, tc, :], psTt[:, 0:16], rtb[:, :], ALU.mult),
                reads=[(psTt, "t"), rtb], writes=[(lg_tm, f"t{tc}")])
            l4 = lg_tm[:, tc, :].rearrange("p (h w) -> p h w", h=4)
            P.op("vector", lambda e, l4=l4: e.tensor_reduce(
                smallA[:, 0:4].rearrange("p (h w) -> p h w", h=4), l4, AX.X, ALU.max),
                reads=[(lg_tm, f"t{tc}")], writes=[(smallA, "gm")])
            P.op("vector", lambda e, l4=l4: e.tensor_tensor(
                l4, l4, smallA[:, 0:4].rearrange("p (h w) -> p h w", w=1).broadcast_to((128, 4, 4)),
                ALU.subtract),
                reads=[(lg_tm, f"t{tc}"), (smallA, "gm")], writes=[(lg_tm, f"t{tc}")])
            P.op("scalar", lambda e, tc=tc: e.activation(
                w_tm[:, tc, :], lg_tm[:, tc, :], AF.Exp, bias=0.0, scale=1.0),
                reads=[(lg_tm, f"t{tc}")], writes=[(w_tm, f"t{tc}")])
            w4 = w_tm[:, tc, :].rearrange("p (h w) -> p h w", h=4)
            P.op("vector", lambda e, w4=w4: e.tensor_reduce(
                smallB[:, 0:4].rearrange("p (h w) -> p h w", h=4), w4, AX.X, ALU.add),
                reads=[(w_tm, f"t{tc}")], writes=[(smallB, "gs")])
            P.op("vector", lambda e: e.reciprocal(smallB[:, 4:8], smallB[:, 0:4]),
                 reads=[(smallB, "gs")], writes=[(smallB, "gr")])
            P.op("vector", lambda e, w4=w4: e.tensor_tensor(
                w4, w4, smallB[:, 4:8].rearrange("p (h w) -> p h w", w=1).broadcast_to((128, 4, 4)),
                ALU.mult),
                reads=[(w_tm, f"t{tc}"), (smallB, "gr")], writes=[(w_tm, f"t{tc}")])
            P.op("tensor", lambda e, tc=tc: e.transpose(
                ps5[0:16, 0:128], w_tm[:, tc, :], C.idnf[:]),
                reads=[(w_tm, f"t{tc}")], writes=[(ps5, "g")])
            P.op("vector", lambda e, tc=tc: e.tensor_copy(
                wfm16[0:16, tc * 128:(tc + 1) * 128], ps5[0:16, 0:128]),
                reads=[(ps5, "g")], writes=[(wfm16, f"t{tc}")])
        if dbg and b == 0:
            P.dma("sync", lambda e: e.dma_start(out=dbg_w[:], in_=w_tm[:, :, :]),
                  reads=[(w_tm, f"t{t_}") for t_ in range(16)])
            P.dma("sync", lambda e: e.dma_start(out=dbg_lg[:], in_=lg_tm[:, :, :]),
                  reads=[(lg_tm, f"t{t_}") for t_ in range(16)])
            P.dma("sync", lambda e: e.dma_start(out=dbg_rtb[:], in_=rtb[:, :]), reads=[rtb])

        if dbg and b == 0:
            P.dma("scalar", lambda e: e.dma_start(out=dbg_wfm[:], in_=wfm16[:, :]),
                  reads=[(wfm16, f"t{t_}") for t_ in range(16)])
            P.dma("scalar", lambda e: e.dma_start(out=dbg_sel[:], in_=sel_sb[:, :]),
                  reads=[sel_sb])
        retag(ARX, [f"ch{s_}{p_}" for s_ in ("N", "Q", "qa", "qb", "na", "nb", "R") for p_ in range(2)]
              + [f"at{c}" for c in range(16)]
              + [f"{kp}{dc}_{tc}" for kp in ("qn", "knf", "kbf") for dc in range(2) for tc in range(16)]
              + [f"on{ch}" for ch in range(8)], tag="rtcomb")
        retag(ARW, ["brstg", "brs0", "brs1", "brl0", "brl1", "brd0", "brd1"]
              + [f"g1w{k}" for k in range(8)], tag="rtcomb2")
        wrow = rsd_row
        for h in range(4):
            for (vv, scr, kk_) in ((brs, fsscr, "brs"), (brl, flscr, "brl"), (brd, dlscr, "brd")):
                for dc in range(2):
                    P.dma("sync", lambda e, scr=scr, b=b, h=h, dc=dc, vv=vv: e.dma_start(
                        out=vv[:, dc, :], in_=scr[b, 2 * h + dc]),
                        reads=[(scr, f"{b}_{2 * h + dc}")],
                        writes=[(ARW, f"{kk_}{dc}")], tag=f"cb{kk_}{dc}")
            for br in range(4):
                r = h * 4 + br
                P.dma("scalar", lambda e, r=r: e.dma_start(out=wrow, in_=wfm16[r:r + 1, :]),
                      reads=[(wfm16, f"t{t_}") for t_ in range(16)],
                      writes=[(beta_fm, "row")], tag=f"wrow{r}")
                for dc in range(2):
                    ch_ = 2 * h + dc
                    for tt in range(4):
                        sl = slice(tt * 512, (tt + 1) * 512)
                        bt, bk_ = ((brs[:, dc, sl], (ARW, f"brs{dc}")),
                                   (brl[:, dc, sl], (ARW, f"brl{dc}")),
                                   (brd[:, dc, sl], (ARW, f"brd{dc}")),
                                   (v1s(ch_)[:, 64 + tt * 512:64 + (tt + 1) * 512],
                                    (SCRA, f"v{ch_}")))[br]
                        ps_ = (ps0, ps1)[(dc * 4 + tt) % 2]
                        mm(P, ps_[:, :], C.onescol[0:1, 0:128], wrow[0:1, sl],
                           reads=[C.onescol, (beta_fm, "row")], wkey=(ps_, "wb"))
                        P.op("vector", lambda e, ps_=ps_: e.tensor_copy(wbst[:, :], ps_[:, :]),
                             reads=[(ps_, "wb")], writes=[wbst])
                        if br == 0:
                            P.op("vector", lambda e, bt=bt, ch_=ch_, sl=sl: e.tensor_tensor(
                                o_n[:, ch_, sl], bt, wbst[:, :], ALU.mult),
                                reads=[bk_, wbst], writes=[(ARX, f"on{ch_}")])
                        else:
                            P.op("vector", lambda e, bt=bt: e.tensor_tensor(
                                sq_st[:, 0:512], bt, wbst[:, :], ALU.mult),
                                reads=[bk_, wbst], writes=[(sq_st, "cb")])
                            P.op("vector", lambda e, ch_=ch_, sl=sl: e.tensor_tensor(
                                o_n[:, ch_, sl], o_n[:, ch_, sl], sq_st[:, 0:512], ALU.add),
                                reads=[(sq_st, "cb"), (ARX, f"on{ch_}")], writes=[(ARX, f"on{ch_}")])
            for dc in range(2):
                ch_ = 2 * h + dc
                P.op("scalar", lambda e, ch_=ch_: e.activation(
                    sq_st[:, :], o_n[:, ch_, :], AF.Square, bias=0.0, scale=1.0),
                    reads=[(ARX, f"on{ch_}")], writes=[(sq_st, None)])
                for tc in range(16):
                    mm(P, ps4[:, tc:tc + 1], sq_st[:, tc * 128:(tc + 1) * 128], C.onescol[:, 0:1],
                       reads=[(sq_st, None), C.onescol], wkey=(ps4, "g"),
                       start=(dc == 0 and tc == 0), stop=(dc == 1 and tc == 15))
            P.op("vector", lambda e: e.tensor_scalar(nrm16[:, :], ps4[:, 0:16], 1.0 / 256, 1e-5, ALU.mult, ALU.add),
                 reads=[(ps4, "g")], writes=[(nrm16, "a")])
            P.op("scalar", lambda e: e.activation(nrm16[:, :], nrm16[:, :], AF.Sqrt, bias=0.0, scale=1.0),
                 reads=[(nrm16, "a")], writes=[(nrm16, "b")])
            P.op("vector", lambda e: e.reciprocal(nrm16[:, :], nrm16[:, :]),
                 reads=[(nrm16, "b")], writes=[(nrm16, "c")])
            P.op("tensor", lambda e: e.transpose(ps5[0:16, 0:128], nrm16[:, :], C.idnf[:]),
                 reads=[(nrm16, "c")], writes=[(ps5, "g")])
            P.op("vector", lambda e: e.tensor_copy(rsd16[:, :], ps5[0:16, 0:128]),
                 reads=[(ps5, "g")], writes=[rsd16])
            P.dma("sync", lambda e: e.dma_start(out=rsd_dram[:, :], in_=rsd16[:, :]),
                  reads=[rsd16], writes=[(rsd_dram, "d")], tag="rsro")
            P.dma("gpsimd", lambda e: e.dma_start(out=rsd_row, in_=rsd_dram[:].rearrange("a f -> (a f)")[None, :]),
                  reads=[(rsd_dram, "d")], writes=[(beta_fm, "row")], tag="rsrow")
            if dbg and b == 0 and h == 0:
                P.dma("scalar", lambda e: e.dma_start(out=dbg_rsd[:], in_=rsd_row),
                      reads=[(beta_fm, "row")])
            for tt in range(4):
                mm(P, ps0[:, :], C.onescol[0:1, 0:128], rsd_row[0:1, tt * 512:(tt + 1) * 512],
                   reads=[C.onescol, (beta_fm, "row")], wkey=(ps0, "g"))
                for dc in range(2):
                    ch_ = 2 * h + dc
                    sl = slice(tt * 512, (tt + 1) * 512)
                    P.op("vector", lambda e, ch_=ch_, sl=sl, dc=dc: e.scalar_tensor_tensor(
                        o_n[:, ch_, sl], o_n[:, ch_, sl], onw[:, dc:dc + 1], ps0[:, :],
                        ALU.mult, ALU.mult),
                        reads=[(ARX, f"on{ch_}"), onw, (ps0, "g")], writes=[(ARX, f"on{ch_}")])

        if dbg and b == 0:
            P.dma("sync", lambda e: e.dma_start(out=dbg_on[:], in_=ARX[:, 0:16384]),
                  reads=[(ARX, f"on{ch}") for ch in range(8)])
        retag(ARW, [f"wo{k}" for k in range(8)] + ["brs0", "brs1", "brl0", "brl1", "brd0", "brd1", "brstg"], tag="rtwo")
        for k in range(8):
            P.dma("gpsimd", lambda e, k=k: e.dma_start(
                out=wov[:, k, :], in_=wo_d[k * 128:(k + 1) * 128, :]),
                writes=[(ARW, f"wo{k}")], tag=f"wol{k}")
        for tc in range(16):
            for ft in range(2):
                ps = (ps0, ps1)[ft]
                for ch_ in range(8):
                    mm(P, ps[:, :], o_n[:, ch_, tc * 128:(tc + 1) * 128],
                       wov[:, ch_, ft * 512:(ft + 1) * 512],
                       reads=[(ARX, f"on{ch_}"), (ARW, f"wo{ch_}")], wkey=(ps, "g"),
                       start=(ch_ == 0), stop=(ch_ == 7))
                P.op("vector", lambda e, ps=ps, ft=ft: e.tensor_copy(
                    qkst[:, ft * 512 + 0:ft * 512 + 512] if False else qkst[:, ft * 512:(ft + 1) * 512],
                    ps[:, :]),
                    reads=[(ps, "g")], writes=[(qkst, f"s{ft}")])
            P.dma("sync", lambda e, b=b, tc=tc: e.dma_start(
                out=s1scr[b, tc * 128:(tc + 1) * 128, :], in_=qkst[:, 0:1024]),
                reads=[(qkst, "s0"), (qkst, "s1")],
                writes=[(s1scr, f"{b}_{tc}")], tag=f"s1s{tc}")

        logf = logits_fm[0:8, :]
        retag(logits_fm, [f"t{t_}" for t_ in range(4)] + [None] + [f"lf{t_}" for t_ in range(4)], tag="rtlf")
        for tt in range(4):
            ps = (ps2, ps3)[tt % 2]
            for k in range(8):
                mm(P, ps[0:8, :], wg_sb[:, k, :], xn1[:, k, tt * 512:(tt + 1) * 512],
                   reads=[wg_sb, xn1], wkey=(ps, "g"), start=(k == 0), stop=(k == 7))
            P.op("vector", lambda e, ps=ps, tt=tt: e.tensor_scalar(
                logf[:, tt * 512:(tt + 1) * 512], ps[0:8, :], bgc[:, 0:1], None, ALU.add),
                reads=[(ps, "g"), bgc], writes=[(logits_fm, f"lf{tt}")])
        for tc in range(16):
            P.op("tensor", lambda e, tc=tc: e.transpose(
                psTt[:, 0:8], logf[0:8, tc * 128:(tc + 1) * 128], C.idn[0:8, 0:8]),
                reads=[(logits_fm, f"lf{tc // 4}"), C.idn], writes=[(psTt, "t")])
            P.op("vector", lambda e, tc=tc: e.tensor_copy(smallA[:, 0:8], psTt[:, 0:8]),
                 reads=[(psTt, "t")], writes=[(smallA, "lg")])
            P.op("vector", lambda e: e.reduce_max(smallB[:, 0:1], smallA[:, 0:8], axis=AX.X),
                 reads=[(smallA, "lg")], writes=[(smallB, "m1")])
            P.op("vector", lambda e: e.tensor_scalar(msk1[:, :], smallA[:, 0:8], smallB[:, 0:1], None, ALU.is_equal),
                 reads=[(smallA, "lg"), (smallB, "m1")], writes=[msk1])
            P.op("vector", lambda e: e.tensor_scalar(tmp8[:, :], msk1[:, :], -1e30, None, ALU.mult),
                 reads=[msk1], writes=[(tmp8, "a")])
            P.op("vector", lambda e: e.tensor_tensor(lmsk[:, :], smallA[:, 0:8], tmp8[:, :], ALU.add),
                 reads=[(smallA, "lg"), (tmp8, "a")], writes=[lmsk])
            P.op("vector", lambda e: e.reduce_max(smallB[:, 1:2], lmsk[:, :], axis=AX.X),
                 reads=[lmsk], writes=[(smallB, "m2")])
            P.op("vector", lambda e: e.tensor_scalar(msk2[:, :], lmsk[:, :], smallB[:, 1:2], None, ALU.is_equal),
                 reads=[lmsk, (smallB, "m2")], writes=[msk2])
            P.op("vector", lambda e: e.tensor_tensor(smallB[:, 2:3], smallB[:, 0:1], smallB[:, 1:2], ALU.subtract),
                 reads=[(smallB, "m1"), (smallB, "m2")], writes=[(smallB, "d")])
            P.op("scalar", lambda e: e.activation(smallB[:, 3:4], smallB[:, 2:3], AF.Sigmoid, bias=0.0, scale=1.0),
                 reads=[(smallB, "d")], writes=[(smallB, "w1")])
            P.op("vector", lambda e: e.tensor_scalar(smallB[:, 4:5], smallB[:, 3:4], -1.0, 1.0, ALU.mult, ALU.add),
                 reads=[(smallB, "w1")], writes=[(smallB, "w2")])
            P.op("vector", lambda e: e.tensor_scalar(tmp8[:, :], msk2[:, :], smallB[:, 4:5], None, ALU.mult),
                 reads=[msk2, (smallB, "w2")], writes=[(tmp8, "b")])
            P.op("vector", lambda e, tc=tc: e.scalar_tensor_tensor(
                comb_tm[:, tc, :], msk1[:, :], smallB[:, 3:4], tmp8[:, :], ALU.mult, ALU.add),
                reads=[msk1, (smallB, "w1"), (tmp8, "b")], writes=[(comb_tm, f"t{tc}")])
        if dbg and b == 0:
            P.dma("sync", lambda e: e.dma_start(out=dbg_comb[:], in_=comb_tm[:, :, :]),
                  reads=[(comb_tm, f"t{t_}") for t_ in range(16)])
        retag(SCRA, [f"v{mc}" for mc in range(8)] + [f"ga{tc}_{ft}" for tc in range(16) for ft in range(2)], tag="rtg1")
        retag(ARX, [f"on{ch}" for ch in range(8)] + [f"fw2_{k}" for k in range(16)], tag="rtw2")
        retag(ARW, [f"wo{k}" for k in range(8)] + [f"fw1_{k}" for k in range(8)], tag="rtw1")
        retag(ARS, ARSQK + ARSDK + [f"h1_{k}" for k in range(16)], tag="rth1")
        P.op("gpsimd", lambda e: e.memset(SCRA[:, 0:16384], 0.0),
             writes=[(SCRA, f"ga{tc}_{ft}") for tc in range(16) for ft in range(2)])
        for e_ in range(8):
            for k in range(8):
                P.dma("gpsimd", lambda e, k=k, e_=e_: e.dma_start(
                    out=w1e[:, k, :], in_=fw1_d[e_, k * 128:(k + 1) * 128, :]),
                    writes=[(ARW, f"fw1_{k}")], tag=f"fw1l{k}")
            for k in range(16):
                P.dma("gpsimd", lambda e, k=k, e_=e_: e.dma_start(
                    out=w2e[:, k, :], in_=fw2_d[e_, k * 128:(k + 1) * 128, :]),
                    writes=[(ARX, f"fw2_{k}")], tag=f"fw2l{k}")
            P.dma("sync", lambda e, e_=e_: e.dma_start(out=fb1c[:], in_=fb1_d[e_]),
                  writes=[fb1c], tag="fb1l")
            P.dma("gpsimd", lambda e, e_=e_: e.dma_start(
                out=fb2r[:, :], in_=fb2_d[0:1, e_ * 1024:(e_ + 1) * 1024]),
                writes=[fb2r], tag="fb2l")
            for tt in range(4):
                for mh in range(16):
                    ps = (ps0, ps1, ps4)[mh % 3]
                    for k in range(8):
                        mm(P, ps[:, :], w1e[:, k, mh * 128:(mh + 1) * 128],
                           xn1[:, k, tt * 512:(tt + 1) * 512],
                           reads=[(ARW, f"fw1_{k}"), xn1], wkey=(ps, "g"),
                           start=(k == 0), stop=(k == 7))
                    P.op("scalar", lambda e, ps=ps, mh=mh: e.activation(
                        h1f[:, mh, :], ps[:, :], AF.Relu, bias=fb1c[:, mh:mh + 1], scale=1.0),
                        reads=[(ps, "g"), fb1c], writes=[(ARS, f"h1_{mh}")])
                for tc2 in range(4):
                    tc = tt * 4 + tc2
                    for ft in range(2):
                        ps = (ps2, ps3, ps5)[(tc2 * 2 + ft) % 3]
                        for k in range(16):
                            mm(P, ps[:, :], h1f[:, k, tc2 * 128:(tc2 + 1) * 128],
                               w2e[:, k, ft * 512:(ft + 1) * 512],
                               reads=[(ARS, f"h1_{k}"), (ARX, f"fw2_{k}")], wkey=(ps, "g"),
                               start=(k == 0), stop=False)
                        mm(P, ps[:, :], C.onescol[0:1, 0:128], fb2r[0:1, ft * 512:(ft + 1) * 512],
                           reads=[C.onescol, fb2r], wkey=(ps, "g"), start=False, stop=True)
                        P.op("vector", lambda e, ps=ps, tc=tc, ft=ft, e_=e_: e.scalar_tensor_tensor(
                            g1acc[:, tc, ft * 512:(ft + 1) * 512], ps[:, :],
                            comb_tm[:, tc, e_:e_ + 1], g1acc[:, tc, ft * 512:(ft + 1) * 512],
                            ALU.mult, ALU.add),
                            reads=[(ps, "g"), (comb_tm, f"t{tc}"), (SCRA, f"ga{tc}_{ft}")],
                            writes=[(SCRA, f"ga{tc}_{ft}")])
        for tc in range(16):
            P.op("scalar", lambda e, tc=tc: e.activation(
                C.ln_xn[:, :], g1acc[:, tc, :], AF.Silu, bias=0.0, scale=1.0),
                reads=[(SCRA, f"ga{tc}_0"), (SCRA, f"ga{tc}_1")], writes=[C.ln_xn])
            P.dma("sync", lambda e, b=b, tc=tc: e.dma_start(
                out=qkst[:, 0:1024], in_=s1scr[b, tc * 128:(tc + 1) * 128, :]),
                reads=[(s1scr, f"{b}_{tc}")], writes=[(qkst, "s1l")], tag=f"s1l{tc}")
            P.dma("scalar", lambda e, b=b, tc=tc: e.dma_start(
                out=lnxt[:, :], in_=x1_d[b, tc * 128:(tc + 1) * 128, :]),
                writes=[lnxt], tag=f"x1l{tc}")
            P.op("vector", lambda e: e.tensor_tensor(sq_st[:, 0:1024], qkst[:, 0:1024], C.ln_xn[:, :], ALU.mult),
                 reads=[(qkst, "s1l"), C.ln_xn], writes=[(sq_st, "pr")])
            P.op("vector", lambda e: e.tensor_tensor(OUTF[:, :], lnxt[:, :], sq_st[:, 0:1024], ALU.add),
                 reads=[lnxt, (sq_st, "pr")], writes=[OUTF])
            P.dma("sync", lambda e, b=b, tc=tc: e.dma_start(
                out=out_d[b, tc * 128:(tc + 1) * 128, :], in_=OUTF[:, :]),
                reads=[OUTF], tag=f"outs{tc}")
    P.drain_dmas("sync")
    P.drain_dmas("gpsimd")
    P.drain_dmas("scalar")
    return C



def _consts():
    idn = np.eye(128, dtype=BF)
    onescol = np.ones((128, 128), dtype=BF)
    onesf = np.ones((128, 1), np.float32)
    return {"c_idn": idn, "c_ones": onescol, "c_onesf": onesf,
            "c_idnf": np.eye(128, dtype=np.float32)}


def _l0_inputs(inputs, b0):
    t = lambda a: np.ascontiguousarray(np.asarray(a).T)
    col8 = lambda a: np.ascontiguousarray(np.asarray(a).reshape(8, 128).T)
    d = {
        "x": np.ascontiguousarray(inputs["L0_attn_x"][b0:b0 + 2]),
        "l0a_wq": t(inputs["L0_attn_Wq"]), "l0a_wk": t(inputs["L0_attn_Wk"]),
        "l0a_wv": t(inputs["L0_attn_Wv"]), "l0a_wo": t(inputs["L0_attn_Wo"]),
        "l0a_bq": col8(inputs["L0_attn_bq"]), "l0a_bk": col8(inputs["L0_attn_bk"]),
        "l0a_bv": np.ascontiguousarray(inputs["L0_attn_bv"].reshape(1, 1024)),
        "l0a_bo": np.ascontiguousarray(inputs["L0_attn_bo"].reshape(1, 1024)),
        "l0f_wq": t(inputs["L0_ffn_Wq"]), "l0f_wk": t(inputs["L0_ffn_Wk"]),
        "l0f_bq": col8(inputs["L0_ffn_bq"]), "l0f_bk": col8(inputs["L0_ffn_bk"]),
        "l0f_ewq": np.ascontiguousarray(inputs["L0_ffn_eWq"].transpose(0, 2, 1)),
        "l0f_ewk": np.ascontiguousarray(inputs["L0_ffn_eWk"].transpose(0, 2, 1)),
        "l0f_ebq": np.ascontiguousarray(inputs["L0_ffn_ebq"].T),
        "l0f_ebk": np.ascontiguousarray(inputs["L0_ffn_ebk"].T),
        "l0f_wexp": np.ascontiguousarray(inputs["L0_ffn_Wexp"]),
        "l0f_bexp": np.ascontiguousarray(inputs["L0_ffn_bexp"].reshape(1, 8192)),
    }
    d.update(_consts())
    return d


def _l1_inputs(inputs, x1=None):
    cw = lambda a, g: np.ascontiguousarray(np.asarray(a).reshape(g, 128, -1).transpose(1, 0, 2).reshape(128, -1))
    sel = np.zeros((16, 2048), dtype=BF)
    for r in range(16):
        sel[r, r * 128:(r + 1) * 128] = 1
    mlow = -np.tril(np.ones((128, 128), np.float32), -1)
    mtriu = np.triu(np.ones((128, 128), np.float32), 0).astype(BF)
    mup = (-np.triu(np.ones((128, 128), np.float32), 1)).astype(BF)
    d = {
        "l1a_wq": np.ascontiguousarray(inputs["L1_attn_Wq"]),
        "l1a_wk": np.ascontiguousarray(inputs["L1_attn_Wk"]),
        "l1a_wv": np.ascontiguousarray(inputs["L1_attn_Wv"]),
        "l1a_wb": cw(inputs["L1_attn_Wb"], 8),
        "l1a_cq": cw(inputs["L1_attn_conv_q_w"], 8),
        "l1a_ck": cw(inputs["L1_attn_conv_k_w"], 8),
        "l1a_cv": cw(inputs["L1_attn_conv_v_w"], 8),
        "l1a_fs": cw(inputs["L1_attn_fir_short_w"].reshape(1024, 3), 8),
        "l1a_fl": cw(inputs["L1_attn_fir_long_w"].reshape(1024, 63), 8),
        "l1a_w1": np.ascontiguousarray(inputs["L1_attn_mlp_w1"][0:1024]),
        "l1a_w1c9": np.ascontiguousarray(np.pad(inputs["L1_attn_mlp_w1"][1024:1056], ((0, 96), (0, 0)))),
        "l1a_b1": np.ascontiguousarray(inputs["L1_attn_mlp_b1"].reshape(16, 128).T),
        "l1a_w2": cw(inputs["L1_attn_mlp_w2"], 16),
        "l1a_b2": np.ascontiguousarray(inputs["L1_attn_mlp_b2"].reshape(16, 1)),
        "l1a_glt": np.ascontiguousarray(inputs["L1_attn_gate_log_temp"].reshape(1, 4)),
        "l1a_onw": np.ascontiguousarray(inputs["L1_attn_o_norm_w"].reshape(2, 128).T),
        "l1a_wo": np.ascontiguousarray(inputs["L1_attn_Wo"]),
        "c_sel": sel, "c_mlow": mlow, "c_mtriu": mtriu, "c_mup": mup,
        "l1f_wg": cw(inputs["L1_ffn_Wg"], 8),
        "l1f_bg": np.ascontiguousarray(inputs["L1_ffn_bg"].reshape(8, 1)),
        "l1f_w1": np.ascontiguousarray(inputs["L1_ffn_W1"]),
        "l1f_b1": np.ascontiguousarray(inputs["L1_ffn_b1"].reshape(8, 16, 128).transpose(0, 2, 1)),
        "l1f_w2": np.ascontiguousarray(inputs["L1_ffn_W2"]),
        "l1f_b2": np.ascontiguousarray(inputs["L1_ffn_b2"].reshape(1, 8192)),
    }
    if x1 is not None:
        d["x1"] = np.ascontiguousarray(x1)
    d.update(_consts())
    return d


def build_l1_program(dbg=False):
    nc = bass.Bass("TRN2", target_bir_lowering=False, debug=False)
    P = Prog(nc)
    build_layer1(nc, P, dbg=dbg)
    P.finalize()
    return nc, P


def build_fused_program():
    nc = bass.Bass("TRN2", target_bir_lowering=False, debug=False)
    P = Prog(nc)
    C = setup_common(nc, P, need=("masks",))
    P.push_scope()
    build_layer0(nc, P, C=C, out_kind="Internal")
    P.pop_scope()
    P.layer_barrier()
    P.push_scope()
    build_layer1(nc, P, C=C)
    P.pop_scope()
    P.finalize()
    return nc, P


def build_l0_program(dbg=False):
    nc = bass.Bass("TRN2", target_bir_lowering=False, debug=False)
    P = Prog(nc)
    build_layer0(nc, P, dbg=dbg)
    P.finalize()
    return nc, P


def kernel(**inputs):
    nc, _ = build_fused_program()
    in_maps = []
    for c in range(8):
        m = _l0_inputs(inputs, 2 * c)
        m.update(_l1_inputs(inputs))
        in_maps.append(m)
    res = run_bass_kernel_spmd(nc, in_maps, core_ids=list(range(8)))
    return np.concatenate([r["out"] for r in res.results], axis=0)
